# version 01
# baseline (speedup 1.0000x reference)
import math
import contextlib
import numpy as np
import ml_dtypes
import concourse.bass as bass
import concourse.mybir as mybir
from concourse.bass import IndirectOffsetOnAxis
from concourse.bass_utils import run_bass_kernel_spmd

F32 = mybir.dt.float32
BF16 = mybir.dt.bfloat16
I32 = mybir.dt.int32
AF = mybir.ActivationFunctionType
ALU = mybir.AluOpType
AX = mybir.AxisListType

BF16NP = ml_dtypes.bfloat16

ENGINES = "TAVPS"


class Sched:

    def __init__(self):
        self.ops = {e: [] for e in ENGINES}
        self.cnt = {}

    def tick(self, sem, n=1):
        self.cnt[sem] = self.cnt.get(sem, 0) + n
        return self.cnt[sem]

    def val(self, sem):
        return self.cnt.get(sem, 0)

    def op(self, eng, fn, waits=(), incs=()):
        self.ops[eng].append((fn, tuple(waits), tuple(incs)))

    def ser(self, eng, sem, fn, waits=(), incs=()):
        t = self.tick(sem)
        self.op(eng, fn, waits=waits, incs=[(sem, 1)] + list(incs))
        self.op(eng, lambda eng2: eng2.nop(), waits=[(sem, t)])

    def barrier(self):
        for e in ENGINES:
            self.tick("ph", 1)
            self.op(e, lambda eng: eng.nop(), incs=[("ph", 1)])
        tgt = self.val("ph")
        for e in ENGINES:
            self.op(e, lambda eng: eng.nop(), waits=[("ph", tgt)])

    def validate_waits(self):
        for e in ENGINES:
            for i, (fn, waits, incs) in enumerate(self.ops[e]):
                for nm, v in waits:
                    tot = self.cnt.get(nm, 0)
                    assert v <= tot, (
                        f"engine {e} op {i}: wait {nm}>={v} unreachable "
                        f"(final={tot})")

    def replay(self, nc, sems):
        self.validate_waits()
        with nc.Block() as block:
            def mk(engine_key):
                def emit(eng):
                    for fn, waits, incs in self.ops[engine_key]:
                        for nm, v in waits:
                            eng.wait_ge(sems[nm], v)
                        inst = fn(eng)
                        for nm, n in incs:
                            assert inst is not None
                            inst.then_inc(sems[nm], n)
                return emit
            block.tensor(mk("T"))
            block.scalar(mk("A"))
            block.vector(mk("V"))
            block.gpsimd(mk("P"))
            block.sync(mk("S"))


SEM_NAMES = ["ph", "wl", "xa", "mm", "ac", "ve", "po", "st", "ex"]


def dma(eng, out, in_):
    return eng.dma_start(out=out, in_=in_)


class Ctx:
    def __init__(self, nc, s):
        self.nc = nc
        self.s = s
        self.t = {}


def emit_ln(c, x_name, xn_name, T, psA, psB):
    s = c.s
    x = c.t[x_name]
    xn = c.t[xn_name]
    cast = c.t["lncast"]
    sq = c.t["lnsq"]
    rows = c.t["lnrows"]
    onesd = c.t["onesd"]
    onesk1 = c.t["onesk1"]
    t1 = c.t["lnt1"]
    ntg = T // 512
    ndc = 8

    prev_stats_mm = 0
    for tg in range(ntg):
        for dc in range(ndc):
            xs = x[:, dc * T + tg * 512: dc * T + tg * 512 + 512]
            aw = [("mm", prev_stats_mm)] if (dc == 0 and prev_stats_mm) else ()
            s.op("A", (lambda xs=xs, dc=dc: lambda eng: eng.activation(
                cast[:, dc * 512:(dc + 1) * 512], xs, AF.Copy))(), waits=aw)
            inc = []
            if dc == ndc - 1:
                s.tick("ac")
                inc = [("ac", 1)]
            s.op("A", (lambda xs=xs, dc=dc: lambda eng: eng.activation(
                sq[:, dc * 512:(dc + 1) * 512], xs, AF.Square))(), incs=inc)
        a_done = s.val("ac")
        for dc in range(ndc):
            st0 = (dc == 0)
            sp = (dc == ndc - 1)
            w = [("ac", a_done)] if dc == 0 else ()
            s.op("T", (lambda dc=dc, st0=st0, sp=sp: lambda eng: eng.matmul(
                psA[0:1, 0:512], onesd[:, 0:1],
                cast[:, dc * 512:(dc + 1) * 512], start=st0, stop=sp))(), waits=w)
            inc = []
            if sp:
                s.tick("mm")
                inc = [("mm", 1)]
            s.op("T", (lambda dc=dc, st0=st0, sp=sp: lambda eng: eng.matmul(
                psA[0:1, 512:1024], onesd[:, 0:1],
                sq[:, dc * 512:(dc + 1) * 512], start=st0, stop=sp))(), incs=inc)
        mm_done = s.val("mm")
        prev_stats_mm = mm_done
        m_ = rows[0:1, 0:512]
        e2 = rows[0:1, 512:1024]
        var = rows[0:1, 512:1024]
        lnv = rows[0:1, 512:1024]
        rstd = rows[0:1, 1024:1536]
        bias = rows[0:1, 1536:2048]
        s.ser("V", "ve", lambda eng: eng.tensor_scalar_mul(
            m_, psA[0:1, 0:512], 1.0 / 1024.0), waits=[("mm", mm_done)])
        s.ser("V", "ve", lambda eng: eng.tensor_scalar(
            e2, psA[0:1, 512:1024], 1.0 / 1024.0, 1e-5,
            op0=ALU.mult, op1=ALU.add))
        s.ser("V", "ve", lambda eng: eng.scalar_tensor_tensor(
            rstd, m_, -1.0, m_, op0=ALU.mult, op1=ALU.mult))
        vt = s.tick("ve")
        s.op("V", lambda eng: eng.tensor_add(var, rstd, e2), incs=[("ve", 1)])
        at = s.tick("ac")
        s.op("A", lambda eng: eng.activation(lnv, var, AF.Ln),
             waits=[("ve", vt)])
        s.op("A", lambda eng: eng.activation(rstd, lnv, AF.Exp, scale=-0.5),
             incs=[("ac", 1)])
        vt2 = s.tick("ve")
        s.op("V", lambda eng: eng.scalar_tensor_tensor(
            bias, m_, -1.0, rstd, op0=ALU.mult, op1=ALU.mult),
            waits=[("ac", at)], incs=[("ve", 1)])
        s.op("V", lambda eng: eng.nop(), waits=[("ve", vt2)])
        mt = s.tick("mm")
        s.op("T", lambda eng: eng.matmul(psB[:, 0:512], onesk1[0:1, :], rstd,
                                         start=True, stop=True),
             waits=[("ve", vt2)])
        s.op("T", lambda eng: eng.matmul(psB[:, 512:1024], onesk1[0:1, :], bias,
                                         start=True, stop=True),
             incs=[("mm", 1)])
        for dc in range(ndc):
            xs = x[:, dc * T + tg * 512: dc * T + tg * 512 + 512]
            xno = xn[:, dc * T + tg * 512: dc * T + tg * 512 + 512]
            w = [("mm", mt)] if dc == 0 else ()
            s.op("V", (lambda xs=xs: lambda eng: eng.tensor_mul(t1, xs, psB[:, 0:512]))(),
                 waits=w)
            inc = []
            if dc == ndc - 1:
                s.tick("ve")
                inc = [("ve", 1)]
            s.op("V", (lambda xno=xno: lambda eng: eng.tensor_add(
                xno, t1, psB[:, 512:1024]))(), incs=inc)
        ve_done = s.val("ve")
        s.op("T", lambda eng: eng.nop(), waits=[("ve", ve_done)])


def emit_projA(c, w_dram, act_name, out_name, ndc, nfc, T, ps2, wbuf_name,
               evac=None, act_T=None):
    s = c.s
    act = c.t[act_name]
    out = c.t[out_name] if out_name else None
    wbuf = c.t[wbuf_name]
    if act_T is None:
        act_T = T
    ntg = T // 512
    wsz = ndc * 128
    ping_guard = {0: 0, 1: 0}
    wfree = {}
    for fc in range(nfc):
        buf = fc % 2
        wt = s.tick("wl", 16)
        w = [("mm", wfree[fc - 2])] if fc >= 2 else []
        s.op("P", (lambda fc=fc, buf=buf: lambda eng: eng.dma_start(
            out=wbuf[:, buf * wsz:(buf + 1) * wsz], in_=w_dram[fc]))(),
            waits=w, incs=[("wl", 16)])
        for tg in range(ntg):
            png = (fc * ntg + tg) % 2
            pslice = ps2[:, png * 512:(png + 1) * 512]
            for dc in range(ndc):
                w = []
                if dc == 0:
                    if tg == 0:
                        w.append(("wl", wt))
                    if ping_guard[png]:
                        w.append(("ve", ping_guard[png]))
                st0 = (dc == 0)
                sp = (dc == ndc - 1)
                inc = []
                if sp:
                    s.tick("mm")
                    inc = [("mm", 1)]
                s.op("T", (lambda tg=tg, dc=dc, st0=st0, sp=sp, buf=buf,
                           pslice=pslice: lambda eng: eng.matmul(
                    pslice, wbuf[:, buf * wsz + dc * 128: buf * wsz + dc * 128 + 128],
                    act[:, dc * act_T + tg * 512: dc * act_T + tg * 512 + 512],
                    start=st0, stop=sp))(), waits=w, incs=inc)
            mmv = s.val("mm")
            if evac is None:
                ev = s.tick("ve")
                s.op("V", (lambda fc=fc, tg=tg, pslice=pslice: lambda eng:
                           eng.tensor_copy(
                               out[:, fc * T + tg * 512: fc * T + tg * 512 + 512],
                               pslice))(),
                     waits=[("mm", mmv)], incs=[("ve", 1)])
            else:
                evac(fc, tg, pslice, mmv)
                ev = s.val("ve")
            ping_guard[png] = ev
        wfree[fc] = s.val("mm")


def emit_exchange_write(c, payload_ops, epoch):
    s = c.s
    for fn, waits in payload_ops:
        s.tick("ex", 16)
        s.op("P", fn, waits=waits, incs=[("ex", 16)])
    exv = s.val("ex")
    flagvals = c.t["flagvals"]
    fidx = c.t["fidx"]
    shf = c.t["shf"]
    s.tick("ex", 16)
    s.op("P", (lambda epoch=epoch: lambda eng: eng.indirect_dma_start(
        out=shf, out_offset=IndirectOffsetOnAxis(fidx, 0),
        in_=flagvals[:, epoch - 1:epoch], in_offset=None))(),
        waits=[("ex", exv)], incs=[("ex", 16)])


def emit_exchange_poll(c, epoch, msem_i):
    s = c.s
    shf = c.t["shf"]
    pf = c.t["pfidx"]
    flagrd = c.t["flagrd"]

    flagvals = c.t["flagvals"]

    def fn(eng):
        nc = c.nc
        msem = nc.monotonic_semaphore(msem_i, engine=eng)
        go = eng.alloc_register(f"go{epoch}")
        fv = eng.alloc_register(f"fv{epoch}")
        tgt = eng.alloc_register(f"tg{epoch}")
        eng.load(tgt, flagvals[0:1, epoch - 1:epoch])
        eng.reg_mov(go, 1)
        with eng.While(go):
            msem.inc_expected(16)
            eng.indirect_dma_start(
                out=flagrd[:, :], out_offset=None,
                in_=shf, in_offset=IndirectOffsetOnAxis(pf, 0)
            ).then_inc(msem.sem(), 16)
            msem.wait()
            eng.load(fv, flagrd[0:1, 0:1])
            eng.reg_alu(go, fv, tgt, ALU.not_equal)
        return eng.nop()
    s.tick("po")
    s.op("P", fn, incs=[("po", 1)])
    return s.val("po")



def emit_l0_norm_mhat(c, psN, psM, kdup_ready=()):
    s = c.s
    T = 2048
    q = c.t["q"]
    kdup = c.t["kdup"]
    qsq = c.t["qsq"]
    ksq = c.t["ksq"]
    nr = c.t["nrows"]
    mhneg = c.t["mhneg"]
    onesd = c.t["onesd"]
    negk1 = c.t["negk1"]
    at = s.tick("ac")
    s.op("A", lambda eng: eng.activation(ksq[0:64, :], kdup[0:64, :], AF.Square),
         waits=kdup_ready, incs=[("ac", 1)])
    vt = s.tick("ve")
    s.op("V", lambda eng: eng.memset(nr[0:1, 0:4], 0.0), incs=[("ve", 1)])
    for tg in range(8):
        mt = s.tick("mm")
        s.op("T", (lambda tg=tg: lambda eng: eng.matmul(
            psN[0:1, 0:512], onesd[0:64, 0:1],
            ksq[0:64, tg * 512:(tg + 1) * 512], start=True, stop=True))(),
            waits=[("ac", at), ("ve", vt)] if tg == 0 else [("ve", s.val("ve"))],
            incs=[("mm", 1)])
        s.ser("V", "ve", (lambda: lambda eng: eng.reduce_max(
            nr[0:1, 4:5], psN[0:1, 0:512], axis=AX.X))(),
            waits=[("mm", mt)])
        s.ser("V", "ve", lambda eng: eng.tensor_max(
            nr[0:1, 0:1], nr[0:1, 0:1], nr[0:1, 4:5]))
    kmax_done = s.val("ve")
    prev_norm_mm = 0
    for hp in range(8):
        at = s.tick("ac")
        aw = [("mm", prev_norm_mm)] if prev_norm_mm else ()
        s.op("A", (lambda hp=hp: lambda eng: eng.activation(
            qsq[:, :], q[:, hp * T:(hp + 1) * T], AF.Square))(), waits=aw,
            incs=[("ac", 1)])
        s.op("V", (lambda: lambda eng: eng.memset(nr[0:1, 8:9], 0.0))())
        vt0 = s.tick("ve")
        s.op("V", lambda eng: eng.engine_nop(), incs=[("ve", 1)])
        for tg in range(4):
            for half in range(2):
                mt = s.tick("mm")
                s.op("T", (lambda tg=tg, half=half: lambda eng: eng.matmul(
                    psN[0:1, 512:1024],
                    onesd[half * 64:half * 64 + 64, 0:1],
                    qsq[half * 64:half * 64 + 64, tg * 512:(tg + 1) * 512],
                    start=True, stop=True,
                    tile_position=(half * 64, 0)))(),
                    waits=[("ac", at), ("ve", s.val("ve"))], incs=[("mm", 1)])
                s.ser("V", "ve", (lambda: lambda eng: eng.reduce_max(
                    nr[0:1, 9:10], psN[0:1, 512:1024], axis=AX.X))(),
                    waits=[("mm", mt)])
                s.ser("V", "ve", lambda eng: eng.tensor_max(
                    nr[0:1, 8:9], nr[0:1, 8:9], nr[0:1, 9:10]))
        at2 = s.tick("ac")
        s.op("A", lambda eng: eng.activation(nr[0:1, 16:17], nr[0:1, 0:1], AF.Ln),
             waits=[("ve", s.val("ve"))])
        s.op("A", lambda eng: eng.activation(nr[0:1, 17:18], nr[0:1, 8:9], AF.Ln),
             incs=[("ac", 1)])
        s.ser("V", "ve", lambda eng: eng.tensor_add(
            nr[0:1, 18:19], nr[0:1, 16:17], nr[0:1, 17:18]),
            waits=[("ac", at2)])
        vt3 = s.tick("ve")
        s.op("V", lambda eng: eng.tensor_scalar(
            nr[0:1, 19:20], nr[0:1, 18:19], 0.5, -2.0794415416798357,
            op0=ALU.mult, op1=ALU.add), incs=[("ve", 1)])
        at3 = s.tick("ac")
        s.op("A", lambda eng: eng.activation(nr[0:1, 20:21], nr[0:1, 19:20], AF.Exp),
             waits=[("ve", vt3)], incs=[("ac", 1)])
        mt2 = s.tick("mm")
        prev_norm_mm = mt2
        s.op("T", lambda eng: eng.matmul(psM[:, 0:1], negk1[0:1, :], nr[0:1, 20:21],
                                         start=True, stop=True),
             waits=[("ac", at3)], incs=[("mm", 1)])
        s.ser("V", "ve", (lambda hp=hp: lambda eng: eng.tensor_copy(
            mhneg[:, hp:hp + 1], psM[:, 0:1]))(),
            waits=[("mm", mt2)])
        s.op("T", lambda eng: eng.nop(), waits=[("ve", s.val("ve"))])


L0_OFF = [0, 20, 44, 72]


def emit_l0_attn(c, psS, psO, psC):
    prev_evac_ve = 0
    prev_bcast_mm = 0
    """psS: [128, 2048] f32 (4 banks: two sc-slots of [128,1024]);
    psO: [128, 1024] f32 (outT_A bank, outT_B bank); psC: [128, 1024] bcast.
    biascomb [128, 104] f32 = biasmod + mhneg[hp] (computed per hp here);
    masks [128, 8*1024] bf16; p_sb [128, 2048] bf16 (2 slots);
    out: attn_fm [128, 8*T] bf16; linv [1, 1024] f32; osb [128, 1024] bf16.
    """
    s = c.s
    T = 2048
    q = c.t["q"]
    kdup = c.t["kdup"]
    vaug = c.t["vaug"]
    attn = c.t["attn"]
    p_sb = c.t["p_sb"]
    masks = c.t["masks"]
    biasmod = c.t["biasmod"]
    biascomb = c.t["biascomb"]
    mhneg = c.t["mhneg"]
    linv = c.t["linv"]
    osb = c.t["osb"]
    for hp in range(8):
        vtb = s.tick("ve")
        s.op("V", (lambda hp=hp: lambda eng: eng.tensor_scalar(
            biascomb[:, :], biasmod[:, :], mhneg[:, hp:hp + 1], None,
            op0=ALU.add))(), incs=[("ve", 1)])
        s.op("V", lambda eng: eng.nop(), waits=[("ve", vtb)])
        for j in range(4):
            nsc = 4 * j + 20
            exp_tick = {}
            av_tick = {}
            mask_tick = {}
            for sci in range(nsc):
                sc = sci
                slot = sci % 2
                pss = psS[:, slot * 1024: slot * 1024 + 1024]
                w = []
                if sci == 0:
                    w.append(("ve", vtb))
                if sci >= 2:
                    w.append(("ac", exp_tick[sci - 2]))
                s.op("T", (lambda hp=hp, j=j, sc=sc, pss=pss: lambda eng: eng.matmul(
                    pss[:, 0:512], kdup[0:64, sc * 128:(sc + 1) * 128],
                    q[0:64, hp * T + j * 512: hp * T + j * 512 + 512],
                    start=True, stop=True, tile_position=(0, 0)))(), waits=w)
                mt = s.tick("mm")
                s.op("T", (lambda hp=hp, j=j, sc=sc, pss=pss: lambda eng: eng.matmul(
                    pss[:, 512:1024], kdup[64:128, sc * 128:(sc + 1) * 128],
                    q[64:128, hp * T + j * 512: hp * T + j * 512 + 512],
                    start=True, stop=True, tile_position=(64, 0)))(),
                    incs=[("mm", 1)])
                aw = [("mm", mt)]
                if sci >= 2:
                    aw.append(("mm", av_tick[sci - 2]))
                at = s.tick("ac")
                exp_tick[sci] = at
                col = L0_OFF[j] + sc
                s.op("A", (lambda pss=pss, slot=slot, col=col: lambda eng:
                           eng.activation(p_sb[:, slot * 1024: slot * 1024 + 1024],
                                          pss, AF.Exp,
                                          bias=biascomb[:, col:col + 1],
                                          scale=0.125))(),
                    waits=aw, incs=[("ac", 1)])
                grp = None
                if sc >= 4 * j and sc < 4 * j + 4:
                    grp = sc - 4 * j
                elif sc >= 16 + 4 * j and sc < 20 + 4 * j:
                    grp = 4 + (sc - 16 - 4 * j)
                need_mask = grp is not None
                if need_mask:
                    vt = s.tick("ve")
                    mask_tick[sci] = vt
                    s.op("V", (lambda slot=slot, grp=grp: lambda eng:
                               eng.tensor_mul(
                                   p_sb[:, slot * 1024: slot * 1024 + 1024],
                                   p_sb[:, slot * 1024: slot * 1024 + 1024],
                                   masks[:, grp * 1024:(grp + 1) * 1024]))(),
                        waits=[("ac", at)], incs=[("ve", 1)])
                avw = [("ac", at)]
                if need_mask:
                    avw.append(("ve", mask_tick[sci]))
                st0 = (sci == 0)
                sp = (sci == nsc - 1)
                s.op("T", (lambda slot=slot, sc=sc, st0=st0, sp=sp: lambda eng:
                           eng.matmul(psO[0:65, 0:512],
                                      vaug[:, sc * 65: sc * 65 + 65],
                                      p_sb[:, slot * 1024: slot * 1024 + 512],
                                      start=st0, stop=sp))(), waits=avw)
                mt2 = s.tick("mm")
                av_tick[sci] = mt2
                s.op("T", (lambda slot=slot, sc=sc, st0=st0, sp=sp: lambda eng:
                           eng.matmul(psO[0:65, 512:1024],
                                      vaug[:, sc * 65: sc * 65 + 65],
                                      p_sb[:, slot * 1024 + 512: slot * 1024 + 1024],
                                      start=st0, stop=sp))(), incs=[("mm", 1)])
            av_last = av_tick[nsc - 1]
            vt = s.tick("ve")
            rw_ = [("mm", av_last)]
            if prev_bcast_mm:
                rw_.append(("mm", prev_bcast_mm))
            s.op("V", lambda eng: eng.reciprocal(linv[0:1, 0:512], psO[64:65, 0:512]),
                 waits=rw_)
            s.op("V", lambda eng: eng.reciprocal(linv[0:1, 512:1024],
                                                 psO[64:65, 512:1024]),
                 incs=[("ve", 1)])
            s.op("V", lambda eng: eng.nop(), waits=[("ve", vt)])
            mt3 = s.tick("mm")
            s.op("T", lambda eng: eng.matmul(psC[:, 0:512], c.t["onesk1"][0:1, :],
                                             linv[0:1, 0:512], start=True, stop=True),
                 waits=[("ve", vt)])
            s.op("T", lambda eng: eng.matmul(psC[:, 512:1024], c.t["onesk1"][0:1, :],
                                             linv[0:1, 512:1024], start=True,
                                             stop=True), incs=[("mm", 1)])
            at4 = s.tick("ac")
            aw_ = [("mm", av_last)]
            if prev_evac_ve:
                aw_.append(("ve", prev_evac_ve))
            s.op("A", lambda eng: eng.activation(osb[0:64, 0:512], psO[0:64, 0:512],
                                                 AF.Copy), waits=aw_)
            s.op("A", lambda eng: eng.activation(osb[64:128, 512:1024],
                                                 psO[0:64, 512:1024], AF.Copy),
                 incs=[("ac", 1)])
            ve5 = s.tick("ve", 2)
            s.op("V", (lambda hp=hp, j=j: lambda eng: eng.tensor_mul(
                attn[0:64, hp * T + j * 512: hp * T + j * 512 + 512],
                osb[0:64, 0:512], psC[0:64, 0:512]))(),
                waits=[("mm", mt3), ("ac", at4)], incs=[("ve", 1)])
            s.op("V", (lambda hp=hp, j=j: lambda eng: eng.tensor_mul(
                attn[64:128, hp * T + j * 512: hp * T + j * 512 + 512],
                osb[64:128, 512:1024], psC[64:128, 512:1024]))(),
                incs=[("ve", 1)])
            prev_evac_ve = s.val("ve")
            prev_bcast_mm = mt3
            s.op("T", lambda eng: eng.nop(), waits=[("ve", s.val("ve"))])


def emit_residual_proj(c, w_dram, act_name, x_name, ndc, nfc, T, ps2, wbuf):
    x = c.t[x_name]
    s = c.s

    def evac(fc, tg, pslice, mmv):
        s.tick("ve")
        s.op("V", (lambda fc=fc, tg=tg, pslice=pslice: lambda eng: eng.tensor_add(
            x[:, fc * T + tg * 512: fc * T + tg * 512 + 512],
            x[:, fc * T + tg * 512: fc * T + tg * 512 + 512], pslice))(),
            waits=[("mm", mmv)], incs=[("ve", 1)])
    emit_projA(c, w_dram, act_name, None, ndc, nfc, T, ps2, wbuf, evac=evac)


def emit_router_l0(c, rw_sb_name, xn_name, psR, psSt):
    s = c.s
    T = 2048
    xn = c.t[xn_name]
    rw = c.t[rw_sb_name]
    lt = c.t["rlt"]
    rr = c.t["rrows"]
    eq = c.t["reqm"]
    mk = c.t["rmk"]
    msk = c.t["rmsk"]
    re = c.t["re_"]
    mwb = c.t["rmwb"]
    for tt in range(16):
        slot = tt % 2
        psr = psR[:, slot * 512: slot * 512 + 8]
        for dc in range(8):
            w = []
            if tt >= 2 and dc == 0:
                w.append(("ve", c.t[("r_ev", tt - 2)]))
            st0 = (dc == 0)
            sp = (dc == 7)
            inc = []
            if sp:
                s.tick("mm")
                inc = [("mm", 1)]
            s.op("T", (lambda tt=tt, dc=dc, st0=st0, sp=sp, psr=psr: lambda eng:
                       eng.matmul(psr, xn[:, dc * T + tt * 128: dc * T + tt * 128 + 128],
                                  rw[:, dc * 8:(dc + 1) * 8], start=st0, stop=sp))(),
                waits=w, incs=inc)
        mmv = s.val("mm")
        vw = [("mm", mmv)]
        if tt >= 1:
            vw.append(("ac", c.t[("r_ac", tt - 1)]))
            vw.append(("mm", c.t[("r_mt", tt - 1)]))
        ev = s.tick("ve")
        c.t[("r_ev", tt)] = ev
        s.op("V", (lambda psr=psr: lambda eng: eng.tensor_copy(lt, psr))(),
             waits=vw, incs=[("ve", 1)])
        s.op("V", lambda eng: eng.nop(), waits=[("ve", ev)])
        s.ser("V", "ve", lambda eng: eng.reduce_max(rr[:, 0:1], lt, axis=AX.X))
        s.ser("V", "ve", lambda eng: eng.tensor_scalar(eq, lt, rr[:, 0:1], None,
                                                       op0=ALU.is_equal))
        s.ser("V", "ve", lambda eng: eng.scalar_tensor_tensor(
            mk, eq, -1e30, lt, op0=ALU.mult, op1=ALU.add))
        s.ser("V", "ve", lambda eng: eng.reduce_max(rr[:, 1:2], mk, axis=AX.X))
        s.ser("V", "ve", lambda eng: eng.tensor_scalar(msk, lt, rr[:, 1:2], None,
                                                       op0=ALU.is_ge))
        vt = s.tick("ve")
        s.op("V", lambda eng: eng.tensor_scalar_mul(rr[:, 4:5], rr[:, 0:1], -1.0),
             incs=[("ve", 1)])
        at = s.tick("ac")
        c.t[("r_ac", tt)] = at
        s.op("A", lambda eng: eng.activation(re, lt, AF.Exp, bias=rr[:, 4:5]),
             waits=[("ve", vt)], incs=[("ac", 1)])
        s.ser("V", "ve", lambda eng: eng.tensor_mul(re, re, msk),
              waits=[("ac", at)])
        s.ser("V", "ve", lambda eng: eng.reduce_sum(rr[:, 2:3], re, axis=AX.X))
        s.ser("V", "ve", lambda eng: eng.reciprocal(rr[:, 3:4], rr[:, 2:3]))
        s.op("V", lambda eng: eng.tensor_copy(mwb[:, 0:8], msk))
        vt2 = s.tick("ve")
        s.op("V", lambda eng: eng.tensor_scalar(mwb[:, 8:16], re,
                                                rr[:, 3:4], None, op0=ALU.mult),
             incs=[("ve", 1)])
        st0 = (tt == 0)
        sp = (tt == 15)
        mt_ = s.tick("mm")
        c.t[("r_mt", tt)] = mt_
        s.op("T", (lambda st0=st0, sp=sp: lambda eng: eng.matmul(
            psSt[0:16, 0:1], mwb, c.t["onesd"][:, 0:1], start=st0, stop=sp))(),
            waits=[("ve", vt2)], incs=[("mm", 1)])
    mml = s.val("mm")
    vt3 = s.tick("ve")
    s.op("V", lambda eng: eng.tensor_copy(c.t["partials"], psSt[0:16, 0:1]),
         waits=[("mm", mml)], incs=[("ve", 1)])




def emit_l1_vproj(c, wv_sb_name, xn_name, vloc_name, psV):
    s = c.s
    T = 2048
    xn = c.t[xn_name]
    wv = c.t[wv_sb_name]
    vloc = c.t[vloc_name]
    for tt in range(16):
        slot = tt % 2
        psv = psV[:, slot * 1024: slot * 1024 + 1024]
        for dc in range(8):
            w = []
            if dc == 0 and tt >= 2:
                w.append(("ve", c.t[("v_ev", tt - 2)]))
            st0 = (dc == 0)
            sp = (dc == 7)
            for fh in range(2):
                inc = []
                if sp and fh == 1:
                    s.tick("mm")
                    inc = [("mm", 1)]
                s.op("T", (lambda tt=tt, dc=dc, fh=fh, st0=st0, sp=sp, psv=psv:
                           lambda eng: eng.matmul(
                    psv[:, fh * 512:(fh + 1) * 512],
                    xn[:, dc * T + tt * 128: dc * T + tt * 128 + 128],
                    wv[:, dc * 1024 + fh * 512: dc * 1024 + fh * 512 + 512],
                    start=st0, stop=sp))(), waits=w if fh == 0 else (),
                    incs=inc)
        mmv = s.val("mm")
        ev = s.tick("ve")
        c.t[("v_ev", tt)] = ev
        s.op("V", (lambda tt=tt, psv=psv: lambda eng: eng.tensor_copy(
            vloc[:, tt * 1024:(tt + 1) * 1024], psv))(),
            waits=[("mm", mmv)], incs=[("ve", 1)])


def emit_l1_norm_mhat(c, psN, psM, hp, k_ready):
    s = c.s
    T = 2048
    q = c.t["q"]
    nr = c.t["nrows"]
    onesd = c.t["onesd"]
    negk1 = c.t["negk1"]
    mhneg = c.t["mhneg"]
    qsq = c.t["qsq"]
    ksq = c.t["ksq"]
    at = s.tick("ac")
    aw = list(k_ready)
    if c.t.get("l1n_mm"):
        aw.append(("mm", c.t["l1n_mm"]))
    s.op("A", (lambda hp=hp: lambda eng: eng.activation(
        qsq[:, :], q[:, hp * T:(hp + 1) * T], AF.Square))(), waits=aw,
        incs=[("ac", 1)])
    s.ser("V", "ve", lambda eng: eng.memset(nr[0:1, 8:9], 0.0))
    for tg in range(4):
        for half in range(2):
            mt = s.tick("mm")
            s.op("T", (lambda tg=tg, half=half: lambda eng: eng.matmul(
                psN[0:1, 512:1024], onesd[half * 64:half * 64 + 64, 0:1],
                qsq[half * 64:half * 64 + 64, tg * 512:(tg + 1) * 512],
                start=True, stop=True, tile_position=(half * 64, 0)))(),
                waits=[("ac", at), ("ve", s.val("ve"))], incs=[("mm", 1)])
            s.ser("V", "ve", (lambda: lambda eng: eng.reduce_max(
                nr[0:1, 9:10], psN[0:1, 512:1024], axis=AX.X))(),
                waits=[("mm", mt)])
            s.ser("V", "ve", lambda eng: eng.tensor_max(
                nr[0:1, 8:9], nr[0:1, 8:9], nr[0:1, 9:10]))
    s.ser("V", "ve", lambda eng: eng.memset(nr[0:1, 0:1], 0.0))
    for which, tname in ((0, "khp"), (1, "kphp")):
        at2 = s.tick("ac")
        s.op("A", (lambda tname=tname: lambda eng: eng.activation(
            ksq[:, :], c.t[tname][:, :], AF.Square))(),
            waits=[("ve", s.val("ve"))], incs=[("ac", 1)])
        for tg in range(8):
            for half in range(2):
                mt = s.tick("mm")
                s.op("T", (lambda tg=tg, half=half: lambda eng: eng.matmul(
                    psN[0:1, 0:512], onesd[half * 64:half * 64 + 64, 0:1],
                    ksq[half * 64:half * 64 + 64, tg * 512:(tg + 1) * 512],
                    start=True, stop=True, tile_position=(half * 64, 0)))(),
                    waits=[("ac", at2), ("ve", s.val("ve"))], incs=[("mm", 1)])
                s.ser("V", "ve", (lambda: lambda eng: eng.reduce_max(
                    nr[0:1, 4:5], psN[0:1, 0:512], axis=AX.X))(),
                    waits=[("mm", mt)])
                s.ser("V", "ve", lambda eng: eng.tensor_max(
                    nr[0:1, 0:1], nr[0:1, 0:1], nr[0:1, 4:5]))
    at2 = s.tick("ac")
    s.op("A", lambda eng: eng.activation(nr[0:1, 16:17], nr[0:1, 0:1], AF.Ln),
         waits=[("ve", s.val("ve"))])
    s.op("A", lambda eng: eng.activation(nr[0:1, 17:18], nr[0:1, 8:9], AF.Ln),
         incs=[("ac", 1)])
    s.ser("V", "ve", lambda eng: eng.tensor_add(nr[0:1, 18:19], nr[0:1, 16:17],
                                                nr[0:1, 17:18]),
          waits=[("ac", at2)])
    vt3 = s.tick("ve")
    s.op("V", lambda eng: eng.tensor_scalar(
        nr[0:1, 19:20], nr[0:1, 18:19], 0.5, -2.0794415416798357,
        op0=ALU.mult, op1=ALU.add), incs=[("ve", 1)])
    at3 = s.tick("ac")
    s.op("A", lambda eng: eng.activation(nr[0:1, 20:21], nr[0:1, 19:20], AF.Exp),
         waits=[("ve", vt3)], incs=[("ac", 1)])
    mt2 = s.tick("mm")
    c.t["l1n_mm"] = mt2
    s.op("T", lambda eng: eng.matmul(psM[:, 0:1], negk1[0:1, :], nr[0:1, 20:21],
                                     start=True, stop=True),
         waits=[("ac", at3)], incs=[("mm", 1)])
    s.ser("V", "ve", (lambda hp=hp: lambda eng: eng.tensor_copy(
        mhneg[:, hp:hp + 1], psM[:, 0:1]))(), waits=[("mm", mt2)])


def emit_l1_attn_hp(c, psS, psO, hp):
    s = c.s
    T = 2048
    q = c.t["q"]
    attn = c.t["attn"]
    p_sb = c.t["p_sb"]
    mhneg = c.t["mhneg"]
    linv = c.t["linv2"]
    stage = {0: c.t["ostd"], 1: c.t["osp"]}
    alpha_sc = c.t["alpha_sc"]
    st_ = c.t.setdefault("l1a_state", {"prev_av": (0, 0), "prev_ev": 0,
                                       "prev_bc": 0, "prev_comb": 0,
                                       "stage_ac": {0: 0, 1: 0}})
    for j in range(4):
        for var in range(2):
            ksrc = c.t["khp"] if var == 0 else c.t["kphp"]
            exp_tick = {}
            av_tick = {}
            for sci in range(32):
                slot = sci % 2
                pss = psS[:, slot * 1024: slot * 1024 + 1024]
                w = []
                if sci >= 2:
                    w.append(("ac", exp_tick[sci - 2]))
                elif st_["prev_comb"]:
                    w.append(("ve", st_["prev_comb"]))
                s.op("T", (lambda sci=sci, pss=pss, ksrc=ksrc, hp=hp, j=j:
                           lambda eng: eng.matmul(
                    pss[:, 0:512], ksrc[0:64, sci * 128:(sci + 1) * 128],
                    q[0:64, hp * T + j * 512: hp * T + j * 512 + 512],
                    start=True, stop=True, tile_position=(0, 0)))(), waits=w)
                mt = s.tick("mm")
                s.op("T", (lambda sci=sci, pss=pss, ksrc=ksrc, hp=hp, j=j:
                           lambda eng: eng.matmul(
                    pss[:, 512:1024], ksrc[64:128, sci * 128:(sci + 1) * 128],
                    q[64:128, hp * T + j * 512: hp * T + j * 512 + 512],
                    start=True, stop=True, tile_position=(64, 0)))(),
                    incs=[("mm", 1)])
                aw = [("mm", mt)]
                if sci >= 2:
                    aw.append(("mm", av_tick[sci - 2]))
                else:
                    if st_["prev_av"][sci]:
                        aw.append(("mm", st_["prev_av"][sci]))
                at = s.tick("ac")
                exp_tick[sci] = at
                s.op("A", (lambda pss=pss, slot=slot, hp=hp: lambda eng:
                           eng.activation(p_sb[:, slot * 1024: slot * 1024 + 1024],
                                          pss, AF.Exp,
                                          bias=mhneg[:, hp:hp + 1],
                                          scale=0.125))(),
                    waits=aw, incs=[("ac", 1)])
                st0 = (sci == 0)
                sp = (sci == 31)
                avw = [("ac", at)]
                if sci == 0 and st_["prev_ev"]:
                    avw.append(("ve", st_["prev_ev"]))
                s.op("T", (lambda slot=slot, sci=sci, st0=st0, sp=sp: lambda eng:
                           eng.matmul(psO[0:65, 0:512],
                                      c.t["vaughp"][:, sci * 65: sci * 65 + 65],
                                      p_sb[:, slot * 1024: slot * 1024 + 512],
                                      start=st0, stop=sp))(), waits=avw)
                mt2 = s.tick("mm")
                av_tick[sci] = mt2
                s.op("T", (lambda slot=slot, sci=sci, st0=st0, sp=sp: lambda eng:
                           eng.matmul(psO[0:65, 512:1024],
                                      c.t["vaughp"][:, 2080 + sci * 65:
                                                    2080 + sci * 65 + 65],
                                      p_sb[:, slot * 1024 + 512: slot * 1024 + 1024],
                                      start=st0, stop=sp))(), incs=[("mm", 1)])
            av_last = av_tick[31]
            st_["prev_av"] = (av_tick[30], av_tick[31])
            vr = [("mm", av_last)]
            if st_["prev_bc"]:
                vr.append(("mm", st_["prev_bc"]))
            vt = s.tick("ve")
            s.op("V", (lambda var=var: lambda eng: eng.reciprocal(
                linv[0:1, var * 1024: var * 1024 + 512], psO[64:65, 0:512]))(),
                waits=vr)
            s.op("V", (lambda var=var: lambda eng: eng.reciprocal(
                linv[0:1, var * 1024 + 512: var * 1024 + 1024],
                psO[64:65, 512:1024]))(), incs=[("ve", 1)])
            s.op("V", lambda eng: eng.nop(), waits=[("ve", vt)])
            s.ser("V", "ve", (lambda var=var: lambda eng: eng.tensor_scalar(
                linv[0:1, var * 1024: var * 1024 + 1024],
                linv[0:1, var * 1024: var * 1024 + 1024],
                alpha_sc[0:1, var:var + 1], None, op0=ALU.mult))())
            at4 = s.tick("ac")
            aw_ = [("mm", av_last)]
            if st_["prev_comb"]:
                aw_.append(("ve", st_["prev_comb"]))
            s.op("A", (lambda var=var: lambda eng: eng.activation(
                stage[var][0:64, 0:512], psO[0:64, 0:512], AF.Copy))(), waits=aw_)
            s.op("A", (lambda var=var: lambda eng: eng.activation(
                stage[var][64:128, 512:1024], psO[0:64, 512:1024], AF.Copy))(),
                incs=[("ac", 1)])
            st_["stage_ac"][var] = at4
            ev = s.tick("ve")
            s.op("V", lambda eng: eng.nop(), waits=[("ac", at4)], incs=[("ve", 1)])
            st_["prev_ev"] = ev
        lve = s.val("ve")
        mtb = s.tick("mm")
        s.op("T", lambda eng: eng.matmul(psS[:, 0:512], c.t["onesk1"][0:1, :],
                                         linv[0:1, 0:512], start=True, stop=True),
             waits=[("ve", lve)])
        s.op("T", lambda eng: eng.matmul(psS[:, 512:1024], c.t["onesk1"][0:1, :],
                                         linv[0:1, 512:1024], start=True, stop=True))
        s.op("T", lambda eng: eng.matmul(psS[:, 1024:1536], c.t["onesk1"][0:1, :],
                                         linv[0:1, 1024:1536], start=True, stop=True))
        s.op("T", lambda eng: eng.matmul(psS[:, 1536:2048], c.t["onesk1"][0:1, :],
                                         linv[0:1, 1536:2048], start=True,
                                         stop=True), incs=[("mm", 1)])
        st_["prev_bc"] = mtb
        t0 = c.t["l1t0"]
        t1 = c.t["l1t1"]
        s.tick("ve", 6)
        s.op("V", (lambda hp=hp, j=j: lambda eng: eng.tensor_mul(
            t0[0:64, :], c.t["ostd"][0:64, 0:512], psS[0:64, 0:512]))(),
            waits=[("mm", mtb), ("ac", st_["stage_ac"][1])], incs=[("ve", 1)])
        s.op("V", (lambda hp=hp, j=j: lambda eng: eng.tensor_mul(
            t1[0:64, :], c.t["osp"][0:64, 0:512], psS[0:64, 1024:1536]))(),
            incs=[("ve", 1)])
        s.op("V", (lambda hp=hp, j=j: lambda eng: eng.tensor_add(
            attn[0:64, hp * T + j * 512: hp * T + j * 512 + 512],
            t0[0:64, :], t1[0:64, :]))(), incs=[("ve", 1)])
        s.op("V", (lambda hp=hp, j=j: lambda eng: eng.tensor_mul(
            t0[64:128, :], c.t["ostd"][64:128, 512:1024], psS[64:128, 512:1024]))(),
            incs=[("ve", 1)])
        s.op("V", (lambda hp=hp, j=j: lambda eng: eng.tensor_mul(
            t1[64:128, :], c.t["osp"][64:128, 512:1024], psS[64:128, 1536:2048]))(),
            incs=[("ve", 1)])
        s.op("V", (lambda hp=hp, j=j: lambda eng: eng.tensor_add(
            attn[64:128, hp * T + j * 512: hp * T + j * 512 + 512],
            t0[64:128, :], t1[64:128, :]))(), incs=[("ve", 1)])
        ve_end = s.val("ve")
        st_["prev_comb"] = ve_end
        s.op("T", lambda eng: eng.nop(), waits=[("ve", ve_end)])


def emit_stats_meanw(c):
    s = c.s
    pt = c.t["partials_T"]
    mw = c.t["mw"]
    r = c.t["strows"]
    s.ser("V", "ve", lambda eng: eng.reduce_sum(r[:, 0:1], pt[:, 0:8], axis=AX.X))
    s.ser("V", "ve", lambda eng: eng.reduce_sum(r[:, 1:2], pt[:, 8:16], axis=AX.X))
    s.ser("V", "ve", lambda eng: eng.tensor_scalar_max(r[:, 2:3], r[:, 0:1], 1.0))
    s.ser("V", "ve", lambda eng: eng.reciprocal(r[:, 2:3], r[:, 2:3]))
    s.ser("V", "ve", lambda eng: eng.tensor_mul(mw, r[:, 1:2], r[:, 2:3]))
    s.ser("V", "ve", lambda eng: eng.tensor_scalar(r[:, 3:4], r[:, 0:1], 0.0, None,
                                                   op0=ALU.is_gt))
    s.ser("V", "ve", lambda eng: eng.tensor_mul(mw, mw, r[:, 3:4]))


def emit_router_masks_coef(c, rw_sb_name, xn_name, psR, psT, mw_name, coef_name,
                           per_token_w):
    s = c.s
    T = 2048
    xn = c.t[xn_name]
    rw = c.t[rw_sb_name]
    lt = c.t["rlt"]
    rr = c.t["rrows"]
    eq = c.t["reqm"]
    mk = c.t["rmk"]
    msk = c.t["rmsk"]
    re = c.t["re_"]
    cf16 = c.t["rcf16"]
    coef = c.t[coef_name]
    identb = c.t["identb"]
    psTb = c.t["psTb_ap"]
    for tt in range(16):
        slot = tt % 2
        psr = psR[:, slot * 512: slot * 512 + 8]
        for dc in range(8):
            w = []
            if tt >= 2 and dc == 0:
                w.append(("ve", c.t[("rc_ev", tt - 2)]))
            st0 = (dc == 0)
            sp = (dc == 7)
            inc = []
            if sp:
                s.tick("mm")
                inc = [("mm", 1)]
            s.op("T", (lambda tt=tt, dc=dc, st0=st0, sp=sp, psr=psr: lambda eng:
                       eng.matmul(psr, xn[:, dc * T + tt * 128: dc * T + tt * 128 + 128],
                                  rw[:, dc * 8:(dc + 1) * 8], start=st0, stop=sp))(),
                waits=w, incs=inc)
        mmv = s.val("mm")
        vw = [("mm", mmv)]
        if tt >= 1:
            vw.append(("ac", c.t[("rc_ac", tt - 1)]))
            vw.append(("mm", c.t[("rc_mt", tt - 1)]))
        ev = s.tick("ve")
        c.t[("rc_ev", tt)] = ev
        s.op("V", (lambda psr=psr: lambda eng: eng.tensor_copy(lt, psr))(),
             waits=vw, incs=[("ve", 1)])
        s.op("V", lambda eng: eng.nop(), waits=[("ve", ev)])
        s.ser("V", "ve", lambda eng: eng.reduce_max(rr[:, 0:1], lt, axis=AX.X))
        s.ser("V", "ve", lambda eng: eng.tensor_scalar(eq, lt, rr[:, 0:1], None,
                                                       op0=ALU.is_equal))
        s.ser("V", "ve", lambda eng: eng.scalar_tensor_tensor(
            mk, eq, -1e30, lt, op0=ALU.mult, op1=ALU.add))
        s.ser("V", "ve", lambda eng: eng.reduce_max(rr[:, 1:2], mk, axis=AX.X))
        s.ser("V", "ve", lambda eng: eng.tensor_scalar(msk, lt, rr[:, 1:2], None,
                                                       op0=ALU.is_ge))
        if per_token_w:
            vt = s.tick("ve")
            s.op("V", lambda eng: eng.tensor_scalar_mul(rr[:, 4:5], rr[:, 0:1], -1.0),
                 incs=[("ve", 1)])
            at0 = s.tick("ac")
            s.op("A", lambda eng: eng.activation(re, lt, AF.Exp, bias=rr[:, 4:5]),
                 waits=[("ve", vt)], incs=[("ac", 1)])
            s.ser("V", "ve", lambda eng: eng.reduce_sum(rr[:, 2:3], re, axis=AX.X),
                  waits=[("ac", at0)])
            s.ser("V", "ve", lambda eng: eng.reciprocal(rr[:, 3:4], rr[:, 2:3]))
            s.ser("V", "ve", lambda eng: eng.tensor_scalar(re, re, rr[:, 3:4], None,
                                                           op0=ALU.mult))
            vt1 = s.val("ve")
            at = s.tick("ac")
            c.t[("rc_ac", tt)] = at
            s.op("A", lambda eng: eng.activation(eq, re, AF.Exp),
                 waits=[("ve", vt1)], incs=[("ac", 1)])
            s.ser("V", "ve", lambda eng: eng.tensor_mul(eq, eq, msk),
                  waits=[("ac", at)])
            s.ser("V", "ve", lambda eng: eng.reduce_sum(rr[:, 6:7], eq, axis=AX.X))
            s.ser("V", "ve", lambda eng: eng.reciprocal(rr[:, 7:8], rr[:, 6:7]))
            vt2 = s.tick("ve")
            s.op("V", lambda eng: eng.tensor_scalar(cf16, eq, rr[:, 7:8], None,
                                                    op0=ALU.mult),
                 incs=[("ve", 1)])
        else:
            c.t[("rc_ac", tt)] = s.val("ac")
            vt2 = s.tick("ve")
            s.op("V", lambda eng: eng.tensor_copy(cf16, msk), incs=[("ve", 1)])
        mt_ = s.tick("mm")
        c.t[("rc_mt", tt)] = mt_
        s.op("T", (lambda tt=tt: lambda eng: eng.transpose(
            psTb[0:8, 0:128], cf16, identb))(),
            waits=[("ve", vt2)] + ([("ve", c.t[("rc_tev", tt - 1)])] if tt else []),
            incs=[("mm", 1)])
        tev = s.tick("ve")
        c.t[("rc_tev", tt)] = tev
        if per_token_w:
            s.op("V", (lambda tt=tt: lambda eng: eng.tensor_copy(
                coef[0:8, tt * 128:(tt + 1) * 128], psTb[0:8, 0:128]))(),
                waits=[("mm", mt_)], incs=[("ve", 1)])
        else:
            s.op("V", (lambda tt=tt: lambda eng: eng.tensor_scalar(
                coef[0:8, tt * 128:(tt + 1) * 128], psTb[0:8, 0:128],
                c.t[mw_name][:, 0:1], None, op0=ALU.mult))(),
                waits=[("mm", mt_)], incs=[("ve", 1)])


def emit_moe(c, gw_dram, uw_dram, dw_dram, xn_name, x_name, coef_name,
             nE, nI, T, psGU0, psGU1):
    s = c.s
    xn = c.t[xn_name]
    x = c.t[x_name]
    coef = c.t[coef_name]
    h = c.t["h_sb"]
    sg = c.t["sg_sb"]
    gub = c.t["gub"]
    dwb = c.t["dwb"]
    sel8 = c.t["sel8"]
    cb = c.t["cb_sb"]
    nic = nI // 128
    for e in range(nE):
        for th in range(2):
            for ic in range(nic):
                gslot = ic % 2
                wt = s.tick("wl", 32)
                w = []
                if (e, th, ic) != (0, 0, 0):
                    w.append(("mm", c.t.get(("moe_gu_mm", gslot), 0)))
                s.op("P", (lambda e=e, ic=ic, gslot=gslot: lambda eng: eng.dma_start(
                    out=gub[:, gslot * 2048: gslot * 2048 + 1024],
                    in_=gw_dram[e, ic]))(), waits=w, incs=[("wl", 16)])
                s.op("P", (lambda e=e, ic=ic, gslot=gslot: lambda eng: eng.dma_start(
                    out=gub[:, gslot * 2048 + 1024: gslot * 2048 + 2048],
                    in_=uw_dram[e, ic]))(), incs=[("wl", 16)])
                psgu = psGU0 if gslot == 0 else psGU1
                for tgl in range(2):
                    tg = th * 2 + tgl
                    for dc in range(8):
                        w = []
                        if dc == 0 and tgl == 0:
                            w.append(("wl", wt))
                            w.append(("ve", c.t.get(("moe_gu_ve", gslot), 0)))
                        st0 = (dc == 0)
                        sp = (dc == 7)
                        s.op("T", (lambda ic=ic, tg=tg, tgl=tgl, dc=dc, st0=st0,
                                   sp=sp, gslot=gslot, psgu=psgu: lambda eng:
                                   eng.matmul(
                            psgu[:, tgl * 512: tgl * 512 + 512],
                            gub[:, gslot * 2048 + dc * 128: gslot * 2048 + dc * 128 + 128],
                            xn[:, dc * T + tg * 512: dc * T + tg * 512 + 512],
                            start=st0, stop=sp))(), waits=w)
                        inc = []
                        if sp:
                            s.tick("mm")
                            inc = [("mm", 1)]
                        s.op("T", (lambda ic=ic, tg=tg, tgl=tgl, dc=dc, st0=st0,
                                   sp=sp, gslot=gslot, psgu=psgu: lambda eng:
                                   eng.matmul(
                            psgu[:, 1024 + tgl * 512: 1024 + tgl * 512 + 512],
                            gub[:, gslot * 2048 + 1024 + dc * 128:
                                gslot * 2048 + 1024 + dc * 128 + 128],
                            xn[:, dc * T + tg * 512: dc * T + tg * 512 + 512],
                            start=st0, stop=sp))(), incs=inc)
                mmv = s.val("mm")
                c.t[("moe_gu_mm", gslot)] = mmv
                at = s.tick("ac")
                sgw = [("mm", mmv)]
                if c.t.get(("moe_sg_ve", gslot)):
                    sgw.append(("ve", c.t[("moe_sg_ve", gslot)]))
                s.op("A", (lambda psgu=psgu, gslot=gslot: lambda eng: eng.activation(
                    sg[:, gslot * 1024: gslot * 1024 + 1024],
                    psgu[:, 0:1024], AF.Silu))(), waits=sgw, incs=[("ac", 1)])
                ve = s.tick("ve")
                c.t[("moe_gu_ve", gslot)] = ve
                c.t[("moe_sg_ve", gslot)] = ve
                s.op("V", (lambda ic=ic, psgu=psgu, gslot=gslot: lambda eng:
                           eng.tensor_mul(h[:, ic * 1024:(ic + 1) * 1024],
                                          sg[:, gslot * 1024: gslot * 1024 + 1024],
                                          psgu[:, 1024:2048]))(),
                    waits=[("ac", at)], incs=[("ve", 1)])
            gu_all = s.val("ve")
            s.op("T", lambda eng: eng.nop(), waits=[("ve", gu_all)])
            hsz = nic * 128 // 2
            for fc in range(8):
                for half in range(2):
                    wt = s.tick("wl", 16)
                    w = []
                    if (fc, half) != (0, 0):
                        w.append(("mm", c.t.get(("moe_dw_mm", half), 0)))
                    s.op("P", (lambda e=e, fc=fc, half=half, hsz=hsz: lambda eng:
                               eng.dma_start(
                        out=dwb[:, half * hsz: half * hsz + hsz],
                        in_=dw_dram[e, fc, :, half * hsz: half * hsz + hsz]))(),
                        waits=w, incs=[("wl", 16)])
                    c.t[("moe_dw_wt", half)] = wt
                for tgl in range(2):
                    tg = th * 2 + tgl
                    dslot = (fc * 2 + tgl) % 2
                    psd = psGU0[:, dslot * 512: dslot * 512 + 512]
                    if fc == 0:
                        mtc = s.tick("mm")
                        s.op("T", (lambda e=e, tg=tg: lambda eng: eng.matmul(
                            psGU1[:, 1024:1536], sel8[:, e * 128:(e + 1) * 128],
                            coef[0:8, tg * 512: tg * 512 + 512],
                            start=True, stop=True))(),
                            waits=[("ac", c.t.get("moe_cb_ac_prev", 0))],
                            incs=[("mm", 1)])
                        at = s.tick("ac")
                        s.op("A", (lambda tgl=tgl: lambda eng: eng.activation(
                            cb[:, tgl * 512: tgl * 512 + 512],
                            psGU1[:, 1024:1536], AF.Copy))(),
                            waits=[("mm", mtc)], incs=[("ac", 1)])
                        c.t[("moe_cb_ac", tgl)] = at
                        c.t["moe_cb_ac_prev"] = at
                    for ic in range(nic):
                        half = ic // (nic // 2)
                        w = []
                        if ic == 0:
                            w.append(("wl", c.t[("moe_dw_wt", 0)]))
                            w.append(("ve", c.t.get(("moe_dn_ve", dslot), 0)))
                        if ic == nic // 2:
                            w.append(("wl", c.t[("moe_dw_wt", 1)]))
                        st0 = (ic == 0)
                        sp = (ic == nic - 1)
                        inc = []
                        if sp:
                            s.tick("mm")
                            inc = [("mm", 1)]
                        s.op("T", (lambda ic=ic, fc=fc, tg=tg, half=half, st0=st0,
                                   sp=sp, psd=psd, hsz=hsz: lambda eng: eng.matmul(
                            psd, dwb[:, half * hsz + (ic % (nic // 2)) * 128:
                                     half * hsz + (ic % (nic // 2)) * 128 + 128],
                            h[:, ic * 1024 + (tg % 2) * 512:
                              ic * 1024 + (tg % 2) * 512 + 512],
                            start=st0, stop=sp))(), waits=w, incs=inc)
                    mmd = s.val("mm")
                    c.t[("moe_dw_mm", 0)] = mmd
                    c.t[("moe_dw_mm", 1)] = mmd
                    vt = s.tick("ve", 2)
                    c.t[("moe_dn_ve", dslot)] = vt
                    s.op("V", (lambda fc=fc, tg=tg, psd=psd, tgl=tgl: lambda eng:
                               eng.tensor_mul(c.t["moe_t0"], psd,
                                              cb[:, tgl * 512: tgl * 512 + 512]))(),
                        waits=[("mm", mmd), ("ac", c.t[("moe_cb_ac", tgl)])],
                        incs=[("ve", 1)])
                    s.op("V", (lambda fc=fc, tg=tg: lambda eng: eng.tensor_add(
                        x[:, fc * T + tg * 512: fc * T + tg * 512 + 512],
                        x[:, fc * T + tg * 512: fc * T + tg * 512 + 512],
                        c.t["moe_t0"]))(), incs=[("ve", 1)])
            s.barrier()

def tile_lhsT(wT, ndc, nfc):
    d, f = wT.shape
    assert d == ndc * 128 and f == nfc * 128, (wT.shape, ndc, nfc)
    a = wT.reshape(ndc, 128, nfc, 128).transpose(2, 1, 0, 3)
    return np.ascontiguousarray(a.reshape(nfc, 128, ndc * 128))


def fm_pack(a2d):
    D, T = a2d.shape
    return np.ascontiguousarray(
        a2d.reshape(D // 128, 128, T).transpose(1, 0, 2).reshape(128, -1))


def fm_unpack(a, T):
    ndc = a.shape[1] // T
    return np.ascontiguousarray(
        a.reshape(128, ndc, T).transpose(1, 0, 2).reshape(ndc * 128, T))


def build_KA(dev_outputs=False):
    nc = bass.Bass("TRN2", target_bir_lowering=False, num_devices=8)
    s = Sched()
    c = Ctx(nc, s)
    T = 2048
    xio = nc.dram_tensor("x_c", [128, 8 * T], F32, kind="ExternalInput").ap()
    wcd = nc.dram_tensor("wcd", [4, 128, 8 * 128], F32, kind="ExternalInput").ap()
    wq = nc.dram_tensor("wq", [8, 128, 4 * 128], F32, kind="ExternalInput").ap()
    wkv = nc.dram_tensor("wkv", [1, 128, 4 * 128], F32, kind="ExternalInput").ap()
    wcp = nc.dram_tensor("wcp", [8, 128, 8 * 128], F32, kind="ExternalInput").ap()
    rwd = nc.dram_tensor("rw", [128, 64], F32, kind="ExternalInput").ap()
    identd = nc.dram_tensor("identb", [128, 128], BF16, kind="ExternalInput").ap()
    masksd = nc.dram_tensor("masks", [128, 8 * 1024], BF16, kind="ExternalInput").ap()
    biasmodd = nc.dram_tensor("biasmod", [128, 104], F32, kind="ExternalInput").ap()
    sidx = nc.dram_tensor("sidx", [128, 1], I32, kind="ExternalInput").ap()
    fidx_d = nc.dram_tensor("fidx", [2, 1], I32, kind="ExternalInput").ap()
    flagvals_d = nc.dram_tensor("flagvals", [2, 4], I32, kind="ExternalInput").ap()
    pfidx_d = nc.dram_tensor("pfidx", [2, 1], I32, kind="ExternalInput").ap()
    kv_sh = nc.dram_tensor("kv_sh", [256, T], BF16, kind="Internal",
                           addr_space="Shared").ap()
    shf = nc.dram_tensor("shf", [4, 1], I32, kind="Internal",
                         addr_space="Shared").ap()
    x2_out = nc.dram_tensor("x2_out", [128, 8 * T], F32, kind="ExternalOutput").ap()
    xn2_out = nc.dram_tensor("xn2_out", [128, 8 * T], BF16, kind="ExternalOutput").ap()
    part_out = nc.dram_tensor("part_out", [16, 1], F32, kind="ExternalOutput").ap()
    c.t["shf"] = shf

    with contextlib.ExitStack() as ex:
        def sb(name, shape, dt, st=ex):
            return st.enter_context(nc.sbuf_tensor(name, shape, dt))[:, :]
        c.t["x"] = sb("x_sb", [128, 8 * T], F32)
        c.t["xn"] = sb("xn", [128, 8 * T], BF16)
        c.t["attn"] = c.t["xn"]
        c.t["lncast"] = sb("lncast", [128, 8 * 512], BF16)
        c.t["lnsq"] = sb("lnsq", [128, 8 * 512], BF16)
        c.t["lnrows"] = sb("lnrows", [1, 2048], F32)
        c.t["lnt1"] = sb("lnt1", [128, 512], F32)
        c.t["onesd"] = sb("onesd", [128, 1], BF16)
        c.t["onesk1"] = sb("onesk1", [1, 128], F32)
        c.t["negk1"] = sb("negk1", [1, 128], F32)
        c.t["q"] = sb("q", [128, 8 * T], BF16)
        c.t["kvloc"] = sb("kvloc", [128, T], BF16)
        c.t["wbufA"] = sb("wbufA", [128, 2 * 8 * 128], BF16)
        c.t["flagvals"] = sb("flagvals_sb", [2, 4], I32)
        c.t["flagrd"] = sb("flagrd_sb", [2, 1], I32)
        c.t["fidx"] = sb("fidx_sb", [2, 1], I32)
        c.t["pfidx"] = sb("pfidx_sb", [2, 1], I32)
        c.t["sidx"] = sb("sidx_sb", [128, 1], I32)
        c.t["kdup"] = sb("kdup", [128, 2 * T], BF16)
        c.t["vaug"] = sb("vaug", [128, 32 * 65], BF16)
        c.t["identb"] = sb("identb_sb", [128, 128], BF16)
        c.t["biasmod"] = sb("biasmod_sb", [128, 104], F32)
        c.t["biascomb"] = sb("biascomb", [128, 104], F32)
        c.t["mhneg"] = sb("mhneg", [128, 8], F32)
        c.t["nrows"] = sb("nrows", [1, 32], F32)
        c.t["rw"] = sb("rw_sb", [128, 64], BF16)
        c.t["partials"] = sb("partials", [16, 1], F32)

        s.tick("xa", 16)
        s.op("S", lambda eng: dma(eng, c.t["x"], xio), incs=[("xa", 16)])
        for nm, src in (("fidx", fidx_d), ("pfidx", pfidx_d), ("sidx", sidx),
                        ("identb", identd), ("flagvals", flagvals_d),
                        ("biasmod", biasmodd)):
            s.tick("xa", 16)
            s.op("S", (lambda nm=nm, src=src: lambda eng: dma(eng, c.t[nm], src))(),
                 incs=[("xa", 16)])
        s.tick("wl", 16)
        s.op("P", lambda eng: dma(eng, c.t["rw"], rwd), incs=[("wl", 16)])
        rw_loaded = s.val("wl")
        xall = s.val("xa")
        vt = s.tick("ve")
        s.op("V", lambda eng: eng.memset(c.t["onesd"], 1.0))
        s.op("V", lambda eng: eng.memset(c.t["onesk1"], 1.0))
        s.op("V", lambda eng: eng.memset(c.t["negk1"], -1.0))
        s.op("V", lambda eng: eng.memset(c.t["vaug"], 1.0), incs=[("ve", 1)])
        for e in "TAVP":
            s.op(e, lambda eng: eng.nop(), waits=[("xa", xall), ("ve", vt)])

        with contextlib.ExitStack() as sc1:
            c.t["lat"] = sb("lat", [128, 4 * T], BF16, sc1)
            c.t["vfm"] = sb("vfm", [64, 2 * T], BF16, sc1)
            psA = sc1.enter_context(nc.psum_tensor("psA", [1, 1024], F32))[:, :]
            psB = sc1.enter_context(nc.psum_tensor("psB", [128, 1024], F32))[:, :]
            ps2 = sc1.enter_context(nc.psum_tensor("ps2", [128, 1024], F32))[:, :]
            pst = sc1.enter_context(nc.psum_tensor("pst", [128, 2048], BF16))[:, :]

            emit_ln(c, "x", "xn", T, psA, psB)
            s.barrier()
            emit_projA(c, wcd, "xn", "lat", 8, 4, T, ps2, "wbufA")
            s.barrier()
            emit_projA(c, wq, "lat", "q", 4, 8, T, ps2, "wbufA")
            s.barrier()
            emit_projA(c, wkv, "lat", "kvloc", 4, 1, T, ps2, "wbufA")
            s.barrier()

            def wr(eng):
                return eng.indirect_dma_start(
                    out=kv_sh, out_offset=IndirectOffsetOnAxis(c.t["sidx"], 0),
                    in_=c.t["kvloc"], in_offset=None)
            emit_exchange_write(c, [(wr, [])], epoch=1)
            emit_exchange_poll(c, epoch=1, msem_i=0)
            pall = s.val("po")
            for hh in range(2):
                for half in range(2):
                    s.tick("ex", 16)
                    s.op("P", (lambda hh=hh, half=half: lambda eng: dma(
                        eng, c.t["kdup"][hh * 64:hh * 64 + 64,
                                         half * T:(half + 1) * T],
                        kv_sh[half * 128:half * 128 + 64, :]))(),
                        waits=[("po", pall)], incs=[("ex", 16)])
            for half in range(2):
                s.tick("ex", 16)
                s.op("P", (lambda half=half: lambda eng: dma(
                    eng, c.t["vfm"][0:64, half * T:(half + 1) * T],
                    kv_sh[half * 128 + 64:half * 128 + 128, :]))(),
                    waits=[("po", pall)], incs=[("ex", 16)])
            exv = s.val("ex")
            tr_ev = {}
            for ch in range(32):
                sl = (ch % 2) * 1024
                w = []
                if ch == 0:
                    w.append(("ex", exv))
                if ch >= 2:
                    w.append(("ve", tr_ev[ch - 2]))
                mt = s.tick("mm")
                s.op("T", (lambda ch=ch, sl=sl: lambda eng: eng.transpose(
                    pst[:, sl:sl + 64], c.t["vfm"][0:64, ch * 128:(ch + 1) * 128],
                    c.t["identb"][0:64, 0:64]))(), waits=w, incs=[("mm", 1)])
                ev = s.tick("ve")
                tr_ev[ch] = ev
                s.op("V", (lambda ch=ch, sl=sl: lambda eng: eng.tensor_copy(
                    c.t["vaug"][:, ch * 65: ch * 65 + 64], pst[:, sl:sl + 64]))(),
                    waits=[("mm", mt)], incs=[("ve", 1)])
            s.barrier()

        with contextlib.ExitStack() as sc2:
            c.t["qsq"] = sb("qsq", [128, 2048], BF16, sc2)
            c.t["ksq"] = sb("ksq", [64, 2 * T], BF16, sc2)
            psN = sc2.enter_context(nc.psum_tensor("psN", [1, 1024], F32))[:, :]
            psM = sc2.enter_context(nc.psum_tensor("psM", [128, 512], F32))[:, :]
            emit_l0_norm_mhat(c, psN, psM, kdup_ready=[("ex", exv)])
            s.barrier()

        with contextlib.ExitStack() as sc3:
            c.t["masks"] = sb("masks_sb", [128, 8 * 1024], BF16, sc3)
            c.t["p_sb"] = sb("p_sb", [128, 2048], BF16, sc3)
            c.t["linv"] = sb("linv", [1, 1024], F32, sc3)
            c.t["osb"] = sb("osb", [128, 1024], BF16, sc3)
            psS = sc3.enter_context(nc.psum_tensor("psS", [128, 2048], F32))[:, :]
            psO = sc3.enter_context(nc.psum_tensor("psO", [128, 1024], F32))[:, :]
            psC = sc3.enter_context(nc.psum_tensor("psC", [128, 1024], F32))[:, :]
            mk = s.tick("xa", 16)
            s.op("S", lambda eng: dma(eng, c.t["masks"], masksd), incs=[("xa", 16)])
            for e in "TAV":
                s.op(e, lambda eng: eng.nop(), waits=[("xa", mk)])
            emit_l0_attn(c, psS, psO, psC)
            s.barrier()

        with contextlib.ExitStack() as sc4:
            ps2 = sc4.enter_context(nc.psum_tensor("ps2b", [128, 1024], F32))[:, :]
            psA = sc4.enter_context(nc.psum_tensor("psA2", [1, 1024], F32))[:, :]
            psB = sc4.enter_context(nc.psum_tensor("psB2", [128, 1024], F32))[:, :]
            emit_residual_proj(c, wcp, "attn", "x", 8, 8, T, ps2, "wbufA")
            s.barrier()
            emit_ln(c, "x", "xn", T, psA, psB)
            s.barrier()

        with contextlib.ExitStack() as sc5:
            c.t["rlt"] = sb("rlt", [128, 8], F32, sc5)
            c.t["rrows"] = sb("rrows", [128, 8], F32, sc5)
            c.t["reqm"] = sb("reqm", [128, 8], F32, sc5)
            c.t["rmk"] = sb("rmk", [128, 8], F32, sc5)
            c.t["rmsk"] = sb("rmsk", [128, 8], F32, sc5)
            c.t["re_"] = sb("re_", [128, 8], F32, sc5)
            c.t["rmwb"] = sb("rmwb", [128, 16], BF16, sc5)
            psR = sc5.enter_context(nc.psum_tensor("psR", [128, 1024], F32))[:, :]
            psSt = sc5.enter_context(nc.psum_tensor("psSt", [16, 1], F32))[:, :]
            s.op("T", lambda eng: eng.nop(), waits=[("wl", rw_loaded)])
            emit_router_l0(c, "rw", "xn", psR, psSt)
            s.barrier()

        for nm, dst in (("x", x2_out), ("xn", xn2_out), ("partials", part_out)):
            s.tick("st", 16)
            s.op("S", (lambda nm=nm, dst=dst: lambda eng: dma(eng, dst, c.t[nm]))(),
                 incs=[("st", 16)])
        stv = s.val("st")
        s.op("S", lambda eng: eng.nop(), waits=[("st", stv)])

        sems = {nm: ex.enter_context(nc.semaphore(nm)) for nm in SEM_NAMES}
        s.replay(nc, sems)
    return nc


def make_flagvals():
    nonce = np.random.randint(1, 2**30, dtype=np.int32)
    return np.broadcast_to(nonce + np.arange(1, 5, dtype=np.int32), (2, 4)).copy()


def build_l0_masks(h):
    p = np.arange(128).reshape(128, 1)
    f = np.arange(512).reshape(1, 512)
    masks = np.zeros((128, 8, 1024), dtype=BF16NP)
    for r in range(4):
        stair = (r * 128 + p <= f).astype(BF16NP)
        stair2 = np.concatenate([stair, stair], axis=1)
        ones = np.ones((128, 1024), dtype=BF16NP)
        masks[:, r, :] = stair2 if h == 0 else ones
        masks[:, 4 + r, :] = stair2 if h == 1 else ones
    biasmod = np.zeros((128, 104), dtype=np.float32)
    for j in range(4):
        g = 4 * h + j
        for sc in range(4 * j + 20):
            col = L0_OFF[j] + sc
            if sc > 4 * g + 3:
                biasmod[:, col] = -1e9
    return np.ascontiguousarray(masks.reshape(128, 8192)), biasmod


def prep_ka_inputs(x, Wc_down, Wq_up, Wkv_up, Wc_proj, router_w):
    x = np.asarray(x, np.float32)
    wcd = tile_lhsT(np.ascontiguousarray(np.asarray(Wc_down, np.float32)), 8, 4)
    wq = tile_lhsT(np.ascontiguousarray(np.asarray(Wq_up, np.float32)), 4, 8)
    wkv = tile_lhsT(np.ascontiguousarray(np.asarray(Wkv_up, np.float32)), 4, 1)
    wcp = tile_lhsT(np.ascontiguousarray(np.asarray(Wc_proj, np.float32)), 8, 8)
    rw = np.ascontiguousarray(
        np.asarray(router_w, np.float32).T.reshape(8, 128, 8)
        .transpose(1, 0, 2).reshape(128, 64))
    ident = np.eye(128, dtype=BF16NP)
    ar = np.arange(128, dtype=np.int32).reshape(128, 1)
    fv = make_flagvals()
    maps = []
    for cid in range(8):
        b, h = divmod(cid, 2)
        xc = fm_pack(np.ascontiguousarray(x[b].T[:, h * 2048:(h + 1) * 2048]))
        masks, biasmod = build_l0_masks(h)
        maps.append({
            "x_c": xc, "wcd": wcd, "wq": wq, "wkv": wkv, "wcp": wcp, "rw": rw,
            "identb": ident, "flagvals": fv, "masks": masks, "biasmod": biasmod,
            "sidx": (h * 128 + ar).astype(np.int32),
            "fidx": np.array([[2 * h], [2 * h + 1]], np.int32),
            "pfidx": np.array([[2 * (1 - h)], [2 * (1 - h) + 1]], np.int32),
        })
    return maps



def build_KB1():
    nc = bass.Bass("TRN2", target_bir_lowering=False, num_devices=8)
    s = Sched()
    c = Ctx(nc, s)
    T = 2048
    x2d = nc.dram_tensor("x2_c", [128, 8 * T], F32, kind="ExternalInput").ap()
    xn2d = nc.dram_tensor("xn2_c", [128, 8 * T], BF16, kind="ExternalInput").ap()
    ptd = nc.dram_tensor("partials_T", [8, 16], F32, kind="ExternalInput").ap()
    rwd = nc.dram_tensor("rw", [128, 64], F32, kind="ExternalInput").ap()
    gwd = nc.dram_tensor("gw_t", [8, 32, 128, 1024], F32, kind="ExternalInput").ap()
    uwd = nc.dram_tensor("uw_t", [8, 32, 128, 1024], F32, kind="ExternalInput").ap()
    dwd = nc.dram_tensor("dw_t", [8, 8, 128, 4096], F32, kind="ExternalInput").ap()
    identd = nc.dram_tensor("identb", [128, 128], BF16, kind="ExternalInput").ap()
    sel8d = nc.dram_tensor("sel8", [8, 1024], BF16, kind="ExternalInput").ap()
    x3_out = nc.dram_tensor("x3_out", [128, 8 * T], F32, kind="ExternalOutput").ap()
    xn3_out = nc.dram_tensor("xn3_out", [128, 8 * T], BF16, kind="ExternalOutput").ap()

    with contextlib.ExitStack() as ex:
        def sb(name, shape, dt, st=ex):
            return st.enter_context(nc.sbuf_tensor(name, shape, dt))[:, :]
        c.t["x"] = sb("x_sb", [128, 8 * T], F32)
        c.t["xn"] = sb("xn", [128, 8 * T], BF16)
        c.t["onesd"] = sb("onesd", [128, 1], BF16)
        c.t["onesk1"] = sb("onesk1", [1, 128], F32)
        c.t["identb"] = sb("identb_sb", [128, 128], BF16)
        c.t["sel8"] = sb("sel8_sb", [8, 1024], BF16)
        c.t["rw"] = sb("rw_sb", [128, 64], BF16)
        c.t["coefL0"] = sb("coefL0", [8, 2048], BF16)
        c.t["partials_T"] = sb("partials_T_sb", [8, 16], F32)
        c.t["mw"] = sb("mw", [8, 1], F32)
        c.t["strows"] = sb("strows", [8, 4], F32)

        s.tick("xa", 16)
        s.op("S", lambda eng: dma(eng, c.t["x"], x2d), incs=[("xa", 16)])
        for nm, src in (("xn", xn2d), ("identb", identd), ("sel8", sel8d),
                        ("partials_T", ptd)):
            s.tick("xa", 16)
            s.op("S", (lambda nm=nm, src=src: lambda eng: dma(eng, c.t[nm], src))(),
                 incs=[("xa", 16)])
        s.tick("wl", 16)
        s.op("P", lambda eng: dma(eng, c.t["rw"], rwd), incs=[("wl", 16)])
        rwl = s.val("wl")
        xall = s.val("xa")
        vt = s.tick("ve")
        s.op("V", lambda eng: eng.memset(c.t["onesd"], 1.0))
        s.op("V", lambda eng: eng.memset(c.t["onesk1"], 1.0), incs=[("ve", 1)])
        for e in "TAVP":
            s.op(e, lambda eng: eng.nop(), waits=[("xa", xall), ("ve", vt),
                                                  ("wl", rwl)])

        with contextlib.ExitStack() as sc1:
            c.t["rlt"] = sb("rlt", [128, 8], F32, sc1)
            c.t["rrows"] = sb("rrows", [128, 8], F32, sc1)
            c.t["reqm"] = sb("reqm", [128, 8], F32, sc1)
            c.t["rmk"] = sb("rmk", [128, 8], F32, sc1)
            c.t["rmsk"] = sb("rmsk", [128, 8], F32, sc1)
            c.t["re_"] = sb("re_", [128, 8], F32, sc1)
            c.t["rcf16"] = sb("rcf16", [128, 8], BF16, sc1)
            psR = sc1.enter_context(nc.psum_tensor("psR", [128, 1024], F32))[:, :]
            c.t["psTb_ap"] = sc1.enter_context(
                nc.psum_tensor("psTb", [8, 128], BF16))[:, :]
            emit_stats_meanw(c)
            emit_router_masks_coef(c, "rw", "xn", psR, None, "mw", "coefL0",
                                   per_token_w=False)
            s.barrier()

        with contextlib.ExitStack() as sc2:
            c.t["h_sb"] = sb("h_sb", [128, 32 * 1024], BF16, sc2)
            c.t["sg_sb"] = sb("sg_sb", [128, 2048], BF16, sc2)
            c.t["gub"] = sb("gub", [128, 4096], BF16, sc2)
            c.t["dwb"] = sb("dwb", [128, 4096], BF16, sc2)
            c.t["cb_sb"] = sb("cb_sb", [128, 1024], F32, sc2)
            c.t["moe_t0"] = sb("moe_t0", [128, 512], F32, sc2)
            psGU0 = sc2.enter_context(nc.psum_tensor("psGU0", [128, 2048], F32))[:, :]
            psGU1 = sc2.enter_context(nc.psum_tensor("psGU1", [128, 2048], F32))[:, :]
            emit_moe(c, gwd, uwd, dwd, "xn", "x", "coefL0", 8, 4096, T,
                     psGU0, psGU1)
            s.barrier()

        with contextlib.ExitStack() as sc3:
            c.t["lncast"] = sb("lncast", [128, 8 * 512], BF16, sc3)
            c.t["lnsq"] = sb("lnsq", [128, 8 * 512], BF16, sc3)
            c.t["lnrows"] = sb("lnrows", [1, 2048], F32, sc3)
            c.t["lnt1"] = sb("lnt1", [128, 512], F32, sc3)
            psA = sc3.enter_context(nc.psum_tensor("psA", [1, 1024], F32))[:, :]
            psB = sc3.enter_context(nc.psum_tensor("psB", [128, 1024], F32))[:, :]
            emit_ln(c, "x", "xn", T, psA, psB)
            s.barrier()

        for nm, dst in (("x", x3_out), ("xn", xn3_out)):
            s.tick("st", 16)
            s.op("S", (lambda nm=nm, dst=dst: lambda eng: dma(eng, dst, c.t[nm]))(),
                 incs=[("st", 16)])
        stv = s.val("st")
        s.op("S", lambda eng: eng.nop(), waits=[("st", stv)])

        sems = {nm: ex.enter_context(nc.semaphore(nm)) for nm in SEM_NAMES}
        s.replay(nc, sems)
    return nc


def prep_kb1_weights(gw, uw, dw, router_w):
    gw = np.asarray(gw, np.float32)
    uw = np.asarray(uw, np.float32)
    dw = np.asarray(dw, np.float32)
    gw_t = np.stack([tile_lhsT(np.ascontiguousarray(gw[e].T), 8, 32) for e in range(8)])
    uw_t = np.stack([tile_lhsT(np.ascontiguousarray(uw[e].T), 8, 32) for e in range(8)])
    dw_t = np.stack([tile_lhsT(np.ascontiguousarray(dw[e].T), 32, 8) for e in range(8)])
    rw = np.ascontiguousarray(
        np.asarray(router_w, np.float32).T.reshape(8, 128, 8)
        .transpose(1, 0, 2).reshape(128, 64))
    sel8 = np.zeros((8, 1024), dtype=BF16NP)
    for e in range(8):
        sel8[e, e * 128:(e + 1) * 128] = 1.0
    return gw_t, uw_t, dw_t, rw, sel8



def build_KB2(dev=False):
    nc = bass.Bass("TRN2", target_bir_lowering=False, num_devices=8)
    s = Sched()
    c = Ctx(nc, s)
    T = 2048
    x3d = nc.dram_tensor("x3_c", [128, 8 * T], F32, kind="ExternalInput").ap()
    xn3d = nc.dram_tensor("xn3_c", [128, 8 * T], BF16, kind="ExternalInput").ap()
    alphad = nc.dram_tensor("alpha", [1, 1], F32, kind="ExternalInput").ap()
    wqt = nc.dram_tensor("wqt", [8, 128, 1024], F32, kind="ExternalInput").ap()
    wkt = nc.dram_tensor("wkt", [8, 128, 1024], F32, kind="ExternalInput").ap()
    wkpt = nc.dram_tensor("wkpt", [8, 128, 1024], F32, kind="ExternalInput").ap()
    wot = nc.dram_tensor("wot", [8, 128, 1024], F32, kind="ExternalInput").ap()
    wvfm = nc.dram_tensor("wvfm", [128, 8192], F32, kind="ExternalInput").ap()
    rded = nc.dram_tensor("rde", [128, 64], F32, kind="ExternalInput").ap()
    gatd = nc.dram_tensor("gat_t", [8, 16, 128, 1024], F32, kind="ExternalInput").ap()
    uptd = nc.dram_tensor("upt_t", [8, 16, 128, 1024], F32, kind="ExternalInput").ap()
    dnd = nc.dram_tensor("dn_t", [8, 8, 128, 2048], F32, kind="ExternalInput").ap()
    identd = nc.dram_tensor("identb", [128, 128], BF16, kind="ExternalInput").ap()
    sel8d = nc.dram_tensor("sel8", [8, 1024], BF16, kind="ExternalInput").ap()
    kidxd = nc.dram_tensor("kidx", [128, 8], I32, kind="ExternalInput").ap()
    vidxd = nc.dram_tensor("vidx", [128, 16], I32, kind="ExternalInput").ap()
    fidx_d = nc.dram_tensor("fidx", [2, 1], I32, kind="ExternalInput").ap()
    pfidx_d = nc.dram_tensor("pfidx", [2, 1], I32, kind="ExternalInput").ap()
    flagvals_d = nc.dram_tensor("flagvals", [2, 4], I32, kind="ExternalInput").ap()
    ksh = nc.dram_tensor("ksh", [2048, T], BF16, kind="Internal",
                         addr_space="Shared").ap()
    kpsh = nc.dram_tensor("kpsh", [2048, T], BF16, kind="Internal",
                          addr_space="Shared").ap()
    vsh = nc.dram_tensor("vsh", [4096, 1024], BF16, kind="Internal",
                         addr_space="Shared").ap()
    shf = nc.dram_tensor("shf", [4, 1], I32, kind="Internal",
                         addr_space="Shared").ap()
    out_c = nc.dram_tensor("out_c", [128, 8 * T], F32, kind="ExternalOutput").ap()
    if dev:
        attn_out = nc.dram_tensor("attn_out", [128, 8 * T], BF16,
                                  kind="ExternalOutput").ap()
        x4_out = nc.dram_tensor("x4_out", [128, 8 * T], F32,
                                kind="ExternalOutput").ap()
        xn4_out = nc.dram_tensor("xn4_out", [128, 8 * T], BF16,
                                 kind="ExternalOutput").ap()
        coef_out = nc.dram_tensor("coef_out", [8, 2048], BF16,
                                  kind="ExternalOutput").ap()
    c.t["shf"] = shf

    with contextlib.ExitStack() as ex:
        def sb(name, shape, dt, st=ex):
            return st.enter_context(nc.sbuf_tensor(name, shape, dt))[:, :]
        c.t["x"] = sb("x_sb", [128, 8 * T], F32)
        c.t["xn"] = sb("xn", [128, 8 * T], BF16)
        c.t["attn"] = c.t["xn"]
        c.t["q"] = sb("q", [128, 8 * T], BF16)
        c.t["onesd"] = sb("onesd", [128, 1], BF16)
        c.t["onesk1"] = sb("onesk1", [1, 128], F32)
        c.t["negk1"] = sb("negk1", [1, 128], F32)
        c.t["identb"] = sb("identb_sb", [128, 128], BF16)
        c.t["sel8"] = sb("sel8_sb", [8, 1024], BF16)
        c.t["rde"] = sb("rde_sb", [128, 64], BF16)
        c.t["mhneg"] = sb("mhneg", [128, 8], F32)
        c.t["nrows"] = sb("nrows", [1, 32], F32)
        c.t["alpha_sc"] = sb("alpha_sc", [1, 2], F32)
        c.t["alpha_in"] = sb("alpha_in", [1, 1], F32)
        c.t["coefL1"] = sb("coefL1", [8, 2048], BF16)
        c.t["flagvals"] = sb("flagvals_sb", [2, 4], I32)
        c.t["flagrd"] = sb("flagrd_sb", [2, 1], I32)
        c.t["fidx"] = sb("fidx_sb", [2, 1], I32)
        c.t["pfidx"] = sb("pfidx_sb", [2, 1], I32)
        c.t["kidx"] = sb("kidx_sb", [128, 8], I32)
        c.t["vidx"] = sb("vidx_sb", [128, 16], I32)
        c.t["wbufA"] = sb("wbufA", [128, 2 * 8 * 128], BF16)

        s.tick("xa", 16)
        s.op("S", lambda eng: dma(eng, c.t["x"], x3d), incs=[("xa", 16)])
        for nm, src in (("xn", xn3d), ("identb", identd), ("sel8", sel8d),
                        ("kidx", kidxd), ("vidx", vidxd), ("fidx", fidx_d),
                        ("pfidx", pfidx_d), ("flagvals", flagvals_d),
                        ("alpha_in", alphad)):
            s.tick("xa", 16)
            s.op("S", (lambda nm=nm, src=src: lambda eng: dma(eng, c.t[nm], src))(),
                 incs=[("xa", 16)])
        s.tick("wl", 16)
        s.op("P", lambda eng: dma(eng, c.t["rde"], rded), incs=[("wl", 16)])
        rwl = s.val("wl")
        xall = s.val("xa")
        vt = s.tick("ve")
        s.op("V", lambda eng: eng.memset(c.t["onesd"], 1.0))
        s.op("V", lambda eng: eng.memset(c.t["onesk1"], 1.0))
        s.op("V", lambda eng: eng.memset(c.t["negk1"], -1.0), incs=[("ve", 1)])
        for e in "TAVP":
            s.op(e, lambda eng: eng.nop(), waits=[("xa", xall), ("ve", vt),
                                                  ("wl", rwl)])
        s.ser("V", "ve", lambda eng: eng.tensor_copy(c.t["alpha_sc"][0:1, 0:1],
                                                     c.t["alpha_in"]))
        s.ser("V", "ve", lambda eng: eng.tensor_scalar(
            c.t["alpha_sc"][0:1, 1:2], c.t["alpha_in"], -1.0, 1.0,
            op0=ALU.mult, op1=ALU.add))

        with contextlib.ExitStack() as scq:
            ps2 = scq.enter_context(nc.psum_tensor("ps2", [128, 1024], F32))[:, :]
            emit_projA(c, wqt, "xn", "q", 8, 8, T, ps2, "wbufA")
            s.barrier()
            for wnm, wdram, shdst, ep in (("k", wkt, ksh, None),
                                          ("kp", wkpt, kpsh, None)):
                with contextlib.ExitStack() as sck:
                    c.t["kloc"] = sb("kloc_" + wnm, [128, 8 * T], BF16, sck)
                    emit_projA(c, wdram, "xn", "kloc", 8, 8, T, ps2, "wbufA")
                    s.barrier()
                    def wrk(eng, shdst=shdst):
                        return eng.indirect_dma_start(
                            out=shdst,
                            out_offset=IndirectOffsetOnAxis(c.t["kidx"][:, 0:1], 0),
                            in_=c.t["kloc"][:, 0:T], in_offset=None)
                    pls = []
                    for fc in range(8):
                        pls.append(((lambda fc=fc, shdst=shdst: lambda eng:
                                     eng.indirect_dma_start(
                            out=shdst,
                            out_offset=IndirectOffsetOnAxis(
                                c.t["kidx"][:, fc:fc + 1], 0),
                            in_=c.t["kloc"][:, fc * T:(fc + 1) * T],
                            in_offset=None))(), []))
                    for fn, w in pls:
                        s.tick("ex", 16)
                        s.op("P", fn, waits=w, incs=[("ex", 16)])
                    exv = s.val("ex")
                    for e in "TAVPS":
                        s.op(e, lambda eng: eng.nop(), waits=[("ex", exv)])
                    s.barrier()
            with contextlib.ExitStack() as scv:
                c.t["vloc"] = sb("vloc", [128, 16 * 1024], BF16, scv)
                c.t["wv_sb"] = sb("wv_sb", [128, 8192], BF16, scv)
                psV = scv.enter_context(nc.psum_tensor("psV", [128, 2048],
                                                       F32))[:, :]
                wt = s.tick("wl", 16)
                s.op("P", lambda eng: dma(eng, c.t["wv_sb"], wvfm),
                     incs=[("wl", 16)])
                for e in "T":
                    s.op(e, lambda eng: eng.nop(), waits=[("wl", wt)])
                emit_l1_vproj(c, "wv_sb", "xn", "vloc", psV)
                vall = s.val("ve")
                for e in "TAVPS":
                    s.op(e, lambda eng: eng.nop(), waits=[("ve", vall)])
                for tt in range(16):
                    s.tick("ex", 16)
                    s.op("P", (lambda tt=tt: lambda eng: eng.indirect_dma_start(
                        out=vsh,
                        out_offset=IndirectOffsetOnAxis(c.t["vidx"][:, tt:tt + 1], 0),
                        in_=c.t["vloc"][:, tt * 1024:(tt + 1) * 1024],
                        in_offset=None))(), incs=[("ex", 16)])
                exv = s.val("ex")
                for e in "TAVPS":
                    s.op(e, lambda eng: eng.nop(), waits=[("ex", exv)])
                s.barrier()
        emit_exchange_write(c, [], epoch=2)
        emit_exchange_poll(c, epoch=2, msem_i=0)
        pall = s.val("po")

        with contextlib.ExitStack() as sca:
            c.t["khp"] = sb("khp", [128, 4096], BF16, sca)
            c.t["kphp"] = sb("kphp", [128, 4096], BF16, sca)
            c.t["vaughp"] = sb("vaughp", [128, 4160], BF16, sca)
            c.t["qsq"] = sb("qsq", [128, 2048], BF16, sca)
            c.t["ksq"] = sb("ksq", [128, 4096], BF16, sca)
            c.t["p_sb"] = sb("p_sb", [128, 2048], BF16, sca)
            c.t["ostd"] = sb("ostd", [128, 1024], BF16, sca)
            c.t["osp"] = sb("osp", [128, 1024], BF16, sca)
            c.t["linv2"] = sb("linv2", [1, 2048], F32, sca)
            c.t["l1t0"] = sb("l1t0", [128, 512], F32, sca)
            c.t["l1t1"] = sb("l1t1", [128, 512], F32, sca)
            psS = sca.enter_context(nc.psum_tensor("psS", [128, 2048], F32))[:, :]
            psO = sca.enter_context(nc.psum_tensor("psO", [128, 1024], F32))[:, :]
            for hp in range(8):
                kld = s.val("ex")
                for half in range(2):
                    for src, dstn in ((ksh, "khp"), (kpsh, "kphp")):
                        s.tick("ex", 16)
                        s.op("P", (lambda half=half, src=src, dstn=dstn, hp=hp:
                                   lambda eng: dma(
                            eng, c.t[dstn][:, half * T:(half + 1) * T],
                            src[half * 1024 + hp * 128: half * 1024 + hp * 128 + 128,
                                :]))(), waits=[("po", pall)], incs=[("ex", 16)])
                vmt = s.tick("ve")
                s.op("V", lambda eng: eng.memset(c.t["vaughp"], 1.0),
                     incs=[("ve", 1)])
                for ch in range(32):
                    s.tick("ex", 16)
                    s.op("P", (lambda ch=ch, hp=hp: lambda eng: dma(
                        eng,
                        c.t["vaughp"].rearrange("p (h x) -> p h x", h=2)
                        [:, :, ch * 65: ch * 65 + 64],
                        vsh[ch * 128:(ch + 1) * 128,
                            hp * 128:(hp + 1) * 128].rearrange(
                            "s (h w) -> s h w", h=2)))(),
                        waits=[("po", pall), ("ve", vmt)], incs=[("ex", 16)])
                exv = s.val("ex")
                s.barrier()
                for e in "TAV":
                    s.op(e, lambda eng: eng.nop(), waits=[("ex", exv)])
                emit_l1_norm_mhat(c, psS, psO, hp, k_ready=[("ex", exv)])
                s.barrier()
                emit_l1_attn_hp(c, psS, psO, hp)
                s.barrier()

        if dev:
            s.tick("st", 16)
            s.op("S", lambda eng: dma(eng, attn_out, c.t["attn"]),
                 incs=[("st", 16)])
            s.op("S", lambda eng: eng.nop(), waits=[("st", s.val("st"))])
            s.barrier()
        with contextlib.ExitStack() as sco:
            ps2 = sco.enter_context(nc.psum_tensor("ps2o", [128, 1024], F32))[:, :]
            emit_residual_proj(c, wot, "attn", "x", 8, 8, T, ps2, "wbufA")
            s.barrier()

        if dev:
            s.tick("st", 16)
            s.op("S", lambda eng: dma(eng, x4_out, c.t["x"]), incs=[("st", 16)])
            s.op("S", lambda eng: eng.nop(), waits=[("st", s.val("st"))])
            s.barrier()
        with contextlib.ExitStack() as scl:
            c.t["lncast"] = sb("lncast", [128, 8 * 512], BF16, scl)
            c.t["lnsq"] = sb("lnsq", [128, 8 * 512], BF16, scl)
            c.t["lnrows"] = sb("lnrows", [1, 2048], F32, scl)
            c.t["lnt1"] = sb("lnt1", [128, 512], F32, scl)
            psA = scl.enter_context(nc.psum_tensor("psA", [1, 1024], F32))[:, :]
            psB = scl.enter_context(nc.psum_tensor("psB", [128, 1024], F32))[:, :]
            emit_ln(c, "x", "xn", T, psA, psB)
            s.barrier()

        if dev:
            s.tick("st", 16)
            s.op("S", lambda eng: dma(eng, xn4_out, c.t["xn"]), incs=[("st", 16)])
            s.op("S", lambda eng: eng.nop(), waits=[("st", s.val("st"))])
            s.barrier()
        with contextlib.ExitStack() as scr:
            c.t["rlt"] = sb("rlt", [128, 8], F32, scr)
            c.t["rrows"] = sb("rrows", [128, 8], F32, scr)
            c.t["reqm"] = sb("reqm", [128, 8], F32, scr)
            c.t["rmk"] = sb("rmk", [128, 8], F32, scr)
            c.t["rmsk"] = sb("rmsk", [128, 8], F32, scr)
            c.t["re_"] = sb("re_", [128, 8], F32, scr)
            c.t["rcf16"] = sb("rcf16", [128, 8], BF16, scr)
            psR = scr.enter_context(nc.psum_tensor("psR", [128, 1024], F32))[:, :]
            c.t["psTb_ap"] = scr.enter_context(
                nc.psum_tensor("psTb", [8, 128], BF16))[:, :]
            emit_router_masks_coef(c, "rde", "xn", psR, None, None, "coefL1",
                                   per_token_w=True)
            s.barrier()
            if dev:
                s.tick("st", 16)
                s.op("S", lambda eng: dma(eng, coef_out, c.t["coefL1"]),
                     incs=[("st", 16)])
                s.op("S", lambda eng: eng.nop(), waits=[("st", s.val("st"))])
                s.barrier()
        with contextlib.ExitStack() as scm:
            c.t["h_sb"] = sb("h_sb", [128, 16 * 1024], BF16, scm)
            c.t["sg_sb"] = sb("sg_sb", [128, 2048], BF16, scm)
            c.t["gub"] = sb("gub", [128, 4096], BF16, scm)
            c.t["dwb"] = sb("dwb", [128, 2048], BF16, scm)
            c.t["cb_sb"] = sb("cb_sb", [128, 1024], F32, scm)
            c.t["moe_t0"] = sb("moe_t0", [128, 512], F32, scm)
            psGU0 = scm.enter_context(nc.psum_tensor("psGU0", [128, 2048], F32))[:, :]
            psGU1 = scm.enter_context(nc.psum_tensor("psGU1", [128, 2048], F32))[:, :]
            emit_moe(c, gatd, uptd, dnd, "xn", "x", "coefL1", 8, 2048, T,
                     psGU0, psGU1)
            s.barrier()

        s.tick("st", 16)
        s.op("S", lambda eng: dma(eng, out_c, c.t["x"]), incs=[("st", 16)])
        stv = s.val("st")
        s.op("S", lambda eng: eng.nop(), waits=[("st", stv)])

        sems = {nm: ex.enter_context(nc.semaphore(nm)) for nm in SEM_NAMES}
        s.replay(nc, sems)
    return nc


def prep_kb2_weights(W_Q, W_K, W_V, W_Kp, W_O, rde, gat, upt, dn):
    wqt = tile_lhsT(np.ascontiguousarray(np.asarray(W_Q, np.float32).T), 8, 8)
    wkt = tile_lhsT(np.ascontiguousarray(np.asarray(W_K, np.float32).T), 8, 8)
    wkpt = tile_lhsT(np.ascontiguousarray(np.asarray(W_Kp, np.float32).T), 8, 8)
    wot = tile_lhsT(np.ascontiguousarray(np.asarray(W_O, np.float32).T), 8, 8)
    wvfm = np.ascontiguousarray(
        np.asarray(W_V, np.float32).T.reshape(8, 128, 1024)
        .transpose(1, 0, 2).reshape(128, 8192))
    rdep = np.ascontiguousarray(
        np.asarray(rde, np.float32).reshape(8, 128, 8)
        .transpose(1, 0, 2).reshape(128, 64))
    gat = np.asarray(gat, np.float32)
    upt = np.asarray(upt, np.float32)
    dn = np.asarray(dn, np.float32)
    gat_t = np.stack([tile_lhsT(np.ascontiguousarray(gat[e]), 8, 16)
                      for e in range(8)])
    upt_t = np.stack([tile_lhsT(np.ascontiguousarray(upt[e]), 8, 16)
                      for e in range(8)])
    dn_t = np.stack([tile_lhsT(np.ascontiguousarray(dn[e]), 16, 8)
                     for e in range(8)])
    return wqt, wkt, wkpt, wot, wvfm, rdep, gat_t, upt_t, dn_t


def kernel(**inputs):
    x = np.asarray(inputs["L0_attn_x"], np.float32)
    ka_maps = prep_ka_inputs(
        x, inputs["L0_attn_Wc_down"], inputs["L0_attn_Wq_up"],
        inputs["L0_attn_Wkv_up"], inputs["L0_attn_Wc_proj"],
        inputs["L0_ffn_router_w"])
    nca = build_KA()
    ra = run_bass_kernel_spmd(nca, ka_maps, core_ids=list(range(8)))

    parts = [np.asarray(ra.results[cid]["part_out"], np.float32).ravel()
             for cid in range(8)]
    pt = np.zeros((8, 16), np.float32)
    for e in range(8):
        for cc in range(8):
            pt[e, cc] = parts[cc][e]
            pt[e, 8 + cc] = parts[cc][8 + e]
    gw_t, uw_t, dw_t, rwt, sel8 = prep_kb1_weights(
        inputs["L0_ffn_gate_w"], inputs["L0_ffn_up_w"], inputs["L0_ffn_down_w"],
        inputs["L0_ffn_router_w"])
    ident = np.eye(128, dtype=BF16NP)
    ncb1 = build_KB1()
    kb1_maps = []
    for cid in range(8):
        kb1_maps.append({
            "x2_c": np.asarray(ra.results[cid]["x2_out"]),
            "xn2_c": np.asarray(ra.results[cid]["xn2_out"]),
            "partials_T": pt, "rw": rwt, "gw_t": gw_t, "uw_t": uw_t,
            "dw_t": dw_t, "identb": ident, "sel8": sel8,
        })
    rb1 = run_bass_kernel_spmd(ncb1, kb1_maps, core_ids=list(range(8)))

    wqt, wkt, wkpt, wot, wvfm, rdep, gat_t, upt_t, dn_t = prep_kb2_weights(
        inputs["L1_attn_W_Q"], inputs["L1_attn_W_K"], inputs["L1_attn_W_V"],
        inputs["L1_attn_W_K_prime"], inputs["L1_attn_W_O"],
        inputs["L1_ffn_router_kernel_DE"], inputs["L1_ffn_kernel_gating_EDF"],
        inputs["L1_ffn_kernel_up_EDF"], inputs["L1_ffn_kernel_down_EFD"])
    alpha = np.asarray(inputs["L1_attn_alpha"], np.float32).reshape(1, 1)
    fv = make_flagvals()
    ar = np.arange(128, dtype=np.int32)
    ncb2 = build_KB2()
    kb2_maps = []
    for cid in range(8):
        b, h = divmod(cid, 2)
        kidx = np.stack([h * 1024 + fc * 128 + ar for fc in range(8)], 1)
        vidx = np.stack([h * 2048 + tt * 128 + ar for tt in range(16)], 1)
        kb2_maps.append({
            "x3_c": np.asarray(rb1.results[cid]["x3_out"]),
            "xn3_c": np.asarray(rb1.results[cid]["xn3_out"]),
            "alpha": alpha,
            "wqt": wqt, "wkt": wkt, "wkpt": wkpt, "wot": wot, "wvfm": wvfm,
            "rde": rdep, "gat_t": gat_t, "upt_t": upt_t, "dn_t": dn_t,
            "identb": ident, "sel8": sel8,
            "kidx": kidx.astype(np.int32), "vidx": vidx.astype(np.int32),
            "fidx": np.array([[2 * h], [2 * h + 1]], np.int32),
            "pfidx": np.array([[2 * (1 - h)], [2 * (1 - h) + 1]], np.int32),
            "flagvals": fv,
        })
    rb2 = run_bass_kernel_spmd(ncb2, kb2_maps, core_ids=list(range(8)))

    out = np.empty((4, 4096, 1024), np.float32)
    for cid in range(8):
        b, h = divmod(cid, 2)
        shard = fm_unpack(np.asarray(rb2.results[cid]["out_c"], np.float32), 2048)
        out[b, h * 2048:(h + 1) * 2048, :] = shard.T
    return out


# version 03 (final)
# speedup vs baseline: 1.1401x; 1.1401x over previous
import math
import contextlib
import numpy as np
import ml_dtypes
import concourse.bass as bass
import concourse.mybir as mybir
from concourse.bass import IndirectOffsetOnAxis
from concourse.bass_utils import run_bass_kernel_spmd

F32 = mybir.dt.float32
BF16 = mybir.dt.bfloat16
I32 = mybir.dt.int32
AF = mybir.ActivationFunctionType
ALU = mybir.AluOpType
AX = mybir.AxisListType

BF16NP = ml_dtypes.bfloat16

ENGINES = "TAVPS"


class Sched:

    def __init__(self):
        self.ops = {e: [] for e in ENGINES}
        self.cnt = {}

    def tick(self, sem, n=1):
        self.cnt[sem] = self.cnt.get(sem, 0) + n
        return self.cnt[sem]

    def val(self, sem):
        return self.cnt.get(sem, 0)

    def op(self, eng, fn, waits=(), incs=()):
        self.ops[eng].append((fn, tuple(waits), tuple(incs)))

    def ser(self, eng, sem, fn, waits=(), incs=()):
        t = self.tick(sem)
        self.op(eng, fn, waits=waits, incs=[(sem, 1)] + list(incs))
        self.op(eng, lambda eng2: eng2.nop(), waits=[(sem, t)])

    def barrier(self):
        for e in ENGINES:
            self.tick("ph", 1)
            self.op(e, lambda eng: eng.nop(), incs=[("ph", 1)])
        tgt = self.val("ph")
        for e in ENGINES:
            self.op(e, lambda eng: eng.nop(), waits=[("ph", tgt)])

    def validate_waits(self):
        for e in ENGINES:
            for i, (fn, waits, incs) in enumerate(self.ops[e]):
                for nm, v in waits:
                    tot = self.cnt.get(nm, 0)
                    assert v <= tot, (
                        f"engine {e} op {i}: wait {nm}>={v} unreachable "
                        f"(final={tot})")

    def replay(self, nc, sems):
        self.validate_waits()
        with nc.Block() as block:
            def mk(engine_key):
                def emit(eng):
                    for fn, waits, incs in self.ops[engine_key]:
                        for nm, v in waits:
                            eng.wait_ge(sems[nm], v)
                        inst = fn(eng)
                        for nm, n in incs:
                            assert inst is not None
                            inst.then_inc(sems[nm], n)
                return emit
            block.tensor(mk("T"))
            block.scalar(mk("A"))
            block.vector(mk("V"))
            block.gpsimd(mk("P"))
            block.sync(mk("S"))


SEM_NAMES = ["ph", "wl", "xa", "mm", "ac", "ve", "po", "st", "ex"]


def dma(eng, out, in_):
    return eng.dma_start(out=out, in_=in_)


class Ctx:
    def __init__(self, nc, s):
        self.nc = nc
        self.s = s
        self.t = {}


def emit_ln(c, x_name, xn_name, T, psA, psB):
    s = c.s
    x = c.t[x_name]
    xn = c.t[xn_name]
    cast = c.t["lncast"]
    sq = c.t["lnsq"]
    rows = c.t["lnrows"]
    onesd = c.t["onesd"]
    onesk1 = c.t["onesk1"]
    t1 = c.t["lnt1"]
    ntg = T // 512
    ndc = 8

    prev_stats_mm = 0
    for tg in range(ntg):
        for dc in range(ndc):
            xs = x[:, dc * T + tg * 512: dc * T + tg * 512 + 512]
            aw = [("mm", prev_stats_mm)] if (dc == 0 and prev_stats_mm) else ()
            s.op("A", (lambda xs=xs, dc=dc: lambda eng: eng.activation(
                cast[:, dc * 512:(dc + 1) * 512], xs, AF.Copy))(), waits=aw)
            inc = []
            if dc == ndc - 1:
                s.tick("ac")
                inc = [("ac", 1)]
            s.op("A", (lambda xs=xs, dc=dc: lambda eng: eng.activation(
                sq[:, dc * 512:(dc + 1) * 512], xs, AF.Square))(), incs=inc)
        a_done = s.val("ac")
        for dc in range(ndc):
            st0 = (dc == 0)
            sp = (dc == ndc - 1)
            w = [("ac", a_done)] if dc == 0 else ()
            s.op("T", (lambda dc=dc, st0=st0, sp=sp: lambda eng: eng.matmul(
                psA[0:1, 0:512], onesd[:, 0:1],
                cast[:, dc * 512:(dc + 1) * 512], start=st0, stop=sp))(), waits=w)
            inc = []
            if sp:
                s.tick("mm")
                inc = [("mm", 1)]
            s.op("T", (lambda dc=dc, st0=st0, sp=sp: lambda eng: eng.matmul(
                psA[0:1, 512:1024], onesd[:, 0:1],
                sq[:, dc * 512:(dc + 1) * 512], start=st0, stop=sp))(), incs=inc)
        mm_done = s.val("mm")
        prev_stats_mm = mm_done
        m_ = rows[0:1, 0:512]
        e2 = rows[0:1, 512:1024]
        var = rows[0:1, 512:1024]
        lnv = rows[0:1, 512:1024]
        rstd = rows[0:1, 1024:1536]
        bias = rows[0:1, 1536:2048]
        s.ser("V", "ve", lambda eng: eng.tensor_scalar_mul(
            m_, psA[0:1, 0:512], 1.0 / 1024.0), waits=[("mm", mm_done)])
        s.ser("V", "ve", lambda eng: eng.tensor_scalar(
            e2, psA[0:1, 512:1024], 1.0 / 1024.0, 1e-5,
            op0=ALU.mult, op1=ALU.add))
        s.ser("V", "ve", lambda eng: eng.scalar_tensor_tensor(
            rstd, m_, -1.0, m_, op0=ALU.mult, op1=ALU.mult))
        vt = s.tick("ve")
        s.op("V", lambda eng: eng.tensor_add(var, rstd, e2), incs=[("ve", 1)])
        at = s.tick("ac")
        s.op("A", lambda eng: eng.activation(lnv, var, AF.Ln),
             waits=[("ve", vt)])
        s.op("A", lambda eng: eng.activation(rstd, lnv, AF.Exp, scale=-0.5),
             incs=[("ac", 1)])
        vt2 = s.tick("ve")
        s.op("V", lambda eng: eng.scalar_tensor_tensor(
            bias, m_, -1.0, rstd, op0=ALU.mult, op1=ALU.mult),
            waits=[("ac", at)], incs=[("ve", 1)])
        s.op("V", lambda eng: eng.nop(), waits=[("ve", vt2)])
        mt = s.tick("mm")
        s.op("T", lambda eng: eng.matmul(psB[:, 0:512], onesk1[0:1, :], rstd,
                                         start=True, stop=True),
             waits=[("ve", vt2)])
        s.op("T", lambda eng: eng.matmul(psB[:, 512:1024], onesk1[0:1, :], bias,
                                         start=True, stop=True),
             incs=[("mm", 1)])
        for dc in range(ndc):
            xs = x[:, dc * T + tg * 512: dc * T + tg * 512 + 512]
            xno = xn[:, dc * T + tg * 512: dc * T + tg * 512 + 512]
            w = [("mm", mt)] if dc == 0 else ()
            s.op("V", (lambda xs=xs: lambda eng: eng.tensor_mul(t1, xs, psB[:, 0:512]))(),
                 waits=w)
            inc = []
            if dc == ndc - 1:
                s.tick("ve")
                inc = [("ve", 1)]
            s.op("V", (lambda xno=xno: lambda eng: eng.tensor_add(
                xno, t1, psB[:, 512:1024]))(), incs=inc)
        ve_done = s.val("ve")
        s.op("T", lambda eng: eng.nop(), waits=[("ve", ve_done)])


def emit_projA(c, w_dram, act_name, out_name, ndc, nfc, T, ps2, wbuf_name,
               evac=None, act_T=None):
    s = c.s
    act = c.t[act_name]
    out = c.t[out_name] if out_name else None
    wbuf = c.t[wbuf_name]
    if act_T is None:
        act_T = T
    ntg = T // 512
    wsz = ndc * 128
    ping_guard = {0: 0, 1: 0}
    wfree = {}
    for fc in range(nfc):
        buf = fc % 2
        wt = s.tick("wl", 16)
        w = [("mm", wfree[fc - 2])] if fc >= 2 else []
        s.op("P", (lambda fc=fc, buf=buf: lambda eng: eng.dma_start(
            out=wbuf[:, buf * wsz:(buf + 1) * wsz], in_=w_dram[fc]))(),
            waits=w, incs=[("wl", 16)])
        for tg in range(ntg):
            png = (fc * ntg + tg) % 2
            pslice = ps2[:, png * 512:(png + 1) * 512]
            for dc in range(ndc):
                w = []
                if dc == 0:
                    if tg == 0:
                        w.append(("wl", wt))
                    if ping_guard[png]:
                        w.append(("ve", ping_guard[png]))
                st0 = (dc == 0)
                sp = (dc == ndc - 1)
                inc = []
                if sp:
                    s.tick("mm")
                    inc = [("mm", 1)]
                s.op("T", (lambda tg=tg, dc=dc, st0=st0, sp=sp, buf=buf,
                           pslice=pslice: lambda eng: eng.matmul(
                    pslice, wbuf[:, buf * wsz + dc * 128: buf * wsz + dc * 128 + 128],
                    act[:, dc * act_T + tg * 512: dc * act_T + tg * 512 + 512],
                    start=st0, stop=sp))(), waits=w, incs=inc)
            mmv = s.val("mm")
            if evac is None:
                ev = s.tick("ve")
                s.op("V", (lambda fc=fc, tg=tg, pslice=pslice: lambda eng:
                           eng.tensor_copy(
                               out[:, fc * T + tg * 512: fc * T + tg * 512 + 512],
                               pslice))(),
                     waits=[("mm", mmv)], incs=[("ve", 1)])
            else:
                evac(fc, tg, pslice, mmv)
                ev = s.val("ve")
            ping_guard[png] = ev
        wfree[fc] = s.val("mm")


def emit_exchange_write(c, payload_ops, epoch):
    s = c.s
    for fn, waits in payload_ops:
        s.tick("ex", 16)
        s.op("P", fn, waits=waits, incs=[("ex", 16)])
    exv = s.val("ex")
    flagvals = c.t["flagvals"]
    fidx = c.t["fidx"]
    shf = c.t["shf"]
    s.tick("ex", 16)
    s.op("P", (lambda epoch=epoch: lambda eng: eng.indirect_dma_start(
        out=shf, out_offset=IndirectOffsetOnAxis(fidx, 0),
        in_=flagvals[:, epoch - 1:epoch], in_offset=None))(),
        waits=[("ex", exv)], incs=[("ex", 16)])


def emit_exchange_poll(c, epoch, msem_i):
    s = c.s
    shf = c.t["shf"]
    pf = c.t["pfidx"]
    flagrd = c.t["flagrd"]

    flagvals = c.t["flagvals"]

    def fn(eng):
        nc = c.nc
        msem = nc.monotonic_semaphore(msem_i, engine=eng)
        go = eng.alloc_register(f"go{epoch}")
        fv = eng.alloc_register(f"fv{epoch}")
        tgt = eng.alloc_register(f"tg{epoch}")
        eng.load(tgt, flagvals[0:1, epoch - 1:epoch])
        eng.reg_mov(go, 1)
        with eng.While(go):
            msem.inc_expected(16)
            eng.indirect_dma_start(
                out=flagrd[:, :], out_offset=None,
                in_=shf, in_offset=IndirectOffsetOnAxis(pf, 0)
            ).then_inc(msem.sem(), 16)
            msem.wait()
            eng.load(fv, flagrd[0:1, 0:1])
            eng.reg_alu(go, fv, tgt, ALU.not_equal)
        return eng.nop()
    s.tick("po")
    s.op("P", fn, incs=[("po", 1)])
    return s.val("po")



def emit_l0_norm_mhat(c, psN, psM, kdup_ready=()):
    s = c.s
    T = 2048
    q = c.t["q"]
    kdup = c.t["kdup"]
    qsq = c.t["qsq"]
    ksq = c.t["ksq"]
    nr = c.t["nrows"]
    mhneg = c.t["mhneg"]
    onesd = c.t["onesd"]
    negk1 = c.t["negk1"]
    at = s.tick("ac")
    s.op("A", lambda eng: eng.activation(ksq[0:64, :], kdup[0:64, :], AF.Square),
         waits=kdup_ready, incs=[("ac", 1)])
    vt = s.tick("ve")
    s.op("V", lambda eng: eng.memset(nr[0:1, 0:4], 0.0), incs=[("ve", 1)])
    for tg in range(8):
        mt = s.tick("mm")
        s.op("T", (lambda tg=tg: lambda eng: eng.matmul(
            psN[0:1, 0:512], onesd[0:64, 0:1],
            ksq[0:64, tg * 512:(tg + 1) * 512], start=True, stop=True))(),
            waits=[("ac", at), ("ve", vt)] if tg == 0 else [("ve", s.val("ve"))],
            incs=[("mm", 1)])
        s.ser("V", "ve", (lambda: lambda eng: eng.reduce_max(
            nr[0:1, 4:5], psN[0:1, 0:512], axis=AX.X))(),
            waits=[("mm", mt)])
        s.ser("V", "ve", lambda eng: eng.tensor_max(
            nr[0:1, 0:1], nr[0:1, 0:1], nr[0:1, 4:5]))
    kmax_done = s.val("ve")
    prev_norm_mm = 0
    for hp in range(8):
        at = s.tick("ac")
        aw = [("mm", prev_norm_mm)] if prev_norm_mm else ()
        s.op("A", (lambda hp=hp: lambda eng: eng.activation(
            qsq[:, :], q[:, hp * T:(hp + 1) * T], AF.Square))(), waits=aw,
            incs=[("ac", 1)])
        s.op("V", (lambda: lambda eng: eng.memset(nr[0:1, 8:9], 0.0))())
        vt0 = s.tick("ve")
        s.op("V", lambda eng: eng.engine_nop(), incs=[("ve", 1)])
        for tg in range(4):
            for half in range(2):
                mt = s.tick("mm")
                s.op("T", (lambda tg=tg, half=half: lambda eng: eng.matmul(
                    psN[0:1, 512:1024],
                    onesd[half * 64:half * 64 + 64, 0:1],
                    qsq[half * 64:half * 64 + 64, tg * 512:(tg + 1) * 512],
                    start=True, stop=True,
                    tile_position=(half * 64, 0)))(),
                    waits=[("ac", at), ("ve", s.val("ve"))], incs=[("mm", 1)])
                s.ser("V", "ve", (lambda: lambda eng: eng.reduce_max(
                    nr[0:1, 9:10], psN[0:1, 512:1024], axis=AX.X))(),
                    waits=[("mm", mt)])
                s.ser("V", "ve", lambda eng: eng.tensor_max(
                    nr[0:1, 8:9], nr[0:1, 8:9], nr[0:1, 9:10]))
        at2 = s.tick("ac")
        s.op("A", lambda eng: eng.activation(nr[0:1, 16:17], nr[0:1, 0:1], AF.Ln),
             waits=[("ve", s.val("ve"))])
        s.op("A", lambda eng: eng.activation(nr[0:1, 17:18], nr[0:1, 8:9], AF.Ln),
             incs=[("ac", 1)])
        s.ser("V", "ve", lambda eng: eng.tensor_add(
            nr[0:1, 18:19], nr[0:1, 16:17], nr[0:1, 17:18]),
            waits=[("ac", at2)])
        vt3 = s.tick("ve")
        s.op("V", lambda eng: eng.tensor_scalar(
            nr[0:1, 19:20], nr[0:1, 18:19], 0.5, -2.0794415416798357,
            op0=ALU.mult, op1=ALU.add), incs=[("ve", 1)])
        at3 = s.tick("ac")
        s.op("A", lambda eng: eng.activation(nr[0:1, 20:21], nr[0:1, 19:20], AF.Exp),
             waits=[("ve", vt3)], incs=[("ac", 1)])
        mt2 = s.tick("mm")
        prev_norm_mm = mt2
        s.op("T", lambda eng: eng.matmul(psM[:, 0:1], negk1[0:1, :], nr[0:1, 20:21],
                                         start=True, stop=True),
             waits=[("ac", at3)], incs=[("mm", 1)])
        s.ser("V", "ve", (lambda hp=hp: lambda eng: eng.tensor_copy(
            mhneg[:, hp:hp + 1], psM[:, 0:1]))(),
            waits=[("mm", mt2)])
        s.op("T", lambda eng: eng.nop(), waits=[("ve", s.val("ve"))])


L0_OFF = [0, 20, 44, 72]


def emit_l0_attn(c, psS, psO, psC):
    prev_evac_ve = 0
    prev_bcast_mm = 0
    """psS: [128, 2048] f32 (4 banks: two sc-slots of [128,1024]);
    psO: [128, 1024] f32 (outT_A bank, outT_B bank); psC: [128, 1024] bcast.
    biascomb [128, 104] f32 = biasmod + mhneg[hp] (computed per hp here);
    masks [128, 8*1024] bf16; p_sb [128, 2048] bf16 (2 slots);
    out: attn_fm [128, 8*T] bf16; linv [1, 1024] f32; osb [128, 1024] bf16.
    """
    s = c.s
    T = 2048
    q = c.t["q"]
    kdup = c.t["kdup"]
    vaug = c.t["vaug"]
    attn = c.t["attn"]
    p_sb = c.t["p_sb"]
    masks = c.t["masks"]
    biasmod = c.t["biasmod"]
    biascomb = c.t["biascomb"]
    mhneg = c.t["mhneg"]
    linv = c.t["linv"]
    osb = c.t["osb"]
    for hp in range(8):
        vtb = s.tick("ve")
        s.op("V", (lambda hp=hp: lambda eng: eng.tensor_scalar(
            biascomb[:, :], biasmod[:, :], mhneg[:, hp:hp + 1], None,
            op0=ALU.add))(), incs=[("ve", 1)])
        s.op("V", lambda eng: eng.nop(), waits=[("ve", vtb)])
        for j in range(4):
            nsc = 4 * j + 20
            exp_tick = {}
            av_tick = {}
            mask_tick = {}
            for sci in range(nsc):
                sc = sci
                slot = sci % 2
                pss = psS[:, slot * 1024: slot * 1024 + 1024]
                w = []
                if sci == 0:
                    w.append(("ve", vtb))
                if sci >= 2:
                    w.append(("ac", exp_tick[sci - 2]))
                s.op("T", (lambda hp=hp, j=j, sc=sc, pss=pss: lambda eng: eng.matmul(
                    pss[:, 0:512], kdup[0:64, sc * 128:(sc + 1) * 128],
                    q[0:64, hp * T + j * 512: hp * T + j * 512 + 512],
                    start=True, stop=True, tile_position=(0, 0)))(), waits=w)
                mt = s.tick("mm")
                s.op("T", (lambda hp=hp, j=j, sc=sc, pss=pss: lambda eng: eng.matmul(
                    pss[:, 512:1024], kdup[64:128, sc * 128:(sc + 1) * 128],
                    q[64:128, hp * T + j * 512: hp * T + j * 512 + 512],
                    start=True, stop=True, tile_position=(64, 0)))(),
                    incs=[("mm", 1)])
                aw = [("mm", mt)]
                if sci >= 2:
                    aw.append(("mm", av_tick[sci - 2]))
                at = s.tick("ac")
                exp_tick[sci] = at
                col = L0_OFF[j] + sc
                s.op("A", (lambda pss=pss, slot=slot, col=col: lambda eng:
                           eng.activation(p_sb[:, slot * 1024: slot * 1024 + 1024],
                                          pss, AF.Exp,
                                          bias=biascomb[:, col:col + 1],
                                          scale=0.125))(),
                    waits=aw, incs=[("ac", 1)])
                grp = None
                if sc >= 4 * j and sc < 4 * j + 4:
                    grp = sc - 4 * j
                elif sc >= 16 + 4 * j and sc < 20 + 4 * j:
                    grp = 4 + (sc - 16 - 4 * j)
                need_mask = grp is not None
                if need_mask:
                    vt = s.tick("ve")
                    mask_tick[sci] = vt
                    s.op("V", (lambda slot=slot, grp=grp: lambda eng:
                               eng.tensor_mul(
                                   p_sb[:, slot * 1024: slot * 1024 + 1024],
                                   p_sb[:, slot * 1024: slot * 1024 + 1024],
                                   masks[:, grp * 1024:(grp + 1) * 1024]))(),
                        waits=[("ac", at)], incs=[("ve", 1)])
                avw = [("ac", at)]
                if need_mask:
                    avw.append(("ve", mask_tick[sci]))
                st0 = (sci == 0)
                sp = (sci == nsc - 1)
                s.op("T", (lambda slot=slot, sc=sc, st0=st0, sp=sp: lambda eng:
                           eng.matmul(psO[0:65, 0:512],
                                      vaug[:, sc * 65: sc * 65 + 65],
                                      p_sb[:, slot * 1024: slot * 1024 + 512],
                                      start=st0, stop=sp))(), waits=avw)
                mt2 = s.tick("mm")
                av_tick[sci] = mt2
                s.op("T", (lambda slot=slot, sc=sc, st0=st0, sp=sp: lambda eng:
                           eng.matmul(psO[0:65, 512:1024],
                                      vaug[:, sc * 65: sc * 65 + 65],
                                      p_sb[:, slot * 1024 + 512: slot * 1024 + 1024],
                                      start=st0, stop=sp))(), incs=[("mm", 1)])
            av_last = av_tick[nsc - 1]
            vt = s.tick("ve")
            rw_ = [("mm", av_last)]
            if prev_bcast_mm:
                rw_.append(("mm", prev_bcast_mm))
            s.op("V", lambda eng: eng.reciprocal(linv[0:1, 0:512], psO[64:65, 0:512]),
                 waits=rw_)
            s.op("V", lambda eng: eng.reciprocal(linv[0:1, 512:1024],
                                                 psO[64:65, 512:1024]),
                 incs=[("ve", 1)])
            s.op("V", lambda eng: eng.nop(), waits=[("ve", vt)])
            mt3 = s.tick("mm")
            s.op("T", lambda eng: eng.matmul(psC[:, 0:512], c.t["onesk1"][0:1, :],
                                             linv[0:1, 0:512], start=True, stop=True),
                 waits=[("ve", vt)])
            s.op("T", lambda eng: eng.matmul(psC[:, 512:1024], c.t["onesk1"][0:1, :],
                                             linv[0:1, 512:1024], start=True,
                                             stop=True), incs=[("mm", 1)])
            at4 = s.tick("ac")
            aw_ = [("mm", av_last)]
            if prev_evac_ve:
                aw_.append(("ve", prev_evac_ve))
            s.op("A", lambda eng: eng.activation(osb[0:64, 0:512], psO[0:64, 0:512],
                                                 AF.Copy), waits=aw_)
            s.op("A", lambda eng: eng.activation(osb[64:128, 512:1024],
                                                 psO[0:64, 512:1024], AF.Copy),
                 incs=[("ac", 1)])
            ve5 = s.tick("ve", 2)
            s.op("V", (lambda hp=hp, j=j: lambda eng: eng.tensor_mul(
                attn[0:64, hp * T + j * 512: hp * T + j * 512 + 512],
                osb[0:64, 0:512], psC[0:64, 0:512]))(),
                waits=[("mm", mt3), ("ac", at4)], incs=[("ve", 1)])
            s.op("V", (lambda hp=hp, j=j: lambda eng: eng.tensor_mul(
                attn[64:128, hp * T + j * 512: hp * T + j * 512 + 512],
                osb[64:128, 512:1024], psC[64:128, 512:1024]))(),
                incs=[("ve", 1)])
            prev_evac_ve = s.val("ve")
            prev_bcast_mm = mt3
            s.op("T", lambda eng: eng.nop(), waits=[("ve", s.val("ve"))])


def emit_residual_proj(c, w_dram, act_name, x_name, ndc, nfc, T, ps2, wbuf):
    x = c.t[x_name]
    s = c.s

    def evac(fc, tg, pslice, mmv):
        s.tick("ve")
        s.op("V", (lambda fc=fc, tg=tg, pslice=pslice: lambda eng: eng.tensor_add(
            x[:, fc * T + tg * 512: fc * T + tg * 512 + 512],
            x[:, fc * T + tg * 512: fc * T + tg * 512 + 512], pslice))(),
            waits=[("mm", mmv)], incs=[("ve", 1)])
    emit_projA(c, w_dram, act_name, None, ndc, nfc, T, ps2, wbuf, evac=evac)


def emit_router_l0(c, rw_sb_name, xn_name, psR, psSt):
    s = c.s
    T = 2048
    xn = c.t[xn_name]
    rw = c.t[rw_sb_name]
    lt = c.t["rlt"]
    rr = c.t["rrows"]
    eq = c.t["reqm"]
    mk = c.t["rmk"]
    msk = c.t["rmsk"]
    re = c.t["re_"]
    mwb = c.t["rmwb"]
    for tt in range(16):
        slot = tt % 2
        psr = psR[:, slot * 512: slot * 512 + 8]
        for dc in range(8):
            w = []
            if tt >= 2 and dc == 0:
                w.append(("ve", c.t[("r_ev", tt - 2)]))
            st0 = (dc == 0)
            sp = (dc == 7)
            inc = []
            if sp:
                s.tick("mm")
                inc = [("mm", 1)]
            s.op("T", (lambda tt=tt, dc=dc, st0=st0, sp=sp, psr=psr: lambda eng:
                       eng.matmul(psr, xn[:, dc * T + tt * 128: dc * T + tt * 128 + 128],
                                  rw[:, dc * 8:(dc + 1) * 8], start=st0, stop=sp))(),
                waits=w, incs=inc)
        mmv = s.val("mm")
        vw = [("mm", mmv)]
        if tt >= 1:
            vw.append(("ac", c.t[("r_ac", tt - 1)]))
            vw.append(("mm", c.t[("r_mt", tt - 1)]))
        ev = s.tick("ve")
        c.t[("r_ev", tt)] = ev
        s.op("V", (lambda psr=psr: lambda eng: eng.tensor_copy(lt, psr))(),
             waits=vw, incs=[("ve", 1)])
        s.op("V", lambda eng: eng.nop(), waits=[("ve", ev)])
        s.ser("V", "ve", lambda eng: eng.reduce_max(rr[:, 0:1], lt, axis=AX.X))
        s.ser("V", "ve", lambda eng: eng.tensor_scalar(eq, lt, rr[:, 0:1], None,
                                                       op0=ALU.is_equal))
        s.ser("V", "ve", lambda eng: eng.scalar_tensor_tensor(
            mk, eq, -1e30, lt, op0=ALU.mult, op1=ALU.add))
        s.ser("V", "ve", lambda eng: eng.reduce_max(rr[:, 1:2], mk, axis=AX.X))
        s.ser("V", "ve", lambda eng: eng.tensor_scalar(msk, lt, rr[:, 1:2], None,
                                                       op0=ALU.is_ge))
        vt = s.tick("ve")
        s.op("V", lambda eng: eng.tensor_scalar_mul(rr[:, 4:5], rr[:, 0:1], -1.0),
             incs=[("ve", 1)])
        at = s.tick("ac")
        c.t[("r_ac", tt)] = at
        s.op("A", lambda eng: eng.activation(re, lt, AF.Exp, bias=rr[:, 4:5]),
             waits=[("ve", vt)], incs=[("ac", 1)])
        s.ser("V", "ve", lambda eng: eng.tensor_mul(re, re, msk),
              waits=[("ac", at)])
        s.ser("V", "ve", lambda eng: eng.reduce_sum(rr[:, 2:3], re, axis=AX.X))
        s.ser("V", "ve", lambda eng: eng.reciprocal(rr[:, 3:4], rr[:, 2:3]))
        s.op("V", lambda eng: eng.tensor_copy(mwb[:, 0:8], msk))
        vt2 = s.tick("ve")
        s.op("V", lambda eng: eng.tensor_scalar(mwb[:, 8:16], re,
                                                rr[:, 3:4], None, op0=ALU.mult),
             incs=[("ve", 1)])
        st0 = (tt == 0)
        sp = (tt == 15)
        mt_ = s.tick("mm")
        c.t[("r_mt", tt)] = mt_
        s.op("T", (lambda st0=st0, sp=sp: lambda eng: eng.matmul(
            psSt[0:16, 0:1], mwb, c.t["onesd"][:, 0:1], start=st0, stop=sp))(),
            waits=[("ve", vt2)], incs=[("mm", 1)])
    mml = s.val("mm")
    vt3 = s.tick("ve")
    s.op("V", lambda eng: eng.tensor_copy(c.t["partials"], psSt[0:16, 0:1]),
         waits=[("mm", mml)], incs=[("ve", 1)])




def emit_l1_vproj(c, wv_sb_name, xn_name, vloc_name, psV):
    s = c.s
    T = 2048
    xn = c.t[xn_name]
    wv = c.t[wv_sb_name]
    vloc = c.t[vloc_name]
    for tt in range(16):
        slot = tt % 2
        psv = psV[:, slot * 1024: slot * 1024 + 1024]
        for dc in range(8):
            w = []
            if dc == 0 and tt >= 2:
                w.append(("ve", c.t[("v_ev", tt - 2)]))
            st0 = (dc == 0)
            sp = (dc == 7)
            for fh in range(2):
                inc = []
                if sp and fh == 1:
                    s.tick("mm")
                    inc = [("mm", 1)]
                s.op("T", (lambda tt=tt, dc=dc, fh=fh, st0=st0, sp=sp, psv=psv:
                           lambda eng: eng.matmul(
                    psv[:, fh * 512:(fh + 1) * 512],
                    xn[:, dc * T + tt * 128: dc * T + tt * 128 + 128],
                    wv[:, dc * 1024 + fh * 512: dc * 1024 + fh * 512 + 512],
                    start=st0, stop=sp))(), waits=w if fh == 0 else (),
                    incs=inc)
        mmv = s.val("mm")
        ev = s.tick("ve")
        c.t[("v_ev", tt)] = ev
        s.op("V", (lambda tt=tt, psv=psv: lambda eng: eng.tensor_copy(
            vloc[:, tt * 1024:(tt + 1) * 1024], psv))(),
            waits=[("mm", mmv)], incs=[("ve", 1)])


def emit_l1_norm_mhat(c, psN, psM, hp, k_ready):
    s = c.s
    T = 2048
    q = c.t["q"]
    nr = c.t["nrows"]
    onesd = c.t["onesd"]
    negk1 = c.t["negk1"]
    mhneg = c.t["mhneg"]
    qsq = c.t["qsq"]
    ksq = c.t["ksq"]
    at = s.tick("ac")
    aw = list(k_ready)
    if c.t.get("l1n_mm"):
        aw.append(("mm", c.t["l1n_mm"]))
    s.op("A", (lambda hp=hp: lambda eng: eng.activation(
        qsq[:, :], q[:, hp * T:(hp + 1) * T], AF.Square))(), waits=aw,
        incs=[("ac", 1)])
    s.ser("V", "ve", lambda eng: eng.memset(nr[0:1, 8:9], 0.0))
    for tg in range(4):
        for half in range(2):
            mt = s.tick("mm")
            s.op("T", (lambda tg=tg, half=half: lambda eng: eng.matmul(
                psN[0:1, 512:1024], onesd[half * 64:half * 64 + 64, 0:1],
                qsq[half * 64:half * 64 + 64, tg * 512:(tg + 1) * 512],
                start=True, stop=True, tile_position=(half * 64, 0)))(),
                waits=[("ac", at), ("ve", s.val("ve"))], incs=[("mm", 1)])
            s.ser("V", "ve", (lambda: lambda eng: eng.reduce_max(
                nr[0:1, 9:10], psN[0:1, 512:1024], axis=AX.X))(),
                waits=[("mm", mt)])
            s.ser("V", "ve", lambda eng: eng.tensor_max(
                nr[0:1, 8:9], nr[0:1, 8:9], nr[0:1, 9:10]))
    s.ser("V", "ve", lambda eng: eng.memset(nr[0:1, 0:1], 0.0))
    for which, tname in ((0, "khp"), (1, "kphp")):
        at2 = s.tick("ac")
        s.op("A", (lambda tname=tname: lambda eng: eng.activation(
            ksq[:, :], c.t[tname][:, :], AF.Square))(),
            waits=[("ve", s.val("ve"))], incs=[("ac", 1)])
        for tg in range(8):
            for half in range(2):
                mt = s.tick("mm")
                s.op("T", (lambda tg=tg, half=half: lambda eng: eng.matmul(
                    psN[0:1, 0:512], onesd[half * 64:half * 64 + 64, 0:1],
                    ksq[half * 64:half * 64 + 64, tg * 512:(tg + 1) * 512],
                    start=True, stop=True, tile_position=(half * 64, 0)))(),
                    waits=[("ac", at2), ("ve", s.val("ve"))], incs=[("mm", 1)])
                s.ser("V", "ve", (lambda: lambda eng: eng.reduce_max(
                    nr[0:1, 4:5], psN[0:1, 0:512], axis=AX.X))(),
                    waits=[("mm", mt)])
                s.ser("V", "ve", lambda eng: eng.tensor_max(
                    nr[0:1, 0:1], nr[0:1, 0:1], nr[0:1, 4:5]))
    at2 = s.tick("ac")
    s.op("A", lambda eng: eng.activation(nr[0:1, 16:17], nr[0:1, 0:1], AF.Ln),
         waits=[("ve", s.val("ve"))])
    s.op("A", lambda eng: eng.activation(nr[0:1, 17:18], nr[0:1, 8:9], AF.Ln),
         incs=[("ac", 1)])
    s.ser("V", "ve", lambda eng: eng.tensor_add(nr[0:1, 18:19], nr[0:1, 16:17],
                                                nr[0:1, 17:18]),
          waits=[("ac", at2)])
    vt3 = s.tick("ve")
    s.op("V", lambda eng: eng.tensor_scalar(
        nr[0:1, 19:20], nr[0:1, 18:19], 0.5, -2.0794415416798357,
        op0=ALU.mult, op1=ALU.add), incs=[("ve", 1)])
    at3 = s.tick("ac")
    s.op("A", lambda eng: eng.activation(nr[0:1, 20:21], nr[0:1, 19:20], AF.Exp),
         waits=[("ve", vt3)], incs=[("ac", 1)])
    mt2 = s.tick("mm")
    c.t["l1n_mm"] = mt2
    s.op("T", lambda eng: eng.matmul(psM[:, 0:1], negk1[0:1, :], nr[0:1, 20:21],
                                     start=True, stop=True),
         waits=[("ac", at3)], incs=[("mm", 1)])
    s.ser("V", "ve", (lambda hp=hp: lambda eng: eng.tensor_copy(
        mhneg[:, hp:hp + 1], psM[:, 0:1]))(), waits=[("mm", mt2)])


def emit_l1_attn_hp(c, psS, psO, hp):
    s = c.s
    T = 2048
    q = c.t["q"]
    attn = c.t["attn"]
    p_sb = c.t["p_sb"]
    mhneg = c.t["mhneg"]
    linv = c.t["linv2"]
    stage = {0: c.t["ostd"], 1: c.t["osp"]}
    alpha_sc = c.t["alpha_sc"]
    st_ = c.t.setdefault("l1a_state", {"prev_av": (0, 0), "prev_ev": 0,
                                       "prev_bc": 0, "prev_comb": 0,
                                       "stage_ac": {0: 0, 1: 0}})
    for j in range(4):
        for var in range(2):
            ksrc = c.t["khp"] if var == 0 else c.t["kphp"]
            exp_tick = {}
            av_tick = {}
            for sci in range(32):
                slot = sci % 2
                pss = psS[:, slot * 1024: slot * 1024 + 1024]
                w = []
                if sci >= 2:
                    w.append(("ac", exp_tick[sci - 2]))
                elif st_["prev_comb"]:
                    w.append(("ve", st_["prev_comb"]))
                s.op("T", (lambda sci=sci, pss=pss, ksrc=ksrc, hp=hp, j=j:
                           lambda eng: eng.matmul(
                    pss[:, 0:512], ksrc[0:64, sci * 128:(sci + 1) * 128],
                    q[0:64, hp * T + j * 512: hp * T + j * 512 + 512],
                    start=True, stop=True, tile_position=(0, 0)))(), waits=w)
                mt = s.tick("mm")
                s.op("T", (lambda sci=sci, pss=pss, ksrc=ksrc, hp=hp, j=j:
                           lambda eng: eng.matmul(
                    pss[:, 512:1024], ksrc[64:128, sci * 128:(sci + 1) * 128],
                    q[64:128, hp * T + j * 512: hp * T + j * 512 + 512],
                    start=True, stop=True, tile_position=(64, 0)))(),
                    incs=[("mm", 1)])
                aw = [("mm", mt)]
                if sci >= 2:
                    aw.append(("mm", av_tick[sci - 2]))
                else:
                    if st_["prev_av"][sci]:
                        aw.append(("mm", st_["prev_av"][sci]))
                at = s.tick("ac")
                exp_tick[sci] = at
                s.op("A", (lambda pss=pss, slot=slot, hp=hp: lambda eng:
                           eng.activation(p_sb[:, slot * 1024: slot * 1024 + 1024],
                                          pss, AF.Exp,
                                          bias=mhneg[:, hp:hp + 1],
                                          scale=0.125))(),
                    waits=aw, incs=[("ac", 1)])
                st0 = (sci == 0)
                sp = (sci == 31)
                avw = [("ac", at)]
                if sci == 0 and st_["prev_ev"]:
                    avw.append(("ve", st_["prev_ev"]))
                s.op("T", (lambda slot=slot, sci=sci, st0=st0, sp=sp: lambda eng:
                           eng.matmul(psO[0:65, 0:512],
                                      c.t["vaughp"][:, sci * 65: sci * 65 + 65],
                                      p_sb[:, slot * 1024: slot * 1024 + 512],
                                      start=st0, stop=sp))(), waits=avw)
                mt2 = s.tick("mm")
                av_tick[sci] = mt2
                s.op("T", (lambda slot=slot, sci=sci, st0=st0, sp=sp: lambda eng:
                           eng.matmul(psO[0:65, 512:1024],
                                      c.t["vaughp"][:, 2080 + sci * 65:
                                                    2080 + sci * 65 + 65],
                                      p_sb[:, slot * 1024 + 512: slot * 1024 + 1024],
                                      start=st0, stop=sp))(), incs=[("mm", 1)])
            av_last = av_tick[31]
            st_["prev_av"] = (av_tick[30], av_tick[31])
            vr = [("mm", av_last)]
            if st_["prev_bc"]:
                vr.append(("mm", st_["prev_bc"]))
            vt = s.tick("ve")
            s.op("V", (lambda var=var: lambda eng: eng.reciprocal(
                linv[0:1, var * 1024: var * 1024 + 512], psO[64:65, 0:512]))(),
                waits=vr)
            s.op("V", (lambda var=var: lambda eng: eng.reciprocal(
                linv[0:1, var * 1024 + 512: var * 1024 + 1024],
                psO[64:65, 512:1024]))(), incs=[("ve", 1)])
            s.op("V", lambda eng: eng.nop(), waits=[("ve", vt)])
            s.ser("V", "ve", (lambda var=var: lambda eng: eng.tensor_scalar(
                linv[0:1, var * 1024: var * 1024 + 1024],
                linv[0:1, var * 1024: var * 1024 + 1024],
                alpha_sc[0:1, var:var + 1], None, op0=ALU.mult))())
            at4 = s.tick("ac")
            aw_ = [("mm", av_last)]
            if st_["prev_comb"]:
                aw_.append(("ve", st_["prev_comb"]))
            s.op("A", (lambda var=var: lambda eng: eng.activation(
                stage[var][0:64, 0:512], psO[0:64, 0:512], AF.Copy))(), waits=aw_)
            s.op("A", (lambda var=var: lambda eng: eng.activation(
                stage[var][64:128, 512:1024], psO[0:64, 512:1024], AF.Copy))(),
                incs=[("ac", 1)])
            st_["stage_ac"][var] = at4
            ev = s.tick("ve")
            s.op("V", lambda eng: eng.nop(), waits=[("ac", at4)], incs=[("ve", 1)])
            st_["prev_ev"] = ev
        lve = s.val("ve")
        mtb = s.tick("mm")
        s.op("T", lambda eng: eng.matmul(psS[:, 0:512], c.t["onesk1"][0:1, :],
                                         linv[0:1, 0:512], start=True, stop=True),
             waits=[("ve", lve)])
        s.op("T", lambda eng: eng.matmul(psS[:, 512:1024], c.t["onesk1"][0:1, :],
                                         linv[0:1, 512:1024], start=True, stop=True))
        s.op("T", lambda eng: eng.matmul(psS[:, 1024:1536], c.t["onesk1"][0:1, :],
                                         linv[0:1, 1024:1536], start=True, stop=True))
        s.op("T", lambda eng: eng.matmul(psS[:, 1536:2048], c.t["onesk1"][0:1, :],
                                         linv[0:1, 1536:2048], start=True,
                                         stop=True), incs=[("mm", 1)])
        st_["prev_bc"] = mtb
        t0 = c.t["l1t0"]
        t1 = c.t["l1t1"]
        s.tick("ve", 6)
        s.op("V", (lambda hp=hp, j=j: lambda eng: eng.tensor_mul(
            t0[0:64, :], c.t["ostd"][0:64, 0:512], psS[0:64, 0:512]))(),
            waits=[("mm", mtb), ("ac", st_["stage_ac"][1])], incs=[("ve", 1)])
        s.op("V", (lambda hp=hp, j=j: lambda eng: eng.tensor_mul(
            t1[0:64, :], c.t["osp"][0:64, 0:512], psS[0:64, 1024:1536]))(),
            incs=[("ve", 1)])
        s.op("V", (lambda hp=hp, j=j: lambda eng: eng.tensor_add(
            attn[0:64, hp * T + j * 512: hp * T + j * 512 + 512],
            t0[0:64, :], t1[0:64, :]))(), incs=[("ve", 1)])
        s.op("V", (lambda hp=hp, j=j: lambda eng: eng.tensor_mul(
            t0[64:128, :], c.t["ostd"][64:128, 512:1024], psS[64:128, 512:1024]))(),
            incs=[("ve", 1)])
        s.op("V", (lambda hp=hp, j=j: lambda eng: eng.tensor_mul(
            t1[64:128, :], c.t["osp"][64:128, 512:1024], psS[64:128, 1536:2048]))(),
            incs=[("ve", 1)])
        s.op("V", (lambda hp=hp, j=j: lambda eng: eng.tensor_add(
            attn[64:128, hp * T + j * 512: hp * T + j * 512 + 512],
            t0[64:128, :], t1[64:128, :]))(), incs=[("ve", 1)])
        ve_end = s.val("ve")
        st_["prev_comb"] = ve_end
        s.op("T", lambda eng: eng.nop(), waits=[("ve", ve_end)])


def emit_stats_meanw(c):
    s = c.s
    pt = c.t["partials_T"]
    mw = c.t["mw"]
    r = c.t["strows"]
    s.ser("V", "ve", lambda eng: eng.reduce_sum(r[:, 0:1], pt[:, 0:8], axis=AX.X))
    s.ser("V", "ve", lambda eng: eng.reduce_sum(r[:, 1:2], pt[:, 8:16], axis=AX.X))
    s.ser("V", "ve", lambda eng: eng.tensor_scalar_max(r[:, 2:3], r[:, 0:1], 1.0))
    s.ser("V", "ve", lambda eng: eng.reciprocal(r[:, 2:3], r[:, 2:3]))
    s.ser("V", "ve", lambda eng: eng.tensor_mul(mw, r[:, 1:2], r[:, 2:3]))
    s.ser("V", "ve", lambda eng: eng.tensor_scalar(r[:, 3:4], r[:, 0:1], 0.0, None,
                                                   op0=ALU.is_gt))
    s.ser("V", "ve", lambda eng: eng.tensor_mul(mw, mw, r[:, 3:4]))


def emit_router_masks_coef(c, rw_sb_name, xn_name, psR, psT, mw_name, coef_name,
                           per_token_w):
    s = c.s
    T = 2048
    xn = c.t[xn_name]
    rw = c.t[rw_sb_name]
    lt = c.t["rlt"]
    rr = c.t["rrows"]
    eq = c.t["reqm"]
    mk = c.t["rmk"]
    msk = c.t["rmsk"]
    re = c.t["re_"]
    cf16 = c.t["rcf16"]
    coef = c.t[coef_name]
    identb = c.t["identb"]
    psTb = c.t["psTb_ap"]
    for tt in range(16):
        slot = tt % 2
        psr = psR[:, slot * 512: slot * 512 + 8]
        for dc in range(8):
            w = []
            if tt >= 2 and dc == 0:
                w.append(("ve", c.t[("rc_ev", tt - 2)]))
            st0 = (dc == 0)
            sp = (dc == 7)
            inc = []
            if sp:
                s.tick("mm")
                inc = [("mm", 1)]
            s.op("T", (lambda tt=tt, dc=dc, st0=st0, sp=sp, psr=psr: lambda eng:
                       eng.matmul(psr, xn[:, dc * T + tt * 128: dc * T + tt * 128 + 128],
                                  rw[:, dc * 8:(dc + 1) * 8], start=st0, stop=sp))(),
                waits=w, incs=inc)
        mmv = s.val("mm")
        vw = [("mm", mmv)]
        if tt >= 1:
            vw.append(("ac", c.t[("rc_ac", tt - 1)]))
            vw.append(("mm", c.t[("rc_mt", tt - 1)]))
        ev = s.tick("ve")
        c.t[("rc_ev", tt)] = ev
        s.op("V", (lambda psr=psr: lambda eng: eng.tensor_copy(lt, psr))(),
             waits=vw, incs=[("ve", 1)])
        s.op("V", lambda eng: eng.nop(), waits=[("ve", ev)])
        s.ser("V", "ve", lambda eng: eng.reduce_max(rr[:, 0:1], lt, axis=AX.X))
        s.ser("V", "ve", lambda eng: eng.tensor_scalar(eq, lt, rr[:, 0:1], None,
                                                       op0=ALU.is_equal))
        s.ser("V", "ve", lambda eng: eng.scalar_tensor_tensor(
            mk, eq, -1e30, lt, op0=ALU.mult, op1=ALU.add))
        s.ser("V", "ve", lambda eng: eng.reduce_max(rr[:, 1:2], mk, axis=AX.X))
        s.ser("V", "ve", lambda eng: eng.tensor_scalar(msk, lt, rr[:, 1:2], None,
                                                       op0=ALU.is_ge))
        if per_token_w:
            vt = s.tick("ve")
            s.op("V", lambda eng: eng.tensor_scalar_mul(rr[:, 4:5], rr[:, 0:1], -1.0),
                 incs=[("ve", 1)])
            at0 = s.tick("ac")
            s.op("A", lambda eng: eng.activation(re, lt, AF.Exp, bias=rr[:, 4:5]),
                 waits=[("ve", vt)], incs=[("ac", 1)])
            s.ser("V", "ve", lambda eng: eng.reduce_sum(rr[:, 2:3], re, axis=AX.X),
                  waits=[("ac", at0)])
            s.ser("V", "ve", lambda eng: eng.reciprocal(rr[:, 3:4], rr[:, 2:3]))
            s.ser("V", "ve", lambda eng: eng.tensor_scalar(re, re, rr[:, 3:4], None,
                                                           op0=ALU.mult))
            vt1 = s.val("ve")
            at = s.tick("ac")
            c.t[("rc_ac", tt)] = at
            s.op("A", lambda eng: eng.activation(eq, re, AF.Exp),
                 waits=[("ve", vt1)], incs=[("ac", 1)])
            s.ser("V", "ve", lambda eng: eng.tensor_mul(eq, eq, msk),
                  waits=[("ac", at)])
            s.ser("V", "ve", lambda eng: eng.reduce_sum(rr[:, 6:7], eq, axis=AX.X))
            s.ser("V", "ve", lambda eng: eng.reciprocal(rr[:, 7:8], rr[:, 6:7]))
            vt2 = s.tick("ve")
            s.op("V", lambda eng: eng.tensor_scalar(cf16, eq, rr[:, 7:8], None,
                                                    op0=ALU.mult),
                 incs=[("ve", 1)])
        else:
            c.t[("rc_ac", tt)] = s.val("ac")
            vt2 = s.tick("ve")
            s.op("V", lambda eng: eng.tensor_copy(cf16, msk), incs=[("ve", 1)])
        mt_ = s.tick("mm")
        c.t[("rc_mt", tt)] = mt_
        s.op("T", (lambda tt=tt: lambda eng: eng.transpose(
            psTb[0:8, 0:128], cf16, identb))(),
            waits=[("ve", vt2)] + ([("ve", c.t[("rc_tev", tt - 1)])] if tt else []),
            incs=[("mm", 1)])
        tev = s.tick("ve")
        c.t[("rc_tev", tt)] = tev
        if per_token_w:
            s.op("V", (lambda tt=tt: lambda eng: eng.tensor_copy(
                coef[0:8, tt * 128:(tt + 1) * 128], psTb[0:8, 0:128]))(),
                waits=[("mm", mt_)], incs=[("ve", 1)])
        else:
            s.op("V", (lambda tt=tt: lambda eng: eng.tensor_scalar(
                coef[0:8, tt * 128:(tt + 1) * 128], psTb[0:8, 0:128],
                c.t[mw_name][:, 0:1], None, op0=ALU.mult))(),
                waits=[("mm", mt_)], incs=[("ve", 1)])


def emit_moe(c, gw_dram, uw_dram, dw_dram, xn_name, x_name, coef_name,
             nE, nI, T, psGU0, psGU1):
    s = c.s
    xn = c.t[xn_name]
    x = c.t[x_name]
    coef = c.t[coef_name]
    h = c.t["h_sb"]
    sg = c.t["sg_sb"]
    gub = c.t["gub"]
    dwb = c.t["dwb"]
    sel8 = c.t["sel8"]
    cb = c.t["cb_sb"]
    mt0 = c.t["moe_t0"]
    nic = nI // 128
    for e in range(nE):
        for th in range(2):
            for ic in range(nic):
                gslot = ic % 2
                wt = s.tick("wl", 32)
                w = []
                if (e, th, ic) != (0, 0, 0):
                    w.append(("mm", c.t.get(("moe_gu_mm", gslot), 0)))
                s.op("P", (lambda e=e, ic=ic, gslot=gslot: lambda eng: eng.dma_start(
                    out=gub[:, gslot * 2048: gslot * 2048 + 1024],
                    in_=gw_dram[e, ic]))(), waits=w, incs=[("wl", 16)])
                s.op("P", (lambda e=e, ic=ic, gslot=gslot: lambda eng: eng.dma_start(
                    out=gub[:, gslot * 2048 + 1024: gslot * 2048 + 2048],
                    in_=uw_dram[e, ic]))(), incs=[("wl", 16)])
                psgu = psGU0 if gslot == 0 else psGU1
                for tgl in range(2):
                    tg = th * 2 + tgl
                    for dc in range(8):
                        w = []
                        if dc == 0 and tgl == 0:
                            w.append(("wl", wt))
                            w.append(("ve", c.t.get(("moe_gu_ve", gslot), 0)))
                        st0 = (dc == 0)
                        sp = (dc == 7)
                        s.op("T", (lambda ic=ic, tg=tg, tgl=tgl, dc=dc, st0=st0,
                                   sp=sp, gslot=gslot, psgu=psgu: lambda eng:
                                   eng.matmul(
                            psgu[:, tgl * 512: tgl * 512 + 512],
                            gub[:, gslot * 2048 + dc * 128: gslot * 2048 + dc * 128 + 128],
                            xn[:, dc * T + tg * 512: dc * T + tg * 512 + 512],
                            start=st0, stop=sp))(), waits=w)
                        inc = []
                        if sp:
                            s.tick("mm")
                            inc = [("mm", 1)]
                        s.op("T", (lambda ic=ic, tg=tg, tgl=tgl, dc=dc, st0=st0,
                                   sp=sp, gslot=gslot, psgu=psgu: lambda eng:
                                   eng.matmul(
                            psgu[:, 1024 + tgl * 512: 1024 + tgl * 512 + 512],
                            gub[:, gslot * 2048 + 1024 + dc * 128:
                                gslot * 2048 + 1024 + dc * 128 + 128],
                            xn[:, dc * T + tg * 512: dc * T + tg * 512 + 512],
                            start=st0, stop=sp))(), incs=inc)
                mmv = s.val("mm")
                c.t[("moe_gu_mm", gslot)] = mmv
                at = s.tick("ac")
                sgw = [("mm", mmv)]
                if c.t.get(("moe_sg_ve", gslot)):
                    sgw.append(("ve", c.t[("moe_sg_ve", gslot)]))
                s.op("A", (lambda psgu=psgu, gslot=gslot: lambda eng: eng.activation(
                    sg[:, gslot * 1024: gslot * 1024 + 1024],
                    psgu[:, 0:1024], AF.Silu))(), waits=sgw, incs=[("ac", 1)])
                ve = s.tick("ve")
                c.t[("moe_gu_ve", gslot)] = ve
                c.t[("moe_sg_ve", gslot)] = ve
                s.op("V", (lambda ic=ic, psgu=psgu, gslot=gslot: lambda eng:
                           eng.tensor_mul(h[:, ic * 1024:(ic + 1) * 1024],
                                          sg[:, gslot * 1024: gslot * 1024 + 1024],
                                          psgu[:, 1024:2048]))(),
                    waits=[("ac", at)], incs=[("ve", 1)])
            gu_all = s.val("ve")
            s.op("T", lambda eng: eng.nop(), waits=[("ve", gu_all)])
            s.barrier()
            for tgl in range(2):
                tg = th * 2 + tgl
                mtc = s.tick("mm")
                s.op("T", (lambda e=e, tg=tg: lambda eng: eng.matmul(
                    psGU1[:, 1024:1536], sel8[:, e * 128:(e + 1) * 128],
                    coef[0:8, tg * 512: tg * 512 + 512],
                    start=True, stop=True))(),
                    waits=[("ac", c.t.get("moe_cb_ac_prev", 0))],
                    incs=[("mm", 1)])
                at = s.tick("ac")
                s.op("A", (lambda tgl=tgl: lambda eng: eng.activation(
                    cb[:, tgl * 512: tgl * 512 + 512],
                    psGU1[:, 1024:1536], AF.Copy))(),
                    waits=[("mm", mtc)], incs=[("ac", 1)])
                c.t[("moe_cb_ac", tgl)] = at
                c.t["moe_cb_ac_prev"] = at
            s.barrier()
            hsz = nic * 128 // 2
            for fc in range(8):
                for half in range(2):
                    wt = s.tick("wl", 16)
                    w = []
                    if (fc, half) != (0, 0):
                        w.append(("mm", c.t.get(("moe_dw_mm", half), 0)))
                    s.op("P", (lambda e=e, fc=fc, half=half, hsz=hsz: lambda eng:
                               eng.dma_start(
                        out=dwb[:, half * hsz: half * hsz + hsz],
                        in_=dw_dram[e, fc, :, half * hsz: half * hsz + hsz]))(),
                        waits=w, incs=[("wl", 16)])
                    c.t[("moe_dw_wt", half)] = wt
                for tgl in range(2):
                    tg = th * 2 + tgl
                    dslot = (fc * 2 + tgl) % 2
                    psd = psGU0[:, dslot * 512: dslot * 512 + 512]
                    for ic in range(nic):
                        half = ic // (nic // 2)
                        w = []
                        if ic == 0:
                            w.append(("wl", c.t[("moe_dw_wt", 0)]))
                            w.append(("ve", c.t.get(("moe_dn_ve", dslot), 0)))
                        if ic == nic // 2:
                            w.append(("wl", c.t[("moe_dw_wt", 1)]))
                        st0 = (ic == 0)
                        sp = (ic == nic - 1)
                        inc = []
                        if sp:
                            s.tick("mm")
                            inc = [("mm", 1)]
                        s.op("T", (lambda ic=ic, fc=fc, tg=tg, half=half, st0=st0,
                                   sp=sp, psd=psd, hsz=hsz: lambda eng: eng.matmul(
                            psd, dwb[:, half * hsz + (ic % (nic // 2)) * 128:
                                     half * hsz + (ic % (nic // 2)) * 128 + 128],
                            h[:, ic * 1024 + (tg % 2) * 512:
                              ic * 1024 + (tg % 2) * 512 + 512],
                            start=st0, stop=sp))(), waits=w, incs=inc)
                    mmd = s.val("mm")
                    c.t[("moe_dw_mm", 0)] = mmd
                    c.t[("moe_dw_mm", 1)] = mmd
                    vt = s.tick("ve", 2)
                    c.t[("moe_dn_ve", dslot)] = vt
                    s.op("V", (lambda fc=fc, tg=tg, psd=psd, tgl=tgl: lambda eng:
                               eng.tensor_mul(mt0, psd,
                                              cb[:, tgl * 512: tgl * 512 + 512]))(),
                        waits=[("mm", mmd), ("ac", c.t[("moe_cb_ac", tgl)])],
                        incs=[("ve", 1)])
                    s.op("V", (lambda fc=fc, tg=tg: lambda eng: eng.tensor_add(
                        x[:, fc * T + tg * 512: fc * T + tg * 512 + 512],
                        x[:, fc * T + tg * 512: fc * T + tg * 512 + 512],
                        mt0))(), incs=[("ve", 1)])
            s.barrier()

def tile_lhsT(wT, ndc, nfc):
    d, f = wT.shape
    assert d == ndc * 128 and f == nfc * 128, (wT.shape, ndc, nfc)
    a = wT.reshape(ndc, 128, nfc, 128).transpose(2, 1, 0, 3)
    return np.ascontiguousarray(a.reshape(nfc, 128, ndc * 128))


def fm_pack(a2d):
    D, T = a2d.shape
    return np.ascontiguousarray(
        a2d.reshape(D // 128, 128, T).transpose(1, 0, 2).reshape(128, -1))


def fm_unpack(a, T):
    ndc = a.shape[1] // T
    return np.ascontiguousarray(
        a.reshape(128, ndc, T).transpose(1, 0, 2).reshape(ndc * 128, T))


def build_KA(dev_outputs=False):
    nc = bass.Bass("TRN2", target_bir_lowering=False, num_devices=8)
    s = Sched()
    c = Ctx(nc, s)
    T = 2048
    xio = nc.dram_tensor("x_c", [128, 8 * T], F32, kind="ExternalInput").ap()
    wcd = nc.dram_tensor("wcd", [4, 128, 8 * 128], F32, kind="ExternalInput").ap()
    wq = nc.dram_tensor("wq", [8, 128, 4 * 128], F32, kind="ExternalInput").ap()
    wkv = nc.dram_tensor("wkv", [1, 128, 4 * 128], F32, kind="ExternalInput").ap()
    wcp = nc.dram_tensor("wcp", [8, 128, 8 * 128], F32, kind="ExternalInput").ap()
    rwd = nc.dram_tensor("rw", [128, 64], F32, kind="ExternalInput").ap()
    identd = nc.dram_tensor("identb", [128, 128], BF16, kind="ExternalInput").ap()
    masksd = nc.dram_tensor("masks", [128, 8 * 1024], BF16, kind="ExternalInput").ap()
    biasmodd = nc.dram_tensor("biasmod", [128, 104], F32, kind="ExternalInput").ap()
    sidx = nc.dram_tensor("sidx", [128, 1], I32, kind="ExternalInput").ap()
    fidx_d = nc.dram_tensor("fidx", [2, 1], I32, kind="ExternalInput").ap()
    flagvals_d = nc.dram_tensor("flagvals", [2, 4], I32, kind="ExternalInput").ap()
    pfidx_d = nc.dram_tensor("pfidx", [2, 1], I32, kind="ExternalInput").ap()
    kv_sh = nc.dram_tensor("kv_sh", [256, T], BF16, kind="Internal",
                           addr_space="Shared").ap()
    shf = nc.dram_tensor("shf", [4, 1], I32, kind="Internal",
                         addr_space="Shared").ap()
    x2_out = nc.dram_tensor("x2_out", [128, 8 * T], F32, kind="ExternalOutput").ap()
    xn2_out = nc.dram_tensor("xn2_out", [128, 8 * T], BF16, kind="ExternalOutput").ap()
    part_out = nc.dram_tensor("part_out", [16, 1], F32, kind="ExternalOutput").ap()
    c.t["shf"] = shf

    with contextlib.ExitStack() as ex:
        def sb(name, shape, dt, st=ex):
            return st.enter_context(nc.sbuf_tensor(name, shape, dt))[:, :]
        c.t["x"] = sb("x_sb", [128, 8 * T], F32)
        c.t["xn"] = sb("xn", [128, 8 * T], BF16)
        c.t["attn"] = c.t["xn"]
        c.t["lncast"] = sb("lncast", [128, 8 * 512], BF16)
        c.t["lnsq"] = sb("lnsq", [128, 8 * 512], BF16)
        c.t["lnrows"] = sb("lnrows", [1, 2048], F32)
        c.t["lnt1"] = sb("lnt1", [128, 512], F32)
        c.t["onesd"] = sb("onesd", [128, 1], BF16)
        c.t["onesk1"] = sb("onesk1", [1, 128], F32)
        c.t["negk1"] = sb("negk1", [1, 128], F32)
        c.t["q"] = sb("q", [128, 8 * T], BF16)
        c.t["kvloc"] = sb("kvloc", [128, T], BF16)
        c.t["wbufA"] = sb("wbufA", [128, 2 * 8 * 128], BF16)
        c.t["flagvals"] = sb("flagvals_sb", [2, 4], I32)
        c.t["flagrd"] = sb("flagrd_sb", [2, 1], I32)
        c.t["fidx"] = sb("fidx_sb", [2, 1], I32)
        c.t["pfidx"] = sb("pfidx_sb", [2, 1], I32)
        c.t["sidx"] = sb("sidx_sb", [128, 1], I32)
        c.t["kdup"] = sb("kdup", [128, 2 * T], BF16)
        c.t["vaug"] = sb("vaug", [128, 32 * 65], BF16)
        c.t["identb"] = sb("identb_sb", [128, 128], BF16)
        c.t["biasmod"] = sb("biasmod_sb", [128, 104], F32)
        c.t["biascomb"] = sb("biascomb", [128, 104], F32)
        c.t["mhneg"] = sb("mhneg", [128, 8], F32)
        c.t["nrows"] = sb("nrows", [1, 32], F32)
        c.t["rw"] = sb("rw_sb", [128, 64], BF16)
        c.t["partials"] = sb("partials", [16, 1], F32)

        s.tick("xa", 16)
        s.op("S", lambda eng: dma(eng, c.t["x"], xio), incs=[("xa", 16)])
        for nm, src in (("fidx", fidx_d), ("pfidx", pfidx_d), ("sidx", sidx),
                        ("identb", identd), ("flagvals", flagvals_d),
                        ("biasmod", biasmodd)):
            s.tick("xa", 16)
            s.op("S", (lambda nm=nm, src=src: lambda eng: dma(eng, c.t[nm], src))(),
                 incs=[("xa", 16)])
        s.tick("wl", 16)
        s.op("P", lambda eng: dma(eng, c.t["rw"], rwd), incs=[("wl", 16)])
        rw_loaded = s.val("wl")
        xall = s.val("xa")
        vt = s.tick("ve")
        s.op("V", lambda eng: eng.memset(c.t["onesd"], 1.0))
        s.op("V", lambda eng: eng.memset(c.t["onesk1"], 1.0))
        s.op("V", lambda eng: eng.memset(c.t["negk1"], -1.0))
        s.op("V", lambda eng: eng.memset(c.t["vaug"], 1.0), incs=[("ve", 1)])
        for e in "TAVP":
            s.op(e, lambda eng: eng.nop(), waits=[("xa", xall), ("ve", vt)])

        with contextlib.ExitStack() as sc1:
            c.t["lat"] = sb("lat", [128, 4 * T], BF16, sc1)
            c.t["vfm"] = sb("vfm", [64, 2 * T], BF16, sc1)
            psA = sc1.enter_context(nc.psum_tensor("psA", [1, 1024], F32))[:, :]
            psB = sc1.enter_context(nc.psum_tensor("psB", [128, 1024], F32))[:, :]
            ps2 = sc1.enter_context(nc.psum_tensor("ps2", [128, 1024], F32))[:, :]
            pst = sc1.enter_context(nc.psum_tensor("pst", [128, 2048], BF16))[:, :]

            emit_ln(c, "x", "xn", T, psA, psB)
            s.barrier()
            emit_projA(c, wcd, "xn", "lat", 8, 4, T, ps2, "wbufA")
            s.barrier()
            emit_projA(c, wq, "lat", "q", 4, 8, T, ps2, "wbufA")
            s.barrier()
            emit_projA(c, wkv, "lat", "kvloc", 4, 1, T, ps2, "wbufA")
            s.barrier()

            def wr(eng):
                return eng.indirect_dma_start(
                    out=kv_sh, out_offset=IndirectOffsetOnAxis(c.t["sidx"], 0),
                    in_=c.t["kvloc"], in_offset=None)
            emit_exchange_write(c, [(wr, [])], epoch=1)
            emit_exchange_poll(c, epoch=1, msem_i=0)
            pall = s.val("po")
            for hh in range(2):
                for half in range(2):
                    s.tick("ex", 16)
                    s.op("P", (lambda hh=hh, half=half: lambda eng: dma(
                        eng, c.t["kdup"][hh * 64:hh * 64 + 64,
                                         half * T:(half + 1) * T],
                        kv_sh[half * 128:half * 128 + 64, :]))(),
                        waits=[("po", pall)], incs=[("ex", 16)])
            for half in range(2):
                s.tick("ex", 16)
                s.op("P", (lambda half=half: lambda eng: dma(
                    eng, c.t["vfm"][0:64, half * T:(half + 1) * T],
                    kv_sh[half * 128 + 64:half * 128 + 128, :]))(),
                    waits=[("po", pall)], incs=[("ex", 16)])
            exv = s.val("ex")
            tr_ev = {}
            for ch in range(32):
                sl = (ch % 2) * 1024
                w = []
                if ch == 0:
                    w.append(("ex", exv))
                if ch >= 2:
                    w.append(("ve", tr_ev[ch - 2]))
                mt = s.tick("mm")
                s.op("T", (lambda ch=ch, sl=sl: lambda eng: eng.transpose(
                    pst[:, sl:sl + 64], c.t["vfm"][0:64, ch * 128:(ch + 1) * 128],
                    c.t["identb"][0:64, 0:64]))(), waits=w, incs=[("mm", 1)])
                ev = s.tick("ve")
                tr_ev[ch] = ev
                s.op("V", (lambda ch=ch, sl=sl: lambda eng: eng.tensor_copy(
                    c.t["vaug"][:, ch * 65: ch * 65 + 64], pst[:, sl:sl + 64]))(),
                    waits=[("mm", mt)], incs=[("ve", 1)])
            s.barrier()

        with contextlib.ExitStack() as sc2:
            c.t["qsq"] = sb("qsq", [128, 2048], BF16, sc2)
            c.t["ksq"] = sb("ksq", [64, 2 * T], BF16, sc2)
            psN = sc2.enter_context(nc.psum_tensor("psN", [1, 1024], F32))[:, :]
            psM = sc2.enter_context(nc.psum_tensor("psM", [128, 512], F32))[:, :]
            emit_l0_norm_mhat(c, psN, psM, kdup_ready=[("ex", exv)])
            s.barrier()

        with contextlib.ExitStack() as sc3:
            c.t["masks"] = sb("masks_sb", [128, 8 * 1024], BF16, sc3)
            c.t["p_sb"] = sb("p_sb", [128, 2048], BF16, sc3)
            c.t["linv"] = sb("linv", [1, 1024], F32, sc3)
            c.t["osb"] = sb("osb", [128, 1024], BF16, sc3)
            psS = sc3.enter_context(nc.psum_tensor("psS", [128, 2048], F32))[:, :]
            psO = sc3.enter_context(nc.psum_tensor("psO", [128, 1024], F32))[:, :]
            psC = sc3.enter_context(nc.psum_tensor("psC", [128, 1024], F32))[:, :]
            mk = s.tick("xa", 16)
            s.op("S", lambda eng: dma(eng, c.t["masks"], masksd), incs=[("xa", 16)])
            for e in "TAV":
                s.op(e, lambda eng: eng.nop(), waits=[("xa", mk)])
            emit_l0_attn(c, psS, psO, psC)
            s.barrier()

        with contextlib.ExitStack() as sc4:
            ps2 = sc4.enter_context(nc.psum_tensor("ps2b", [128, 1024], F32))[:, :]
            psA = sc4.enter_context(nc.psum_tensor("psA2", [1, 1024], F32))[:, :]
            psB = sc4.enter_context(nc.psum_tensor("psB2", [128, 1024], F32))[:, :]
            emit_residual_proj(c, wcp, "attn", "x", 8, 8, T, ps2, "wbufA")
            s.barrier()
            emit_ln(c, "x", "xn", T, psA, psB)
            s.barrier()

        with contextlib.ExitStack() as sc5:
            c.t["rlt"] = sb("rlt", [128, 8], F32, sc5)
            c.t["rrows"] = sb("rrows", [128, 8], F32, sc5)
            c.t["reqm"] = sb("reqm", [128, 8], F32, sc5)
            c.t["rmk"] = sb("rmk", [128, 8], F32, sc5)
            c.t["rmsk"] = sb("rmsk", [128, 8], F32, sc5)
            c.t["re_"] = sb("re_", [128, 8], F32, sc5)
            c.t["rmwb"] = sb("rmwb", [128, 16], BF16, sc5)
            psR = sc5.enter_context(nc.psum_tensor("psR", [128, 1024], F32))[:, :]
            psSt = sc5.enter_context(nc.psum_tensor("psSt", [16, 1], F32))[:, :]
            s.op("T", lambda eng: eng.nop(), waits=[("wl", rw_loaded)])
            emit_router_l0(c, "rw", "xn", psR, psSt)
            s.barrier()

        for nm, dst in (("x", x2_out), ("xn", xn2_out), ("partials", part_out)):
            s.tick("st", 16)
            s.op("S", (lambda src=c.t[nm], dst=dst: lambda eng: dma(eng, dst, src))(),
                 incs=[("st", 16)])
        stv = s.val("st")
        s.op("S", lambda eng: eng.nop(), waits=[("st", stv)])

        sems = {nm: ex.enter_context(nc.semaphore(nm)) for nm in SEM_NAMES}
        s.replay(nc, sems)
    return nc


def make_flagvals():
    nonce = np.random.randint(1, 2**30, dtype=np.int32)
    return np.broadcast_to(nonce + np.arange(1, 5, dtype=np.int32), (2, 4)).copy()


def build_l0_masks(h):
    p = np.arange(128).reshape(128, 1)
    f = np.arange(512).reshape(1, 512)
    masks = np.zeros((128, 8, 1024), dtype=BF16NP)
    for r in range(4):
        stair = (r * 128 + p <= f).astype(BF16NP)
        stair2 = np.concatenate([stair, stair], axis=1)
        ones = np.ones((128, 1024), dtype=BF16NP)
        masks[:, r, :] = stair2 if h == 0 else ones
        masks[:, 4 + r, :] = stair2 if h == 1 else ones
    biasmod = np.zeros((128, 104), dtype=np.float32)
    for j in range(4):
        g = 4 * h + j
        for sc in range(4 * j + 20):
            col = L0_OFF[j] + sc
            if sc > 4 * g + 3:
                biasmod[:, col] = -1e9
    return np.ascontiguousarray(masks.reshape(128, 8192)), biasmod


def prep_ka_inputs(x, Wc_down, Wq_up, Wkv_up, Wc_proj, router_w):
    x = np.asarray(x, np.float32)
    wcd = tile_lhsT(np.ascontiguousarray(np.asarray(Wc_down, np.float32)), 8, 4)
    wq = tile_lhsT(np.ascontiguousarray(np.asarray(Wq_up, np.float32)), 4, 8)
    wkv = tile_lhsT(np.ascontiguousarray(np.asarray(Wkv_up, np.float32)), 4, 1)
    wcp = tile_lhsT(np.ascontiguousarray(np.asarray(Wc_proj, np.float32)), 8, 8)
    rw = np.ascontiguousarray(
        np.asarray(router_w, np.float32).T.reshape(8, 128, 8)
        .transpose(1, 0, 2).reshape(128, 64))
    ident = np.eye(128, dtype=BF16NP)
    ar = np.arange(128, dtype=np.int32).reshape(128, 1)
    fv = make_flagvals()
    maps = []
    for cid in range(8):
        b, h = divmod(cid, 2)
        xc = fm_pack(np.ascontiguousarray(x[b].T[:, h * 2048:(h + 1) * 2048]))
        masks, biasmod = build_l0_masks(h)
        maps.append({
            "x_c": xc, "wcd": wcd, "wq": wq, "wkv": wkv, "wcp": wcp, "rw": rw,
            "identb": ident, "flagvals": fv, "masks": masks, "biasmod": biasmod,
            "sidx": (h * 128 + ar).astype(np.int32),
            "fidx": np.array([[2 * h], [2 * h + 1]], np.int32),
            "pfidx": np.array([[2 * (1 - h)], [2 * (1 - h) + 1]], np.int32),
        })
    return maps



def build_KB1():
    nc = bass.Bass("TRN2", target_bir_lowering=False, num_devices=8)
    s = Sched()
    c = Ctx(nc, s)
    T = 2048
    x2d = nc.dram_tensor("x2_c", [128, 8 * T], F32, kind="ExternalInput").ap()
    xn2d = nc.dram_tensor("xn2_c", [128, 8 * T], BF16, kind="ExternalInput").ap()
    ptd = nc.dram_tensor("partials_T", [8, 16], F32, kind="ExternalInput").ap()
    rwd = nc.dram_tensor("rw", [128, 64], F32, kind="ExternalInput").ap()
    gwd = nc.dram_tensor("gw_t", [8, 32, 128, 1024], F32, kind="ExternalInput").ap()
    uwd = nc.dram_tensor("uw_t", [8, 32, 128, 1024], F32, kind="ExternalInput").ap()
    dwd = nc.dram_tensor("dw_t", [8, 8, 128, 4096], F32, kind="ExternalInput").ap()
    identd = nc.dram_tensor("identb", [128, 128], BF16, kind="ExternalInput").ap()
    sel8d = nc.dram_tensor("sel8", [8, 1024], BF16, kind="ExternalInput").ap()
    x3_out = nc.dram_tensor("x3_out", [128, 8 * T], F32, kind="ExternalOutput").ap()
    xn3_out = nc.dram_tensor("xn3_out", [128, 8 * T], BF16, kind="ExternalOutput").ap()

    with contextlib.ExitStack() as ex:
        def sb(name, shape, dt, st=ex):
            return st.enter_context(nc.sbuf_tensor(name, shape, dt))[:, :]
        c.t["x"] = sb("x_sb", [128, 8 * T], F32)
        c.t["xn"] = sb("xn", [128, 8 * T], BF16)
        c.t["onesd"] = sb("onesd", [128, 1], BF16)
        c.t["onesk1"] = sb("onesk1", [1, 128], F32)
        c.t["identb"] = sb("identb_sb", [128, 128], BF16)
        c.t["sel8"] = sb("sel8_sb", [8, 1024], BF16)
        c.t["rw"] = sb("rw_sb", [128, 64], BF16)
        c.t["coefL0"] = sb("coefL0", [8, 2048], BF16)
        c.t["partials_T"] = sb("partials_T_sb", [8, 16], F32)
        c.t["mw"] = sb("mw", [8, 1], F32)
        c.t["strows"] = sb("strows", [8, 4], F32)

        s.tick("xa", 16)
        s.op("S", lambda eng: dma(eng, c.t["x"], x2d), incs=[("xa", 16)])
        for nm, src in (("xn", xn2d), ("identb", identd), ("sel8", sel8d),
                        ("partials_T", ptd)):
            s.tick("xa", 16)
            s.op("S", (lambda nm=nm, src=src: lambda eng: dma(eng, c.t[nm], src))(),
                 incs=[("xa", 16)])
        s.tick("wl", 16)
        s.op("P", lambda eng: dma(eng, c.t["rw"], rwd), incs=[("wl", 16)])
        rwl = s.val("wl")
        xall = s.val("xa")
        vt = s.tick("ve")
        s.op("V", lambda eng: eng.memset(c.t["onesd"], 1.0))
        s.op("V", lambda eng: eng.memset(c.t["onesk1"], 1.0), incs=[("ve", 1)])
        for e in "TAVP":
            s.op(e, lambda eng: eng.nop(), waits=[("xa", xall), ("ve", vt),
                                                  ("wl", rwl)])

        with contextlib.ExitStack() as sc1:
            c.t["rlt"] = sb("rlt", [128, 8], F32, sc1)
            c.t["rrows"] = sb("rrows", [128, 8], F32, sc1)
            c.t["reqm"] = sb("reqm", [128, 8], F32, sc1)
            c.t["rmk"] = sb("rmk", [128, 8], F32, sc1)
            c.t["rmsk"] = sb("rmsk", [128, 8], F32, sc1)
            c.t["re_"] = sb("re_", [128, 8], F32, sc1)
            c.t["rcf16"] = sb("rcf16", [128, 8], BF16, sc1)
            psR = sc1.enter_context(nc.psum_tensor("psR", [128, 1024], F32))[:, :]
            c.t["psTb_ap"] = sc1.enter_context(
                nc.psum_tensor("psTb", [8, 128], BF16))[:, :]
            emit_stats_meanw(c)
            emit_router_masks_coef(c, "rw", "xn", psR, None, "mw", "coefL0",
                                   per_token_w=False)
            s.barrier()

        with contextlib.ExitStack() as sc2:
            c.t["h_sb"] = sb("h_sb", [128, 32 * 1024], BF16, sc2)
            c.t["sg_sb"] = sb("sg_sb", [128, 2048], BF16, sc2)
            c.t["gub"] = sb("gub", [128, 4096], BF16, sc2)
            c.t["dwb"] = sb("dwb", [128, 4096], BF16, sc2)
            c.t["cb_sb"] = sb("cb_sb", [128, 1024], F32, sc2)
            c.t["moe_t0"] = sb("moe_t0", [128, 512], F32, sc2)
            psGU0 = sc2.enter_context(nc.psum_tensor("psGU0", [128, 2048], F32))[:, :]
            psGU1 = sc2.enter_context(nc.psum_tensor("psGU1", [128, 2048], F32))[:, :]
            emit_moe(c, gwd, uwd, dwd, "xn", "x", "coefL0", 8, 4096, T,
                     psGU0, psGU1)
            s.barrier()

        with contextlib.ExitStack() as sc3:
            c.t["lncast"] = sb("lncast", [128, 8 * 512], BF16, sc3)
            c.t["lnsq"] = sb("lnsq", [128, 8 * 512], BF16, sc3)
            c.t["lnrows"] = sb("lnrows", [1, 2048], F32, sc3)
            c.t["lnt1"] = sb("lnt1", [128, 512], F32, sc3)
            psA = sc3.enter_context(nc.psum_tensor("psA", [1, 1024], F32))[:, :]
            psB = sc3.enter_context(nc.psum_tensor("psB", [128, 1024], F32))[:, :]
            emit_ln(c, "x", "xn", T, psA, psB)
            s.barrier()

        for nm, dst in (("x", x3_out), ("xn", xn3_out)):
            s.tick("st", 16)
            s.op("S", (lambda src=c.t[nm], dst=dst: lambda eng: dma(eng, dst, src))(),
                 incs=[("st", 16)])
        stv = s.val("st")
        s.op("S", lambda eng: eng.nop(), waits=[("st", stv)])

        sems = {nm: ex.enter_context(nc.semaphore(nm)) for nm in SEM_NAMES}
        s.replay(nc, sems)
    return nc


def prep_kb1_weights(gw, uw, dw, router_w):
    gw = np.asarray(gw, np.float32)
    uw = np.asarray(uw, np.float32)
    dw = np.asarray(dw, np.float32)
    gw_t = np.stack([tile_lhsT(np.ascontiguousarray(gw[e].T), 8, 32) for e in range(8)])
    uw_t = np.stack([tile_lhsT(np.ascontiguousarray(uw[e].T), 8, 32) for e in range(8)])
    dw_t = np.stack([tile_lhsT(np.ascontiguousarray(dw[e].T), 32, 8) for e in range(8)])
    rw = np.ascontiguousarray(
        np.asarray(router_w, np.float32).T.reshape(8, 128, 8)
        .transpose(1, 0, 2).reshape(128, 64))
    sel8 = np.zeros((8, 1024), dtype=BF16NP)
    for e in range(8):
        sel8[e, e * 128:(e + 1) * 128] = 1.0
    return gw_t, uw_t, dw_t, rw, sel8



def build_KB2(dev=False):
    nc = bass.Bass("TRN2", target_bir_lowering=False, num_devices=8)
    s = Sched()
    c = Ctx(nc, s)
    T = 2048
    x3d = nc.dram_tensor("x3_c", [128, 8 * T], F32, kind="ExternalInput").ap()
    xn3d = nc.dram_tensor("xn3_c", [128, 8 * T], BF16, kind="ExternalInput").ap()
    alphad = nc.dram_tensor("alpha", [1, 1], F32, kind="ExternalInput").ap()
    wqt = nc.dram_tensor("wqt", [8, 128, 1024], F32, kind="ExternalInput").ap()
    wkt = nc.dram_tensor("wkt", [8, 128, 1024], F32, kind="ExternalInput").ap()
    wkpt = nc.dram_tensor("wkpt", [8, 128, 1024], F32, kind="ExternalInput").ap()
    wot = nc.dram_tensor("wot", [8, 128, 1024], F32, kind="ExternalInput").ap()
    wvfm = nc.dram_tensor("wvfm", [128, 8192], F32, kind="ExternalInput").ap()
    rded = nc.dram_tensor("rde", [128, 64], F32, kind="ExternalInput").ap()
    gatd = nc.dram_tensor("gat_t", [8, 16, 128, 1024], F32, kind="ExternalInput").ap()
    uptd = nc.dram_tensor("upt_t", [8, 16, 128, 1024], F32, kind="ExternalInput").ap()
    dnd = nc.dram_tensor("dn_t", [8, 8, 128, 2048], F32, kind="ExternalInput").ap()
    identd = nc.dram_tensor("identb", [128, 128], BF16, kind="ExternalInput").ap()
    sel8d = nc.dram_tensor("sel8", [8, 1024], BF16, kind="ExternalInput").ap()
    kidxd = nc.dram_tensor("kidx", [128, 8], I32, kind="ExternalInput").ap()
    vidxd = nc.dram_tensor("vidx", [128, 16], I32, kind="ExternalInput").ap()
    fidx_d = nc.dram_tensor("fidx", [2, 1], I32, kind="ExternalInput").ap()
    pfidx_d = nc.dram_tensor("pfidx", [2, 1], I32, kind="ExternalInput").ap()
    flagvals_d = nc.dram_tensor("flagvals", [2, 4], I32, kind="ExternalInput").ap()
    ksh = nc.dram_tensor("ksh", [2048, T], BF16, kind="Internal",
                         addr_space="Shared").ap()
    kpsh = nc.dram_tensor("kpsh", [2048, T], BF16, kind="Internal",
                          addr_space="Shared").ap()
    vsh = nc.dram_tensor("vsh", [4096, 1024], BF16, kind="Internal",
                         addr_space="Shared").ap()
    shf = nc.dram_tensor("shf", [4, 1], I32, kind="Internal",
                         addr_space="Shared").ap()
    out_c = nc.dram_tensor("out_c", [128, 8 * T], F32, kind="ExternalOutput").ap()
    if dev:
        attn_out = nc.dram_tensor("attn_out", [128, 8 * T], BF16,
                                  kind="ExternalOutput").ap()
        x4_out = nc.dram_tensor("x4_out", [128, 8 * T], F32,
                                kind="ExternalOutput").ap()
        xn4_out = nc.dram_tensor("xn4_out", [128, 8 * T], BF16,
                                 kind="ExternalOutput").ap()
        coef_out = nc.dram_tensor("coef_out", [8, 2048], BF16,
                                  kind="ExternalOutput").ap()
    c.t["shf"] = shf

    with contextlib.ExitStack() as ex:
        def sb(name, shape, dt, st=ex):
            return st.enter_context(nc.sbuf_tensor(name, shape, dt))[:, :]
        c.t["x"] = sb("x_sb", [128, 8 * T], F32)
        c.t["xn"] = sb("xn", [128, 8 * T], BF16)
        c.t["attn"] = c.t["xn"]
        c.t["q"] = sb("q", [128, 8 * T], BF16)
        c.t["onesd"] = sb("onesd", [128, 1], BF16)
        c.t["onesk1"] = sb("onesk1", [1, 128], F32)
        c.t["negk1"] = sb("negk1", [1, 128], F32)
        c.t["identb"] = sb("identb_sb", [128, 128], BF16)
        c.t["sel8"] = sb("sel8_sb", [8, 1024], BF16)
        c.t["rde"] = sb("rde_sb", [128, 64], BF16)
        c.t["mhneg"] = sb("mhneg", [128, 8], F32)
        c.t["nrows"] = sb("nrows", [1, 32], F32)
        c.t["alpha_sc"] = sb("alpha_sc", [1, 2], F32)
        c.t["alpha_in"] = sb("alpha_in", [1, 1], F32)
        c.t["coefL1"] = sb("coefL1", [8, 2048], BF16)
        c.t["flagvals"] = sb("flagvals_sb", [2, 4], I32)
        c.t["flagrd"] = sb("flagrd_sb", [2, 1], I32)
        c.t["fidx"] = sb("fidx_sb", [2, 1], I32)
        c.t["pfidx"] = sb("pfidx_sb", [2, 1], I32)
        c.t["kidx"] = sb("kidx_sb", [128, 8], I32)
        c.t["vidx"] = sb("vidx_sb", [128, 16], I32)
        c.t["wbufA"] = sb("wbufA", [128, 2 * 8 * 128], BF16)

        s.tick("xa", 16)
        s.op("S", lambda eng: dma(eng, c.t["x"], x3d), incs=[("xa", 16)])
        for nm, src in (("xn", xn3d), ("identb", identd), ("sel8", sel8d),
                        ("kidx", kidxd), ("vidx", vidxd), ("fidx", fidx_d),
                        ("pfidx", pfidx_d), ("flagvals", flagvals_d),
                        ("alpha_in", alphad)):
            s.tick("xa", 16)
            s.op("S", (lambda nm=nm, src=src: lambda eng: dma(eng, c.t[nm], src))(),
                 incs=[("xa", 16)])
        s.tick("wl", 16)
        s.op("P", lambda eng: dma(eng, c.t["rde"], rded), incs=[("wl", 16)])
        rwl = s.val("wl")
        xall = s.val("xa")
        vt = s.tick("ve")
        s.op("V", lambda eng: eng.memset(c.t["onesd"], 1.0))
        s.op("V", lambda eng: eng.memset(c.t["onesk1"], 1.0))
        s.op("V", lambda eng: eng.memset(c.t["negk1"], -1.0), incs=[("ve", 1)])
        for e in "TAVP":
            s.op(e, lambda eng: eng.nop(), waits=[("xa", xall), ("ve", vt),
                                                  ("wl", rwl)])
        s.ser("V", "ve", lambda eng: eng.tensor_copy(c.t["alpha_sc"][0:1, 0:1],
                                                     c.t["alpha_in"]))
        s.ser("V", "ve", lambda eng: eng.tensor_scalar(
            c.t["alpha_sc"][0:1, 1:2], c.t["alpha_in"], -1.0, 1.0,
            op0=ALU.mult, op1=ALU.add))

        with contextlib.ExitStack() as scq:
            ps2 = scq.enter_context(nc.psum_tensor("ps2", [128, 1024], F32))[:, :]
            emit_projA(c, wqt, "xn", "q", 8, 8, T, ps2, "wbufA")
            s.barrier()
            for wnm, wdram, shdst, ep in (("k", wkt, ksh, None),
                                          ("kp", wkpt, kpsh, None)):
                with contextlib.ExitStack() as sck:
                    c.t["kloc"] = sb("kloc_" + wnm, [128, 8 * T], BF16, sck)
                    emit_projA(c, wdram, "xn", "kloc", 8, 8, T, ps2, "wbufA")
                    s.barrier()
                    def wrk(eng, shdst=shdst):
                        return eng.indirect_dma_start(
                            out=shdst,
                            out_offset=IndirectOffsetOnAxis(c.t["kidx"][:, 0:1], 0),
                            in_=c.t["kloc"][:, 0:T], in_offset=None)
                    pls = []
                    for fc in range(8):
                        pls.append(((lambda fc=fc, shdst=shdst: lambda eng:
                                     eng.indirect_dma_start(
                            out=shdst,
                            out_offset=IndirectOffsetOnAxis(
                                c.t["kidx"][:, fc:fc + 1], 0),
                            in_=c.t["kloc"][:, fc * T:(fc + 1) * T],
                            in_offset=None))(), []))
                    for fn, w in pls:
                        s.tick("ex", 16)
                        s.op("P", fn, waits=w, incs=[("ex", 16)])
                    exv = s.val("ex")
                    for e in "TAVPS":
                        s.op(e, lambda eng: eng.nop(), waits=[("ex", exv)])
                    s.barrier()
            with contextlib.ExitStack() as scv:
                c.t["vloc"] = sb("vloc", [128, 16 * 1024], BF16, scv)
                c.t["wv_sb"] = sb("wv_sb", [128, 8192], BF16, scv)
                psV = scv.enter_context(nc.psum_tensor("psV", [128, 2048],
                                                       F32))[:, :]
                wt = s.tick("wl", 16)
                s.op("P", lambda eng: dma(eng, c.t["wv_sb"], wvfm),
                     incs=[("wl", 16)])
                for e in "T":
                    s.op(e, lambda eng: eng.nop(), waits=[("wl", wt)])
                emit_l1_vproj(c, "wv_sb", "xn", "vloc", psV)
                vall = s.val("ve")
                for e in "TAVPS":
                    s.op(e, lambda eng: eng.nop(), waits=[("ve", vall)])
                for tt in range(16):
                    s.tick("ex", 16)
                    s.op("P", (lambda tt=tt: lambda eng: eng.indirect_dma_start(
                        out=vsh,
                        out_offset=IndirectOffsetOnAxis(c.t["vidx"][:, tt:tt + 1], 0),
                        in_=c.t["vloc"][:, tt * 1024:(tt + 1) * 1024],
                        in_offset=None))(), incs=[("ex", 16)])
                exv = s.val("ex")
                for e in "TAVPS":
                    s.op(e, lambda eng: eng.nop(), waits=[("ex", exv)])
                s.barrier()
        emit_exchange_write(c, [], epoch=2)
        emit_exchange_poll(c, epoch=2, msem_i=0)
        pall = s.val("po")

        with contextlib.ExitStack() as sca:
            c.t["khp"] = sb("khp", [128, 4096], BF16, sca)
            c.t["kphp"] = sb("kphp", [128, 4096], BF16, sca)
            c.t["vaughp"] = sb("vaughp", [128, 4160], BF16, sca)
            c.t["qsq"] = sb("qsq", [128, 2048], BF16, sca)
            c.t["ksq"] = sb("ksq", [128, 4096], BF16, sca)
            c.t["p_sb"] = sb("p_sb", [128, 2048], BF16, sca)
            c.t["ostd"] = sb("ostd", [128, 1024], BF16, sca)
            c.t["osp"] = sb("osp", [128, 1024], BF16, sca)
            c.t["linv2"] = sb("linv2", [1, 2048], F32, sca)
            c.t["l1t0"] = sb("l1t0", [128, 512], F32, sca)
            c.t["l1t1"] = sb("l1t1", [128, 512], F32, sca)
            psS = sca.enter_context(nc.psum_tensor("psS", [128, 2048], F32))[:, :]
            psO = sca.enter_context(nc.psum_tensor("psO", [128, 1024], F32))[:, :]
            for hp in range(8):
                kld = s.val("ex")
                for half in range(2):
                    for src, dstn in ((ksh, "khp"), (kpsh, "kphp")):
                        s.tick("ex", 16)
                        s.op("P", (lambda half=half, src=src, dstn=dstn, hp=hp:
                                   lambda eng: dma(
                            eng, c.t[dstn][:, half * T:(half + 1) * T],
                            src[half * 1024 + hp * 128: half * 1024 + hp * 128 + 128,
                                :]))(), waits=[("po", pall)], incs=[("ex", 16)])
                vmt = s.tick("ve")
                s.op("V", lambda eng: eng.memset(c.t["vaughp"], 1.0),
                     incs=[("ve", 1)])
                for ch in range(32):
                    s.tick("ex", 16)
                    s.op("P", (lambda ch=ch, hp=hp: lambda eng: dma(
                        eng,
                        c.t["vaughp"].rearrange("p (h x) -> p h x", h=2)
                        [:, :, ch * 65: ch * 65 + 64],
                        vsh[ch * 128:(ch + 1) * 128,
                            hp * 128:(hp + 1) * 128].rearrange(
                            "s (h w) -> s h w", h=2)))(),
                        waits=[("po", pall), ("ve", vmt)], incs=[("ex", 16)])
                exv = s.val("ex")
                s.barrier()
                for e in "TAV":
                    s.op(e, lambda eng: eng.nop(), waits=[("ex", exv)])
                emit_l1_norm_mhat(c, psS, psO, hp, k_ready=[("ex", exv)])
                s.barrier()
                emit_l1_attn_hp(c, psS, psO, hp)
                s.barrier()

        if dev:
            s.tick("st", 16)
            s.op("S", lambda eng: dma(eng, attn_out, c.t["attn"]),
                 incs=[("st", 16)])
            s.op("S", lambda eng: eng.nop(), waits=[("st", s.val("st"))])
            s.barrier()
        with contextlib.ExitStack() as sco:
            ps2 = sco.enter_context(nc.psum_tensor("ps2o", [128, 1024], F32))[:, :]
            emit_residual_proj(c, wot, "attn", "x", 8, 8, T, ps2, "wbufA")
            s.barrier()

        if dev:
            s.tick("st", 16)
            s.op("S", lambda eng: dma(eng, x4_out, c.t["x"]), incs=[("st", 16)])
            s.op("S", lambda eng: eng.nop(), waits=[("st", s.val("st"))])
            s.barrier()
        with contextlib.ExitStack() as scl:
            c.t["lncast"] = sb("lncast", [128, 8 * 512], BF16, scl)
            c.t["lnsq"] = sb("lnsq", [128, 8 * 512], BF16, scl)
            c.t["lnrows"] = sb("lnrows", [1, 2048], F32, scl)
            c.t["lnt1"] = sb("lnt1", [128, 512], F32, scl)
            psA = scl.enter_context(nc.psum_tensor("psA", [1, 1024], F32))[:, :]
            psB = scl.enter_context(nc.psum_tensor("psB", [128, 1024], F32))[:, :]
            emit_ln(c, "x", "xn", T, psA, psB)
            s.barrier()

        if dev:
            s.tick("st", 16)
            s.op("S", lambda eng: dma(eng, xn4_out, c.t["xn"]), incs=[("st", 16)])
            s.op("S", lambda eng: eng.nop(), waits=[("st", s.val("st"))])
            s.barrier()
        with contextlib.ExitStack() as scr:
            c.t["rlt"] = sb("rlt", [128, 8], F32, scr)
            c.t["rrows"] = sb("rrows", [128, 8], F32, scr)
            c.t["reqm"] = sb("reqm", [128, 8], F32, scr)
            c.t["rmk"] = sb("rmk", [128, 8], F32, scr)
            c.t["rmsk"] = sb("rmsk", [128, 8], F32, scr)
            c.t["re_"] = sb("re_", [128, 8], F32, scr)
            c.t["rcf16"] = sb("rcf16", [128, 8], BF16, scr)
            psR = scr.enter_context(nc.psum_tensor("psR", [128, 1024], F32))[:, :]
            c.t["psTb_ap"] = scr.enter_context(
                nc.psum_tensor("psTb", [8, 128], BF16))[:, :]
            emit_router_masks_coef(c, "rde", "xn", psR, None, None, "coefL1",
                                   per_token_w=True)
            s.barrier()
            if dev:
                s.tick("st", 16)
                s.op("S", lambda eng: dma(eng, coef_out, c.t["coefL1"]),
                     incs=[("st", 16)])
                s.op("S", lambda eng: eng.nop(), waits=[("st", s.val("st"))])
                s.barrier()
        with contextlib.ExitStack() as scm:
            c.t["h_sb"] = sb("h_sb", [128, 16 * 1024], BF16, scm)
            c.t["sg_sb"] = sb("sg_sb", [128, 2048], BF16, scm)
            c.t["gub"] = sb("gub", [128, 4096], BF16, scm)
            c.t["dwb"] = sb("dwb", [128, 2048], BF16, scm)
            c.t["cb_sb"] = sb("cb_sb", [128, 1024], F32, scm)
            c.t["moe_t0"] = sb("moe_t0", [128, 512], F32, scm)
            psGU0 = scm.enter_context(nc.psum_tensor("psGU0", [128, 2048], F32))[:, :]
            psGU1 = scm.enter_context(nc.psum_tensor("psGU1", [128, 2048], F32))[:, :]
            emit_moe(c, gatd, uptd, dnd, "xn", "x", "coefL1", 8, 2048, T,
                     psGU0, psGU1)
            s.barrier()

        s.tick("st", 16)
        s.op("S", lambda eng: dma(eng, out_c, c.t["x"]), incs=[("st", 16)])
        stv = s.val("st")
        s.op("S", lambda eng: eng.nop(), waits=[("st", stv)])

        sems = {nm: ex.enter_context(nc.semaphore(nm)) for nm in SEM_NAMES}
        s.replay(nc, sems)
    return nc


def prep_kb2_weights(W_Q, W_K, W_V, W_Kp, W_O, rde, gat, upt, dn):
    wqt = tile_lhsT(np.ascontiguousarray(np.asarray(W_Q, np.float32).T), 8, 8)
    wkt = tile_lhsT(np.ascontiguousarray(np.asarray(W_K, np.float32).T), 8, 8)
    wkpt = tile_lhsT(np.ascontiguousarray(np.asarray(W_Kp, np.float32).T), 8, 8)
    wot = tile_lhsT(np.ascontiguousarray(np.asarray(W_O, np.float32).T), 8, 8)
    wvfm = np.ascontiguousarray(
        np.asarray(W_V, np.float32).T.reshape(8, 128, 1024)
        .transpose(1, 0, 2).reshape(128, 8192))
    rdep = np.ascontiguousarray(
        np.asarray(rde, np.float32).reshape(8, 128, 8)
        .transpose(1, 0, 2).reshape(128, 64))
    gat = np.asarray(gat, np.float32)
    upt = np.asarray(upt, np.float32)
    dn = np.asarray(dn, np.float32)
    gat_t = np.stack([tile_lhsT(np.ascontiguousarray(gat[e]), 8, 16)
                      for e in range(8)])
    upt_t = np.stack([tile_lhsT(np.ascontiguousarray(upt[e]), 8, 16)
                      for e in range(8)])
    dn_t = np.stack([tile_lhsT(np.ascontiguousarray(dn[e]), 16, 8)
                     for e in range(8)])
    return wqt, wkt, wkpt, wot, wvfm, rdep, gat_t, upt_t, dn_t



def build_KB(dev=False, skip_router=False, n_e=8):
    nc = bass.Bass("TRN2", target_bir_lowering=False, num_devices=8)
    s = Sched()
    c = Ctx(nc, s)
    T = 2048
    x2d = nc.dram_tensor("x2_c", [128, 8 * T], F32, kind="ExternalInput").ap()
    xn2d = nc.dram_tensor("xn2_c", [128, 8 * T], BF16, kind="ExternalInput").ap()
    ptd = nc.dram_tensor("partials_T", [8, 16], F32, kind="ExternalInput").ap()
    rwd = nc.dram_tensor("rw", [128, 64], F32, kind="ExternalInput").ap()
    gwd = nc.dram_tensor("gw_t", [8, 32, 128, 1024], F32, kind="ExternalInput").ap()
    uwd = nc.dram_tensor("uw_t", [8, 32, 128, 1024], F32, kind="ExternalInput").ap()
    dwd = nc.dram_tensor("dw_t", [8, 8, 128, 4096], F32, kind="ExternalInput").ap()
    alphad = nc.dram_tensor("alpha", [1, 1], F32, kind="ExternalInput").ap()
    wqt = nc.dram_tensor("wqt", [8, 128, 1024], F32, kind="ExternalInput").ap()
    wkt = nc.dram_tensor("wkt", [8, 128, 1024], F32, kind="ExternalInput").ap()
    wkpt = nc.dram_tensor("wkpt", [8, 128, 1024], F32, kind="ExternalInput").ap()
    wot = nc.dram_tensor("wot", [8, 128, 1024], F32, kind="ExternalInput").ap()
    wvfm = nc.dram_tensor("wvfm", [128, 8192], F32, kind="ExternalInput").ap()
    rded = nc.dram_tensor("rde", [128, 64], F32, kind="ExternalInput").ap()
    gatd = nc.dram_tensor("gat_t", [8, 16, 128, 1024], F32, kind="ExternalInput").ap()
    uptd = nc.dram_tensor("upt_t", [8, 16, 128, 1024], F32, kind="ExternalInput").ap()
    dnd = nc.dram_tensor("dn_t", [8, 8, 128, 2048], F32, kind="ExternalInput").ap()
    identd = nc.dram_tensor("identb", [128, 128], BF16, kind="ExternalInput").ap()
    sel8d = nc.dram_tensor("sel8", [8, 1024], BF16, kind="ExternalInput").ap()
    kidxd = nc.dram_tensor("kidx", [128, 8], I32, kind="ExternalInput").ap()
    vidxd = nc.dram_tensor("vidx", [128, 16], I32, kind="ExternalInput").ap()
    fidx_d = nc.dram_tensor("fidx", [2, 1], I32, kind="ExternalInput").ap()
    pfidx_d = nc.dram_tensor("pfidx", [2, 1], I32, kind="ExternalInput").ap()
    flagvals_d = nc.dram_tensor("flagvals", [2, 4], I32, kind="ExternalInput").ap()
    ksh = nc.dram_tensor("ksh", [2048, T], BF16, kind="Internal",
                         addr_space="Shared").ap()
    kpsh = nc.dram_tensor("kpsh", [2048, T], BF16, kind="Internal",
                          addr_space="Shared").ap()
    vsh = nc.dram_tensor("vsh", [4096, 1024], BF16, kind="Internal",
                         addr_space="Shared").ap()
    shf = nc.dram_tensor("shf", [4, 1], I32, kind="Internal",
                         addr_space="Shared").ap()
    out_c = nc.dram_tensor("out_c", [128, 8 * T], F32, kind="ExternalOutput").ap()
    if dev:
        x3_out = nc.dram_tensor("x3_out", [128, 8 * T], F32,
                                kind="ExternalOutput").ap()
        x4_out = nc.dram_tensor("x4_out", [128, 8 * T], F32,
                                kind="ExternalOutput").ap()
        coef0_out = nc.dram_tensor("coef0_out", [8, 2048], BF16,
                                   kind="ExternalOutput").ap()
        mw_out = nc.dram_tensor("mw_out", [8, 1], F32,
                                kind="ExternalOutput").ap()
    c.t["shf"] = shf

    with contextlib.ExitStack() as ex:
        def sb(name, shape, dt, st=ex):
            return st.enter_context(nc.sbuf_tensor(name, shape, dt))[:, :]
        c.t["x"] = sb("x_sb", [128, 8 * T], F32)
        c.t["xn"] = sb("xn", [128, 8 * T], BF16)
        c.t["attn"] = c.t["xn"]
        c.t["onesd"] = sb("onesd", [128, 1], BF16)
        c.t["onesk1"] = sb("onesk1", [1, 128], F32)
        c.t["negk1"] = sb("negk1", [1, 128], F32)
        c.t["identb"] = sb("identb_sb", [128, 128], BF16)
        c.t["sel8"] = sb("sel8_sb", [8, 1024], BF16)
        c.t["rw"] = sb("rw_sb", [128, 64], BF16)
        c.t["rde"] = sb("rde_sb", [128, 64], BF16)
        c.t["mhneg"] = sb("mhneg", [128, 8], F32)
        c.t["nrows"] = sb("nrows", [1, 32], F32)
        c.t["alpha_sc"] = sb("alpha_sc", [1, 2], F32)
        c.t["alpha_in"] = sb("alpha_in", [1, 1], F32)
        c.t["coefL1"] = sb("coefL1", [8, 2048], BF16)
        c.t["partials_T"] = sb("partials_T_sb", [8, 16], F32)
        c.t["mw"] = sb("mw", [8, 1], F32)
        c.t["strows"] = sb("strows", [8, 4], F32)
        c.t["flagvals"] = sb("flagvals_sb", [2, 4], I32)
        c.t["flagrd"] = sb("flagrd_sb", [2, 1], I32)
        c.t["fidx"] = sb("fidx_sb", [2, 1], I32)
        c.t["pfidx"] = sb("pfidx_sb", [2, 1], I32)
        c.t["kidx"] = sb("kidx_sb", [128, 8], I32)
        c.t["vidx"] = sb("vidx_sb", [128, 16], I32)
        c.t["wbufA"] = sb("wbufA", [128, 2 * 8 * 128], BF16)

        s.tick("xa", 16)
        s.op("S", lambda eng: dma(eng, c.t["x"], x2d), incs=[("xa", 16)])
        for nm, src in (("xn", xn2d), ("identb", identd), ("sel8", sel8d),
                        ("partials_T", ptd), ("kidx", kidxd), ("vidx", vidxd),
                        ("fidx", fidx_d), ("pfidx", pfidx_d),
                        ("flagvals", flagvals_d), ("alpha_in", alphad)):
            s.tick("xa", 16)
            s.op("S", (lambda dstt=c.t[nm], src=src: lambda eng: dma(eng, dstt, src))(),
                 incs=[("xa", 16)])
        s.tick("wl", 16)
        s.op("P", lambda eng: dma(eng, c.t["rw"], rwd), incs=[("wl", 16)])
        s.tick("wl", 16)
        s.op("P", lambda eng: dma(eng, c.t["rde"], rded), incs=[("wl", 16)])
        rwl = s.val("wl")
        xall = s.val("xa")
        vt = s.tick("ve")
        s.op("V", lambda eng: eng.memset(c.t["onesd"], 1.0))
        s.op("V", lambda eng: eng.memset(c.t["onesk1"], 1.0))
        s.op("V", lambda eng: eng.memset(c.t["negk1"], -1.0), incs=[("ve", 1)])
        for e in "TAVP":
            s.op(e, lambda eng: eng.nop(), waits=[("xa", xall), ("ve", vt),
                                                  ("wl", rwl)])
        s.ser("V", "ve", lambda eng: eng.tensor_copy(c.t["alpha_sc"][0:1, 0:1],
                                                     c.t["alpha_in"]))
        s.ser("V", "ve", lambda eng: eng.tensor_scalar(
            c.t["alpha_sc"][0:1, 1:2], c.t["alpha_in"], -1.0, 1.0,
            op0=ALU.mult, op1=ALU.add))

        if skip_router:
            coef0_in = nc.dram_tensor("coef0_in", [8, 2048], BF16,
                                      kind="ExternalInput").ap()
        with contextlib.ExitStack() as sA:
            c.t["coefL0"] = sb("coefL0", [8, 2048], BF16, sA)
            if skip_router:
                ct = s.tick("xa", 16)
                s.op("S", lambda eng: dma(eng, c.t["coefL0"], coef0_in),
                     incs=[("xa", 16)])
                for e in "TAVP":
                    s.op(e, lambda eng: eng.nop(), waits=[("xa", ct)])
                s.barrier()
            if not skip_router:
              with contextlib.ExitStack() as sA1:
                c.t["rlt"] = sb("rlt0", [128, 8], F32, sA1)
                c.t["rrows"] = sb("rrows0", [128, 8], F32, sA1)
                c.t["reqm"] = sb("reqm0", [128, 8], F32, sA1)
                c.t["rmk"] = sb("rmk0", [128, 8], F32, sA1)
                c.t["rmsk"] = sb("rmsk0", [128, 8], F32, sA1)
                c.t["re_"] = sb("re0_", [128, 8], F32, sA1)
                c.t["rcf16"] = sb("rcf160", [128, 8], BF16, sA1)
                psR = sA1.enter_context(nc.psum_tensor("psR0", [128, 1024],
                                                       F32))[:, :]
                c.t["psTb_ap"] = sA1.enter_context(
                    nc.psum_tensor("psTb0", [8, 128], BF16))[:, :]
                emit_stats_meanw(c)
                emit_router_masks_coef(c, "rw", "xn", psR, None, "mw", "coefL0",
                                       per_token_w=False)
                s.barrier()
                if dev and not skip_router:
                    for nm, dst in (("coefL0", coef0_out), ("mw", mw_out)):
                        s.tick("st", 16)
                        s.op("S", (lambda nm=nm, dst=dst: lambda eng:
                                   dma(eng, dst, c.t[nm]))(), incs=[("st", 16)])
                    s.op("S", lambda eng: eng.nop(), waits=[("st", s.val("st"))])
                    s.barrier()
            for k in [k for k in c.t if isinstance(k, tuple) and k[0].startswith("rc_")]:
                del c.t[k]
            with contextlib.ExitStack() as sA2:
                c.t["h_sb"] = sb("h0_sb", [128, 32 * 1024], BF16, sA2)
                c.t["sg_sb"] = sb("sg0_sb", [128, 2048], BF16, sA2)
                c.t["gub"] = sb("gub0", [128, 4096], BF16, sA2)
                c.t["dwb"] = sb("dwb0", [128, 4096], BF16, sA2)
                c.t["cb_sb"] = sb("cb0_sb", [128, 1024], F32, sA2)
                c.t["moe_t0"] = sb("moe0_t0", [128, 512], F32, sA2)
                psGU0 = sA2.enter_context(nc.psum_tensor("psGU00", [128, 2048],
                                                         F32))[:, :]
                psGU1 = sA2.enter_context(nc.psum_tensor("psGU01", [128, 2048],
                                                         F32))[:, :]
                emit_moe(c, gwd, uwd, dwd, "xn", "x", "coefL0", n_e, 4096, T,
                         psGU0, psGU1)
                s.barrier()
                if dev:
                    hdump = nc.dram_tensor("h_dump", [128, 32 * 1024], BF16,
                                           kind="ExternalOutput").ap()
                    cbdump = nc.dram_tensor("cb_dump", [128, 1024], F32,
                                            kind="ExternalOutput").ap()
                    for nm, dst in (("h_sb", hdump), ("cb_sb", cbdump)):
                        s.tick("st", 16)
                        s.op("S", (lambda src=c.t[nm], dst=dst: lambda eng:
                                   dma(eng, dst, src))(), incs=[("st", 16)])
                    s.op("S", lambda eng: eng.nop(), waits=[("st", s.val("st"))])
                    s.barrier()
        for k in [k for k in c.t if isinstance(k, tuple) and k[0].startswith("moe_")]:
            del c.t[k]
        for k in ["moe_cb_ac_prev"]:
            c.t.pop(k, None)

        with contextlib.ExitStack() as sL3:
            c.t["lncast"] = sb("lncast3", [128, 8 * 512], BF16, sL3)
            c.t["lnsq"] = sb("lnsq3", [128, 8 * 512], BF16, sL3)
            c.t["lnrows"] = sb("lnrows3", [1, 2048], F32, sL3)
            c.t["lnt1"] = sb("lnt13", [128, 512], F32, sL3)
            psA = sL3.enter_context(nc.psum_tensor("psA3", [1, 1024], F32))[:, :]
            psB = sL3.enter_context(nc.psum_tensor("psB3", [128, 1024], F32))[:, :]
            emit_ln(c, "x", "xn", T, psA, psB)
            s.barrier()

        if dev:
            s.tick("st", 16)
            s.op("S", lambda eng: dma(eng, x3_out, c.t["x"]), incs=[("st", 16)])
            s.op("S", lambda eng: eng.nop(), waits=[("st", s.val("st"))])
            s.barrier()
        with contextlib.ExitStack() as sB:
            c.t["q"] = sb("q", [128, 8 * T], BF16, sB)
            with contextlib.ExitStack() as scq:
                ps2 = scq.enter_context(nc.psum_tensor("ps2q", [128, 1024],
                                                       F32))[:, :]
                emit_projA(c, wqt, "xn", "q", 8, 8, T, ps2, "wbufA")
                s.barrier()
                for wnm, wdram, shdst in (("k", wkt, ksh), ("kp", wkpt, kpsh)):
                    with contextlib.ExitStack() as sck:
                        c.t["kloc"] = sb("kloc_" + wnm, [128, 8 * T], BF16, sck)
                        emit_projA(c, wdram, "xn", "kloc", 8, 8, T, ps2, "wbufA")
                        s.barrier()
                        for fc in range(8):
                            s.tick("ex", 16)
                            s.op("P", (lambda fc=fc, shdst=shdst: lambda eng:
                                       eng.indirect_dma_start(
                                out=shdst,
                                out_offset=IndirectOffsetOnAxis(
                                    c.t["kidx"][:, fc:fc + 1], 0),
                                in_=c.t["kloc"][:, fc * T:(fc + 1) * T],
                                in_offset=None))(), incs=[("ex", 16)])
                        exv = s.val("ex")
                        for e in "TAVPS":
                            s.op(e, lambda eng: eng.nop(), waits=[("ex", exv)])
                        s.barrier()
                with contextlib.ExitStack() as scv:
                    c.t["vloc"] = sb("vloc", [128, 16 * 1024], BF16, scv)
                    c.t["wv_sb"] = sb("wv_sb", [128, 8192], BF16, scv)
                    psV = scv.enter_context(nc.psum_tensor("psV", [128, 2048],
                                                           F32))[:, :]
                    wt = s.tick("wl", 16)
                    s.op("P", lambda eng: dma(eng, c.t["wv_sb"], wvfm),
                         incs=[("wl", 16)])
                    s.op("T", lambda eng: eng.nop(), waits=[("wl", wt)])
                    emit_l1_vproj(c, "wv_sb", "xn", "vloc", psV)
                    vall = s.val("ve")
                    for e in "TAVPS":
                        s.op(e, lambda eng: eng.nop(), waits=[("ve", vall)])
                    for tt in range(16):
                        s.tick("ex", 16)
                        s.op("P", (lambda tt=tt: lambda eng: eng.indirect_dma_start(
                            out=vsh,
                            out_offset=IndirectOffsetOnAxis(
                                c.t["vidx"][:, tt:tt + 1], 0),
                            in_=c.t["vloc"][:, tt * 1024:(tt + 1) * 1024],
                            in_offset=None))(), incs=[("ex", 16)])
                    exv = s.val("ex")
                    for e in "TAVPS":
                        s.op(e, lambda eng: eng.nop(), waits=[("ex", exv)])
                    s.barrier()
            emit_exchange_write(c, [], epoch=2)
            emit_exchange_poll(c, epoch=2, msem_i=0)
            pall = s.val("po")

            with contextlib.ExitStack() as sca:
                c.t["khp"] = sb("khp", [128, 4096], BF16, sca)
                c.t["kphp"] = sb("kphp", [128, 4096], BF16, sca)
                c.t["vaughp"] = sb("vaughp", [128, 4160], BF16, sca)
                c.t["qsq"] = sb("qsq", [128, 2048], BF16, sca)
                c.t["ksq"] = sb("ksq", [128, 4096], BF16, sca)
                c.t["p_sb"] = sb("p_sb", [128, 2048], BF16, sca)
                c.t["ostd"] = sb("ostd", [128, 1024], BF16, sca)
                c.t["osp"] = sb("osp", [128, 1024], BF16, sca)
                c.t["linv2"] = sb("linv2", [1, 2048], F32, sca)
                c.t["l1t0"] = sb("l1t0", [128, 512], F32, sca)
                c.t["l1t1"] = sb("l1t1", [128, 512], F32, sca)
                psS = sca.enter_context(nc.psum_tensor("psS", [128, 2048],
                                                       F32))[:, :]
                psO = sca.enter_context(nc.psum_tensor("psO", [128, 1024],
                                                       F32))[:, :]
                for hp in range(8):
                    for half in range(2):
                        for src, dstn in ((ksh, "khp"), (kpsh, "kphp")):
                            s.tick("ex", 16)
                            s.op("P", (lambda half=half, src=src, dstn=dstn, hp=hp:
                                       lambda eng: dma(
                                eng, c.t[dstn][:, half * T:(half + 1) * T],
                                src[half * 1024 + hp * 128:
                                    half * 1024 + hp * 128 + 128, :]))(),
                                waits=[("po", pall)], incs=[("ex", 16)])
                    vmt = s.tick("ve")
                    s.op("V", lambda eng: eng.memset(c.t["vaughp"], 1.0),
                         incs=[("ve", 1)])
                    for ch in range(32):
                        s.tick("ex", 16)
                        s.op("P", (lambda ch=ch, hp=hp: lambda eng: dma(
                            eng,
                            c.t["vaughp"].rearrange("p (h x) -> p h x", h=2)
                            [:, :, ch * 65: ch * 65 + 64],
                            vsh[ch * 128:(ch + 1) * 128,
                                hp * 128:(hp + 1) * 128].rearrange(
                                "s (h w) -> s h w", h=2)))(),
                            waits=[("po", pall), ("ve", vmt)], incs=[("ex", 16)])
                    exv = s.val("ex")
                    s.barrier()
                    for e in "TAV":
                        s.op(e, lambda eng: eng.nop(), waits=[("ex", exv)])
                    emit_l1_norm_mhat(c, psS, psO, hp, k_ready=[("ex", exv)])
                    s.barrier()
                    emit_l1_attn_hp(c, psS, psO, hp)
                    s.barrier()
            with contextlib.ExitStack() as sco:
                ps2 = sco.enter_context(nc.psum_tensor("ps2o", [128, 1024],
                                                       F32))[:, :]
                emit_residual_proj(c, wot, "attn", "x", 8, 8, T, ps2, "wbufA")
                s.barrier()

        if dev:
            s.tick("st", 16)
            s.op("S", lambda eng: dma(eng, x4_out, c.t["x"]), incs=[("st", 16)])
            s.op("S", lambda eng: eng.nop(), waits=[("st", s.val("st"))])
            s.barrier()
        with contextlib.ExitStack() as sL4:
            c.t["lncast"] = sb("lncast4", [128, 8 * 512], BF16, sL4)
            c.t["lnsq"] = sb("lnsq4", [128, 8 * 512], BF16, sL4)
            c.t["lnrows"] = sb("lnrows4", [1, 2048], F32, sL4)
            c.t["lnt1"] = sb("lnt14", [128, 512], F32, sL4)
            psA = sL4.enter_context(nc.psum_tensor("psA4", [1, 1024], F32))[:, :]
            psB = sL4.enter_context(nc.psum_tensor("psB4", [128, 1024], F32))[:, :]
            emit_ln(c, "x", "xn", T, psA, psB)
            s.barrier()

        with contextlib.ExitStack() as scr:
            c.t["rlt"] = sb("rlt1", [128, 8], F32, scr)
            c.t["rrows"] = sb("rrows1", [128, 8], F32, scr)
            c.t["reqm"] = sb("reqm1", [128, 8], F32, scr)
            c.t["rmk"] = sb("rmk1", [128, 8], F32, scr)
            c.t["rmsk"] = sb("rmsk1", [128, 8], F32, scr)
            c.t["re_"] = sb("re1_", [128, 8], F32, scr)
            c.t["rcf16"] = sb("rcf161", [128, 8], BF16, scr)
            psR = scr.enter_context(nc.psum_tensor("psR1", [128, 1024], F32))[:, :]
            c.t["psTb_ap"] = scr.enter_context(
                nc.psum_tensor("psTb1", [8, 128], BF16))[:, :]
            emit_router_masks_coef(c, "rde", "xn", psR, None, None, "coefL1",
                                   per_token_w=True)
            s.barrier()
        with contextlib.ExitStack() as scm:
            c.t["h_sb"] = sb("h1_sb", [128, 16 * 1024], BF16, scm)
            c.t["sg_sb"] = sb("sg1_sb", [128, 2048], BF16, scm)
            c.t["gub"] = sb("gub1", [128, 4096], BF16, scm)
            c.t["dwb"] = sb("dwb1", [128, 2048], BF16, scm)
            c.t["cb_sb"] = sb("cb1_sb", [128, 1024], F32, scm)
            c.t["moe_t0"] = sb("moe1_t0", [128, 512], F32, scm)
            psGU0 = scm.enter_context(nc.psum_tensor("psGU10", [128, 2048],
                                                     F32))[:, :]
            psGU1 = scm.enter_context(nc.psum_tensor("psGU11", [128, 2048],
                                                     F32))[:, :]
            emit_moe(c, gatd, uptd, dnd, "xn", "x", "coefL1", 8, 2048, T,
                     psGU0, psGU1)
            s.barrier()

        s.tick("st", 16)
        s.op("S", lambda eng: dma(eng, out_c, c.t["x"]), incs=[("st", 16)])
        s.op("S", lambda eng: eng.nop(), waits=[("st", s.val("st"))])

        sems = {nm: ex.enter_context(nc.semaphore(nm)) for nm in SEM_NAMES}
        s.replay(nc, sems)
    return nc


def kernel(**inputs):
    x = np.asarray(inputs["L0_attn_x"], np.float32)
    ka_maps = prep_ka_inputs(
        x, inputs["L0_attn_Wc_down"], inputs["L0_attn_Wq_up"],
        inputs["L0_attn_Wkv_up"], inputs["L0_attn_Wc_proj"],
        inputs["L0_ffn_router_w"])
    nca = build_KA()
    ra = run_bass_kernel_spmd(nca, ka_maps, core_ids=list(range(8)))

    parts = [np.asarray(ra.results[cid]["part_out"], np.float32).ravel()
             for cid in range(8)]
    pt = np.zeros((8, 16), np.float32)
    for e in range(8):
        for cc in range(8):
            pt[e, cc] = parts[cc][e]
            pt[e, 8 + cc] = parts[cc][8 + e]
    gw_t, uw_t, dw_t, rwt, sel8 = prep_kb1_weights(
        inputs["L0_ffn_gate_w"], inputs["L0_ffn_up_w"], inputs["L0_ffn_down_w"],
        inputs["L0_ffn_router_w"])
    ident = np.eye(128, dtype=BF16NP)
    wqt, wkt, wkpt, wot, wvfm, rdep, gat_t, upt_t, dn_t = prep_kb2_weights(
        inputs["L1_attn_W_Q"], inputs["L1_attn_W_K"], inputs["L1_attn_W_V"],
        inputs["L1_attn_W_K_prime"], inputs["L1_attn_W_O"],
        inputs["L1_ffn_router_kernel_DE"], inputs["L1_ffn_kernel_gating_EDF"],
        inputs["L1_ffn_kernel_up_EDF"], inputs["L1_ffn_kernel_down_EFD"])
    alpha = np.asarray(inputs["L1_attn_alpha"], np.float32).reshape(1, 1)
    fv = make_flagvals()
    ar = np.arange(128, dtype=np.int32)
    ncb = build_KB()
    kb_maps = []
    for cid in range(8):
        b, h = divmod(cid, 2)
        kidx = np.stack([h * 1024 + fc * 128 + ar for fc in range(8)], 1)
        vidx = np.stack([h * 2048 + tt * 128 + ar for tt in range(16)], 1)
        kb_maps.append({
            "x2_c": np.asarray(ra.results[cid]["x2_out"]),
            "xn2_c": np.asarray(ra.results[cid]["xn2_out"]),
            "partials_T": pt, "rw": rwt, "gw_t": gw_t, "uw_t": uw_t,
            "dw_t": dw_t,
            "alpha": alpha,
            "wqt": wqt, "wkt": wkt, "wkpt": wkpt, "wot": wot, "wvfm": wvfm,
            "rde": rdep, "gat_t": gat_t, "upt_t": upt_t, "dn_t": dn_t,
            "identb": ident, "sel8": sel8,
            "kidx": kidx.astype(np.int32), "vidx": vidx.astype(np.int32),
            "fidx": np.array([[2 * h], [2 * h + 1]], np.int32),
            "pfidx": np.array([[2 * (1 - h)], [2 * (1 - h) + 1]], np.int32),
            "flagvals": fv,
        })
    rb2 = run_bass_kernel_spmd(ncb, kb_maps, core_ids=list(range(8)))

    out = np.empty((4, 4096, 1024), np.float32)
    for cid in range(8):
        b, h = divmod(cid, 2)
        shard = fm_unpack(np.asarray(rb2.results[cid]["out_c"], np.float32), 2048)
        out[b, h * 2048:(h + 1) * 2048, :] = shard.T
    return out
